# version 03
# baseline (speedup 1.0000x reference)
import contextlib
import numpy as np
import ml_dtypes

import concourse.bass as bass
import concourse.mybir as mybir
from concourse.bass_utils import run_bass_kernel_spmd

F32 = mybir.dt.float32
F8 = mybir.dt.float8e4
I32 = mybir.dt.int32
AF = mybir.ActivationFunctionType
ALU = mybir.AluOpType

LN2 = float(np.log(2.0))
M23 = 8388608.0

D = 1024
H = 8
DK = 128
T = 4096
CH = 128
NCH = T // CH
FT = 512
NFT = T // FT
DH = 4096
EPS = 1e-5

ENGINES = ("sync", "scalar", "vector", "tensor", "gpsimd")
DMA_ENG = ("sync", "gpsimd")


class Sched:

    def __init__(self):
        self.done = {e: 0 for e in ENGINES}
        self.watermark = {e: {e2: 0 for e2 in ENGINES} for e in ENGINES}
        self.program = []

    def stage(self, deps, items):
        engines_here = []
        for e, _ in items:
            if e not in engines_here:
                engines_here.append(e)
        for e in engines_here:
            for pe, cnt in deps.items():
                if cnt <= self.watermark[e][pe]:
                    continue
                self.program.append(("wait", e, pe, cnt))
                self.watermark[e][pe] = cnt
        last_idx = {}
        for i, (e, _) in enumerate(items):
            last_idx[e] = i
        for i, (e, fn) in enumerate(items):
            if e in DMA_ENG:
                inc = 16
                self.done[e] += 16
            else:
                inc = 1 if last_idx[e] == i else 0
            self.program.append(("ins", e, fn, inc))
        for e in engines_here:
            if e not in DMA_ENG:
                self.done[e] += 1
        return dict(self.done)

    def snap(self):
        return dict(self.done)


def build_program(debug=False):
    nc = bass.Bass("TRN2", target_bir_lowering=False)

    d = {}
    for nm, shape in [
        ("x_t", [D, T]), ("xpre_t", [D, T]),
        ("wqT", [D, D]), ("wkT", [D, D]), ("wvT", [D, D]), ("woT", [D, D]),
        ("fmqT", [DK, DK]), ("fmkT", [DK, DK]),
        ("fmq_b", [DK, 1]), ("fmk_b", [DK, 1]), ("fmk_br", [1, DK]),
        ("rms_w", [DK, 1]),
        ("w1T", [D, DH]), ("b1", [128, 32]), ("w2T", [DH, D]),
        ("b2", [128, 8]),
        ("triu", [CH, CH]), ("ones", [128, 1]),
    ]:
        d[nm] = nc.dram_tensor(nm, shape, F32, kind="ExternalInput").ap()
    d["maskT"] = nc.dram_tensor("maskT", [D, T], F8, kind="ExternalInput").ap()
    d["out_t"] = nc.dram_tensor("out_t", [D, T], F32, kind="ExternalOutput").ap()
    d["xb"] = nc.dram_tensor("xb", [D, T], F32,
                             kind="ExternalOutput" if debug else "Internal").ap()
    if debug:
        for nm, shape, dtp in [
            ("dbg_xh", [128, 1024], F32), ("dbg_q", [128, 1024], F32),
            ("dbg_qfa", [128, 1024], F32), ("dbg_kfa", [128, 1024], F32),
            ("dbg_kta", [128, 1024], F32), ("dbg_vt", [128, 1024], F32),
            ("dbg_sc", [128, 128], F32), ("dbg_on", [128, 1024], F32),
            ("dbg_Sa", [128, 1024], F32), ("dbg_Sb", [128, 1024], F32),
            ("dbg_fxh", [128, 4096], F32), ("dbg_x2q", [128, 4096], F32),
            ("dbg_fz", [128, 2048], F32), ("dbg_hq", [128, 4096], F32),
            ("dbg_fy", [128, 512], F32),
            ("dbg_opre", [128, 128], F32), ("dbg_osq", [128, 128], F32),
            ("dbg_stat", [1, 1024], F32),
            ("dbg_qfb", [128, 1024], F32), ("dbg_kfb", [128, 1024], F32),
            ("dbg_fmqb", [128, 2], F32),
            ("dbg_w1q", [128, 1024], F32), ("dbg_b1q", [128, 32], F32),
            ("dbg_b1t", [128, 32], F32), ("dbg_bf", [128, 32], F32),
            ("dbg_bw", [128, 32], F32), ("dbg_br", [128, 32], F32),
            ("dbg_bt", [128, 32], F32), ("dbg_bg", [128, 32], F32),
            ("dbg_bis", [128, 32], F32), ("dbg_bb1", [128, 32], F32),
            ("dbg_bb2", [128, 32], F32), ("dbg_bcomb", [128, 32], F32),
            ("dbg_sch", [128, 8 * 128], F32),
            ("dbg_oph", [128, 8 * 128], F32),
        ]:
            d[nm] = nc.dram_tensor(nm, shape, dtp,
                                   kind="ExternalOutput").ap()

    with contextlib.ExitStack() as st:
        def mksbuf(stack):
            def sbuf(name, shape, dt=F32):
                return stack.enter_context(nc.sbuf_tensor(name, shape, dt)).ap()
            return sbuf

        sbuf = mksbuf(st)

        def psum(name, shape):
            return st.enter_context(nc.psum_tensor(name, shape, F32)).ap()

        cb = sbuf("cb", [128, 8])
        CIDX = {"tiny": 0, "eps": 1, "c6": 2, "c14": 3}
        CVALS = {"tiny": 1e-37, "eps": EPS, "c6": 6.0, "c14": 14.0}

        def cba(nm, row128=True):
            i = CIDX[nm]
            return cb[:, i:i + 1] if row128 else cb[0:1, i:i + 1]

        fmq = sbuf("fmq", [128, DK])
        fmk = sbuf("fmk", [128, DK])
        fmqb = sbuf("fmqb", [128, 2])
        fmkb = sbuf("fmkb", [128, 2])
        fmkb_row = sbuf("fmkb_row", [1, DK])
        rmsw = sbuf("rmsw", [128, 1])
        b1t = sbuf("b1t", [128, 32])
        b2t = sbuf("b2t", [128, 8])
        b1q = sbuf("b1q", [128, 32])
        b2q = sbuf("b2q", [128, 8])
        triu = sbuf("triu_s", [128, CH])
        ones = sbuf("ones_s", [128, 1])
        ones_r = sbuf("ones_r", [1, 128])
        fmkb_bc = sbuf("fmkb_bc", [128, DK])
        S_a = sbuf("S_a", [128, H * DK])
        S_b = sbuf("S_b", [128, H * DK])
        stat = sbuf("stat", [1, 1024])
        stat2 = sbuf("stat2", [1, 1024])
        stat_i = sbuf("stat_i", [1, 256], I32)

        attn_stack = contextlib.ExitStack()
        asb = mksbuf(attn_stack)
        wq = asb("wq", [128, 8 * D])
        wk = asb("wk", [128, 8 * D])
        wv = asb("wv", [128, 8 * D])
        wo = asb("wo", [128, 8 * D])
        xc = asb("xc", [128, 8 * CH])
        sqc = asb("sqc", [128, 8 * CH])
        xh = asb("xh", [128, 8 * CH])
        qkv = asb("qkv", [128, 8 * CH])
        kfm = asb("kfm", [128, 8 * CH])
        qf_a = asb("qf_a", [128, H * CH])
        qf_b = asb("qf_b", [128, H * CH])
        kf_a = asb("kf_a", [128, H * CH])
        kf_b = asb("kf_b", [128, H * CH])
        kt_a = asb("kt_a", [128, H * DK])
        kt_b = asb("kt_b", [128, H * DK])
        vt = asb("vt", [128, D])
        sc = asb("sc", [128, CH])
        yt = asb("yt", [128, CH])
        o_n = asb("o_n", [128, H * CH])

        ps_p = psum("ps_p", [128, 512])
        ps_q = psum("ps_q", [128, 512])
        ps_v = psum("ps_v", [128, 1024])
        ps_s = psum("ps_s", [128, 512])
        ps_o = psum("ps_o", [128, 512])
        ps_d = psum("ps_d", [128, 512])

        sems = {e: st.enter_context(nc.semaphore("sem_" + e)) for e in ENGINES}

        nc.const_aps.register(F32, -0.125, cb[:, 7:8])
        nc.const_aps.register(F32, -4 * LN2, cb[:, 6:7])

        sch = Sched()

        def stage(deps, items):
            return sch.stage(deps, items)

        items = [("vector", (lambda i, v: lambda e: e.memset(
            cb[:, i:i + 1], v))(CIDX[nm], CVALS[nm])) for nm in CIDX]
        items.append(("vector", lambda e: e.memset(cb[:, 7:8], -0.125)))
        items.append(("vector", lambda e: e.memset(cb[:, 6:7], -4 * LN2)))
        items.append(("vector", lambda e: e.memset(ones_r, 1.0)))
        items.append(("vector", lambda e: e.memset(S_a, 0.0)))
        items.append(("vector", lambda e: e.memset(S_b, 0.0)))
        loads = [
            (fmq, d["fmqT"]), (fmk, d["fmkT"]),
            (triu, d["triu"]), (ones, d["ones"]),
            (rmsw, d["rms_w"]),
            (fmqb[:, 0:1], d["fmq_b"]),
            (fmkb[:, 0:1], d["fmk_b"]),
            (fmkb_row, d["fmk_br"]),
            (b1t, d["b1"]),
            (b2t, d["b2"]),
            (wq.rearrange("p (c n) -> p c n", c=8),
             d["wqT"].rearrange("(c p) n -> p c n", p=128)),
            (wk.rearrange("p (c n) -> p c n", c=8),
             d["wkT"].rearrange("(c p) n -> p c n", p=128)),
            (wv.rearrange("p (c n) -> p c n", c=8),
             d["wvT"].rearrange("(c p) n -> p c n", p=128)),
            (wo.rearrange("p (c n) -> p c n", c=8),
             d["woT"].rearrange("(c p) n -> p c n", p=128)),
        ]
        for dst, src in loads:
            items.append(("sync", (lambda dst, src: lambda e: e.dma_start(
                out=dst, in_=src))(dst, src)))
        prev = stage(sch.snap(), items)

        prev = stage(prev, [("scalar", lambda e: e.activation(
            fmqb[:, 1:2], fmqb[:, 0:1], AF.Copy, scale=-1.0,
            bias=-4 * LN2))])
        prev = stage(prev, [("scalar", lambda e: e.activation(
            fmqb[:, 0:1], fmqb[:, 0:1], AF.Copy, scale=1.0,
            bias=-4 * LN2))])
        prev = stage(prev, [("scalar", lambda e: e.activation(
            fmkb[:, 1:2], fmkb[:, 0:1], AF.Copy, scale=-1.0))])
        prev = stage(prev, [("tensor", lambda e: e.matmul(
            ps_v[:, 0:DK], ones_r, fmkb_row, start=True, stop=True))])
        prev = stage(prev, [("scalar", lambda e: e.activation(
            fmkb_bc, ps_v[:, 0:DK], AF.Copy))])

        def wslice(wt, c, dd):
            return wt[:, c * D + dd * 128: c * D + (dd + 1) * 128]

        def q_ops(z, out_q, M, clamp, signed, ties, width):
            ops = []
            onv = lambda f: ops.append(("vector", f))
            ons = lambda f: ops.append(("scalar", f))
            l, f, w = QSA[:, :width], QSB[:, :width], QSC[:, :width]
            tt, r = QSE[:, :width], QSF[:, :width]
            bs = QSG[:, :width]
            isx, sx = QSD[:, :width], QSH[:, :width]
            gE, g = QIA[:, :width], QIB[:, :width]
            bEi, p = QIC[:, :width], QIA[:, :width]
            tie, odd = QSG[:, :width], QSD[:, :width]
            if signed:
                ons(lambda e: e.activation(tt, z, AF.Square))
                ons(lambda e: e.activation(l, tt, AF.Ln, bias=cba("tiny")))
                c, inv = 1.0 / (2 * LN2), 2 * LN2
            else:
                ons(lambda e: e.activation(l, z, AF.Ln, bias=cba("tiny")))
                c, inv = 1.0 / LN2, LN2
            ons((lambda c: lambda e: e.activation(gE, l, AF.Copy, scale=c))(c))
            onv((lambda inv: lambda e: e.scalar_tensor_tensor(
                bEi, gE, inv, l, ALU.mult, ALU.is_gt))(inv))
            onv(lambda e: e.scalar_tensor_tensor(
                f, bEi, -1.0, gE, ALU.mult, ALU.add))
            ons((lambda cl: lambda e: e.activation(
                w, f, AF.Relu, bias=cba("c%d" % cl)))(clamp))
            if ties:
                onv((lambda MF: lambda e: e.tensor_scalar(
                    gE, w, -M23, MF * M23, ALU.mult, ALU.add))(
                        float(127 + M + clamp)))
                onv(lambda e: e.tensor_tensor(tt, z, gE.bitcast(F32), ALU.mult))
            else:
                ons(lambda e: e.activation(isx, w, AF.Exp, scale=-LN2))
                onv((lambda s_: lambda e: e.scalar_tensor_tensor(
                    tt, z, s_, isx, ALU.mult, ALU.mult))(
                        float(2.0 ** (M + clamp))))
            if signed:
                onv(lambda e: e.tensor_copy(g, tt))
                onv(lambda e: e.scalar_tensor_tensor(
                    r, g, 0.5, tt, ALU.add, ALU.is_le))
                onv(lambda e: e.scalar_tensor_tensor(
                    bs, g, -0.5, tt, ALU.add, ALU.is_gt))
                onv(lambda e: e.scalar_tensor_tensor(
                    r, bs, -1.0, r, ALU.mult, ALU.add))
                onv(lambda e: e.tensor_tensor(r, r, g, ALU.add))
            else:
                ons(lambda e: e.activation(g, tt, AF.Copy, bias=-0.125))
                onv(lambda e: e.scalar_tensor_tensor(
                    r, g, 0.5, tt, ALU.add, ALU.is_le))
                onv(lambda e: e.tensor_tensor(r, r, g, ALU.add))
            if ties:
                onv(lambda e: e.scalar_tensor_tensor(
                    tie, r, -0.5, tt, ALU.add, ALU.is_equal))
                onv(lambda e: e.tensor_scalar(odd, r, 0.5, None, ALU.mult))
                onv(lambda e: e.tensor_copy(p, odd))
                onv(lambda e: e.tensor_tensor(odd, odd, p, ALU.not_equal))
                onv(lambda e: e.tensor_tensor(tie, tie, odd, ALU.mult))
                onv(lambda e: e.tensor_tensor(r, r, tie, ALU.subtract))
                onv((lambda MF: lambda e: e.tensor_scalar(
                    gE, w, M23, MF * M23, ALU.mult, ALU.add))(
                        float(127 - M - clamp)))
                onv(lambda e: e.tensor_tensor(
                    out_q, r, gE.bitcast(F32), ALU.mult))
            else:
                ons(lambda e: e.activation(sx, w, AF.Exp, scale=LN2))
                onv((lambda s_: lambda e: e.scalar_tensor_tensor(
                    out_q, r, s_, sx, ALU.mult, ALU.mult))(
                        float(2.0 ** (-(M + clamp)))))
            return ops

        def q_h(z, out_q, width):
            ops = []
            onv = lambda f: ops.append(("vector", f))
            ons = lambda f: ops.append(("scalar", f))
            l, f = QSA[:, :width], QSB[:, :width]
            w14, w6 = QSC[:, :width], QSD[:, :width]
            tt, r510 = QSF[:, :width], QSE[:, :width]
            r43 = QSG[:, :width]
            isx, sx = QSH[:, :width], QSH[:, :width]
            gE, g = QIA[:, :width], QIB[:, :width]
            bEi, p = QIC[:, :width], QIA[:, :width]
            tie, odd = QSA[:, :width], QSB[:, :width]
            ons(lambda e: e.activation(l, z, AF.Ln, bias=cba("tiny")))
            ons(lambda e: e.activation(gE, l, AF.Copy, scale=1.0 / LN2))
            onv(lambda e: e.scalar_tensor_tensor(
                bEi, gE, LN2, l, ALU.mult, ALU.is_gt))
            onv(lambda e: e.scalar_tensor_tensor(
                f, bEi, -1.0, gE, ALU.mult, ALU.add))
            ons(lambda e: e.activation(w14, f, AF.Relu, bias=cba("c14")))
            ons(lambda e: e.activation(w6, f, AF.Relu, bias=cba("c6")))
            ons(lambda e: e.activation(isx, w14, AF.Exp, scale=-LN2))
            onv(lambda e: e.scalar_tensor_tensor(
                tt, z, float(2.0 ** 24), isx, ALU.mult, ALU.mult))
            ons(lambda e: e.activation(g, tt, AF.Copy, bias=-0.125))
            onv(lambda e: e.scalar_tensor_tensor(
                r510, g, 0.5, tt, ALU.add, ALU.is_le))
            onv(lambda e: e.tensor_tensor(r510, r510, g, ALU.add))
            onv(lambda e: e.tensor_tensor(w14, w14, w6, ALU.subtract))
            onv(lambda e: e.tensor_scalar(
                gE, w14, M23, (127.0 - 15.0) * M23, ALU.mult, ALU.add))
            onv(lambda e: e.tensor_tensor(tt, r510, gE.bitcast(F32), ALU.mult))
            ons(lambda e: e.activation(g, tt, AF.Copy, bias=-0.125))
            onv(lambda e: e.scalar_tensor_tensor(
                r43, g, 0.5, tt, ALU.add, ALU.is_le))
            onv(lambda e: e.tensor_tensor(r43, r43, g, ALU.add))
            onv(lambda e: e.scalar_tensor_tensor(
                tie, r43, -0.5, tt, ALU.add, ALU.is_equal))
            onv(lambda e: e.tensor_scalar(odd, r43, 0.5, None, ALU.mult))
            onv(lambda e: e.tensor_copy(p, odd))
            onv(lambda e: e.tensor_tensor(odd, odd, p, ALU.not_equal))
            onv(lambda e: e.tensor_tensor(tie, tie, odd, ALU.mult))
            onv(lambda e: e.tensor_tensor(r43, r43, tie, ALU.subtract))
            ons(lambda e: e.activation(sx, w6, AF.Exp, scale=LN2))
            onv(lambda e: e.scalar_tensor_tensor(
                out_q, r43, float(2.0 ** -9), sx, ALU.mult, ALU.mult))
            return ops

        def run_serial(prev, ops):
            for eng, f in ops:
                prev = stage(prev, [(eng, f)])
            return prev

        def emit_ln(prev, src_ap, t0, width, x_tile, sq_tile, out_tile):
            x3 = x_tile.rearrange("p (c t) -> p c t", c=8)
            sq3 = sq_tile.rearrange("p (c t) -> p c t", c=8)
            o3 = out_tile.rearrange("p (c t) -> p c t", c=8)
            prev = stage(prev, [("sync", (lambda: lambda e: e.dma_start(
                out=x3, in_=src_ap.rearrange("(c p) t -> p c t", p=128)
                [:, :, t0:t0 + width]))())])
            prev = stage(prev, [("scalar", lambda e: e.activation(
                sq_tile, x_tile, AF.Square))])

            def mk_stats(e):
                for c in range(8):
                    e.matmul(ps_s[0:1, 0:width], ones, x3[:, c, :],
                             start=(c == 0), stop=(c == 7))
                for c in range(8):
                    ins = e.matmul(ps_d[0:1, 0:width], ones, sq3[:, c, :],
                                   start=(c == 0), stop=(c == 7))
                return ins
            prev = stage(prev, [("tensor", mk_stats)])

            def mk_sv1(e):
                e.tensor_scalar(stat[0:1, 0:width], ps_s[0:1, 0:width],
                                1.0 / D, None, ALU.mult)
                return e.tensor_scalar(stat2[0:1, 0:width],
                                       ps_d[0:1, 0:width],
                                       1.0 / D, None, ALU.mult)
            prev = stage(prev, [("vector", mk_sv1)])

            def mk_sv2(e):
                return e.tensor_tensor(stat[0:1, width:2 * width],
                                       stat[0:1, 0:width],
                                       stat[0:1, 0:width], ALU.mult)
            prev = stage(prev, [("vector", mk_sv2)])

            def mk_sv3(e):
                return e.tensor_tensor(stat2[0:1, 0:width],
                                       stat2[0:1, 0:width],
                                       stat[0:1, width:2 * width],
                                       ALU.subtract)
            prev = stage(prev, [("vector", mk_sv3)])

            def mk_ss(e):
                e.activation(stat2[0:1, width:2 * width],
                             stat2[0:1, 0:width], AF.Ln,
                             bias=cba("eps", False))
                return e.activation(stat2[0:1, 0:width],
                                    stat2[0:1, width:2 * width],
                                    AF.Exp, scale=-0.5)
            prev = stage(prev, [("scalar", mk_ss)])

            def mk_bc(e):
                e.matmul(ps_v[:, 0:width], ones_r, stat[0:1, 0:width],
                         start=True, stop=True)
                return e.matmul(ps_v[:, 512:512 + width], ones_r,
                                stat2[0:1, 0:width], start=True, stop=True)
            prev = stage(prev, [("tensor", mk_bc)])
            m_b = ps_v[:, 0:width].unsqueeze(1).broadcast_to([128, 8, width])
            r_b = ps_v[:, 512:512 + width].unsqueeze(1)\
                .broadcast_to([128, 8, width])

            def mk_ap(e):
                e.tensor_tensor(sq3, x3, m_b, ALU.subtract)
                return e.tensor_tensor(o3, sq3, r_b, ALU.mult)
            prev = stage(prev, [("vector", mk_ap)])
            return prev

        def emit_proj(prev, wt, out_sb, ps):
            for dd in range(8):
                def mk_mm(e, dd=dd):
                    for c in range(8):
                        ins = e.matmul(ps[:, 0:CH], wslice(wt, c, dd),
                                       xh[:, c * CH:(c + 1) * CH],
                                       start=(c == 0), stop=(c == 7))
                    return ins
                prev = stage(prev, [("tensor", mk_mm)])

                def mk_ev(e, dd=dd):
                    e.activation(out_sb[:, dd * CH:(dd + 1) * CH],
                                 ps[:, 0:CH], AF.Copy)
                prev = stage(prev, [("scalar", mk_ev)])
            return prev

        def emit_v_tm(prev):
            def mk_mm(e):
                wv3 = wv.rearrange("p (c n) -> p c n", c=8)
                for half in range(2):
                    for c in range(8):
                        ins = e.matmul(ps_v[:, half * 512:(half + 1) * 512],
                                       xh[:, c * CH:(c + 1) * CH],
                                       wv3[:, c, half * 512:(half + 1) * 512],
                                       start=(c == 0), stop=(c == 7))
                return ins
            prev = stage(prev, [("tensor", mk_mm)])

            def mk_ev(e):
                e.activation(vt[:, 0:512], ps_v[:, 0:512], AF.Copy)
                return e.activation(vt[:, 512:1024], ps_v[:, 512:1024],
                                    AF.Copy)
            return stage(prev, [("scalar", mk_ev)])

        def emit_fm(prev, src_fm, fmw, fmb, out_a, out_b):
            for h in range(H):
                def mk_mm(e, h=h):
                    e.matmul(ps_s[:, 0:CH], fmw,
                             src_fm[:, h * CH:(h + 1) * CH],
                             start=True, stop=True)
                prev = stage(prev, [("tensor", mk_mm)])

                def mk_ev(e, h=h):
                    e.activation(out_a[:, h * CH:(h + 1) * CH], ps_s[:, 0:CH],
                                 AF.Exp, bias=fmb[:, 0:1])
                    return e.activation(out_b[:, h * CH:(h + 1) * CH],
                                        ps_s[:, 0:CH], AF.Exp, scale=-1.0,
                                        bias=fmb[:, 1:2])
                prev = stage(prev, [("scalar", mk_ev)])
            return prev

        def emit_fm_tm(prev, src_fm, fmw, fmb_row, out_a, out_b):
            b_b = fmkb_bc
            for h in range(H):
                def mk_mm(e, h=h):
                    e.matmul(ps_s[:, 128:256],
                             src_fm[:, h * CH:(h + 1) * CH], fmw,
                             start=True, stop=True)
                prev = stage(prev, [("tensor", mk_mm)])

                def mk_add(e, h=h):
                    e.tensor_tensor(yt, ps_s[:, 128:256], b_b, ALU.add)
                prev = stage(prev, [("vector", mk_add)])

                def mk_ev(e, h=h):
                    e.activation(out_a[:, h * DK:(h + 1) * DK], yt, AF.Exp)
                    return e.activation(out_b[:, h * DK:(h + 1) * DK], yt,
                                        AF.Exp, scale=-1.0)
                prev = stage(prev, [("scalar", mk_ev)])
            return prev

        def emit_dS(prev):
            for h in range(H):
                hs = slice(h * CH, (h + 1) * CH)

                def mk_ds(e, hs=hs):
                    e.matmul(ps_d[:, 0:CH], kt_a[:, hs], vt[:, hs],
                             start=True, stop=True)
                    return e.matmul(ps_d[:, CH:2 * CH], kt_b[:, hs],
                                    vt[:, hs], start=True, stop=True)
                prev = stage(prev, [("tensor", mk_ds)])

                def mk_add(e, hs=hs):
                    e.tensor_tensor(S_a[:, hs], S_a[:, hs], ps_d[:, 0:CH],
                                    ALU.add)
                    return e.tensor_tensor(S_b[:, hs], S_b[:, hs],
                                           ps_d[:, CH:2 * CH], ALU.add)
                prev = stage(prev, [("vector", mk_add)])
            return prev

        def dump(prev, name, src_tile, width):
            if not debug:
                return prev
            def mk(e, name=name, src_tile=src_tile, width=width):
                return e.dma_start(out=d[name][:, 0:width],
                                   in_=src_tile[:, 0:width])
            return stage(prev, [("gpsimd", mk)])

        for j in range(NCH):
            t0 = j * CH
            prev = emit_ln(prev, d["xpre_t"], t0, CH, xc, sqc, xh)
            prev = emit_proj(prev, wk, kfm, ps_p)
            prev = emit_fm_tm(prev, kfm, fmk, fmkb_row, kt_a, kt_b)
            prev = emit_v_tm(prev)
            prev = emit_dS(prev)

        prev = dump(prev, "dbg_Sa", S_a, H * DK)
        prev = dump(prev, "dbg_Sb", S_b, H * DK)

        for j in range(NCH):
            t0 = j * CH
            prev = emit_ln(prev, d["x_t"], t0, CH, xc, sqc, xh)
            if debug and j == 0:
                prev = dump(prev, "dbg_xh", xh, 1024)
            prev = emit_proj(prev, wq, qkv, ps_p)
            if debug and j == 0:
                prev = dump(prev, "dbg_q", qkv, 1024)
            prev = emit_fm(prev, qkv, fmq, fmqb, qf_a, qf_b)
            if debug and j == 0:
                prev = dump(prev, "dbg_qfa", qf_a, 1024)
            prev = emit_proj(prev, wk, kfm, ps_q)
            prev = emit_fm(prev, kfm, fmk, fmkb, kf_a, kf_b)
            prev = emit_fm_tm(prev, kfm, fmk, fmkb_row, kt_a, kt_b)
            prev = emit_v_tm(prev)
            if debug and j == 0:
                prev = dump(prev, "dbg_fmqb", fmqb, 2)
                prev = dump(prev, "dbg_qfb", qf_b, 1024)
                prev = dump(prev, "dbg_kfb", kf_b, 1024)
                prev = dump(prev, "dbg_kfa", kf_a, 1024)
                prev = dump(prev, "dbg_kta", kt_a, 1024)
                prev = dump(prev, "dbg_vt", vt, 1024)
            for h in range(H):
                hs = slice(h * CH, (h + 1) * CH)

                def mk_sc(e, hs=hs):
                    e.matmul(ps_s[:, 0:CH], kf_a[:, hs], qf_a[:, hs],
                             start=True, stop=False)
                    return e.matmul(ps_s[:, 0:CH], kf_b[:, hs], qf_b[:, hs],
                                    start=False, stop=True)
                prev = stage(prev, [("tensor", mk_sc)])

                def mk_mask(e, hs=hs):
                    e.tensor_tensor(sc, ps_s[:, 0:CH], triu, ALU.mult)
                prev = stage(prev, [("vector", mk_mask)])

                if debug and j == 0:
                    def mkdsc(e, h=h):
                        return e.dma_start(out=d["dbg_sch"][:, h * 128:
                                                            (h + 1) * 128],
                                           in_=sc)
                    prev = stage(prev, [("gpsimd", mkdsc)])

                def mk_o(e, hs=hs):
                    e.matmul(ps_o[:, 0:CH], vt[:, hs], sc,
                             start=True, stop=False)
                    e.matmul(ps_o[:, 0:CH], S_a[:, hs], qf_a[:, hs],
                             start=False, stop=False)
                    return e.matmul(ps_o[:, 0:CH], S_b[:, hs], qf_b[:, hs],
                                    start=False, stop=True)
                prev = stage(prev, [("tensor", mk_o)])

                def mk_oev(e, hs=hs):
                    e.activation(o_n[:, hs], ps_o[:, 0:CH], AF.Copy)
                    return e.activation(yt, ps_o[:, 0:CH], AF.Square)
                prev = stage(prev, [("scalar", mk_oev)])

                if debug and j == 0:
                    def mkdop(e, h=h):
                        return e.dma_start(out=d["dbg_oph"][:, h * 128:
                                                            (h + 1) * 128],
                                           in_=o_n[:, h * CH:(h + 1) * CH])
                    prev = stage(prev, [("gpsimd", mkdop)])
                if debug and j == 0 and h == 0:
                    prev = dump(prev, "dbg_opre", o_n[:, 0:CH], 128)
                    prev = dump(prev, "dbg_osq", yt, 128)

                def mk_osum(e, hs=hs):
                    e.matmul(ps_s[0:1, 2 * CH:3 * CH], ones, yt,
                             start=True, stop=True)
                prev = stage(prev, [("tensor", mk_osum)])

                def mk_om1(e, hs=hs):
                    return e.tensor_scalar(stat[0:1, 3 * CH:4 * CH],
                                           ps_s[0:1, 2 * CH:3 * CH],
                                           1.0 / DK, EPS, ALU.mult, ALU.add)
                prev = stage(prev, [("vector", mk_om1)])

                def mk_om2(e, hs=hs):
                    return e.tensor_scalar(
                        stat_i[0:1, 0:CH],
                        stat[0:1, 3 * CH:4 * CH].bitcast(I32),
                        float(2.0 ** -23), -127.0, ALU.mult, ALU.add)
                prev = stage(prev, [("vector", mk_om2)])

                def mk_om3(e, hs=hs):
                    return e.tensor_scalar(stat_i[0:1, CH:2 * CH],
                                           stat_i[0:1, 0:CH], -M23,
                                           127.0 * M23, ALU.mult, ALU.add)
                prev = stage(prev, [("vector", mk_om3)])

                def mk_om4(e, hs=hs):
                    return e.tensor_tensor(
                        stat[0:1, 4 * CH:5 * CH], stat[0:1, 3 * CH:4 * CH],
                        stat_i[0:1, CH:2 * CH].bitcast(F32), ALU.mult)
                prev = stage(prev, [("vector", mk_om4)])

                def mk_or(e, hs=hs):
                    return e.activation(stat2[0:1, 0:CH],
                                        stat[0:1, 4 * CH:5 * CH], AF.Ln)
                prev = stage(prev, [("scalar", mk_or)])

                def mk_or2(e, hs=hs):
                    return e.scalar_tensor_tensor(
                        stat[0:1, 5 * CH:6 * CH], stat_i[0:1, 0:CH], LN2,
                        stat2[0:1, 0:CH], ALU.mult, ALU.add)
                prev = stage(prev, [("vector", mk_or2)])

                def mk_or3(e, hs=hs):
                    return e.activation(stat[0:1, 5 * CH:6 * CH],
                                        stat[0:1, 5 * CH:6 * CH],
                                        AF.Exp, scale=-0.5)
                prev = stage(prev, [("scalar", mk_or3)])

                if debug and j == 0 and h == 0:
                    def mkds(e):
                        return e.dma_start(out=d["dbg_stat"], in_=stat)
                    prev = stage(prev, [("gpsimd", mkds)])

                def mk_obc(e, hs=hs):
                    return e.matmul(ps_v[:, 0:CH], ones_r,
                                    stat[0:1, 5 * CH:6 * CH],
                                    start=True, stop=True)
                prev = stage(prev, [("tensor", mk_obc)])

                def mk_oap(e, hs=hs):
                    return e.scalar_tensor_tensor(
                        o_n[:, hs], o_n[:, hs], rmsw[:, 0:1],
                        ps_v[:, 0:CH], ALU.mult, ALU.mult)
                prev = stage(prev, [("vector", mk_oap)])

                def mk_ds(e, hs=hs):
                    e.matmul(ps_d[:, 0:CH], kt_a[:, hs], vt[:, hs],
                             start=True, stop=True)
                    return e.matmul(ps_d[:, CH:2 * CH], kt_b[:, hs],
                                    vt[:, hs], start=True, stop=True)
                prev = stage(prev, [("tensor", mk_ds)])

                def mk_sadd(e, hs=hs):
                    e.tensor_tensor(S_a[:, hs], S_a[:, hs], ps_d[:, 0:CH],
                                    ALU.add)
                    return e.tensor_tensor(S_b[:, hs], S_b[:, hs],
                                           ps_d[:, CH:2 * CH], ALU.add)
                prev = stage(prev, [("vector", mk_sadd)])
            if debug and j == 0:
                prev = dump(prev, "dbg_on", o_n, 1024)
                prev = dump(prev, "dbg_sc", sc, 128)
            for dd in range(8):
                def mk_wo(e, dd=dd):
                    for h in range(H):
                        ins = e.matmul(ps_o[:, 0:CH], wslice(wo, h, dd),
                                       o_n[:, h * CH:(h + 1) * CH],
                                       start=(h == 0), stop=(h == 7))
                    return ins
                prev = stage(prev, [("tensor", mk_wo)])

                def mk_res(e, dd=dd):
                    e.tensor_tensor(xc[:, dd * CH:(dd + 1) * CH],
                                    xc[:, dd * CH:(dd + 1) * CH],
                                    ps_o[:, 0:CH], ALU.add)
                prev = stage(prev, [("vector", mk_res)])

            def mk_st(e, t0=t0):
                e.dma_start(
                    out=d["xb"].rearrange("(c p) t -> p c t", p=128)
                    [:, :, t0:t0 + CH],
                    in_=xc.rearrange("p (c t) -> p c t", c=8))
            prev = stage(prev, [("gpsimd", mk_st)])

        attn_stack.close()
        ffn_stack = contextlib.ExitStack()
        fsb = mksbuf(ffn_stack)
        QW = 1024
        QSA = fsb("QSA", [128, QW]); QSB = fsb("QSB", [128, QW])
        QSC = fsb("QSC", [128, QW]); QSD = fsb("QSD", [128, QW])
        QSE = fsb("QSE", [128, QW]); QSF = fsb("QSF", [128, QW])
        QSG = fsb("QSG", [128, QW]); QSH = fsb("QSH", [128, QW])
        QIA = fsb("QIA", [128, QW], I32)
        QIB = fsb("QIB", [128, QW], I32)
        QIC = fsb("QIC", [128, QW], I32)
        fx = fsb("fx", [128, 8 * FT])
        fxh = fsb("fxh", [128, 8 * FT])
        fz = fsb("fz", [128, 1024])
        fy = fsb("fy", [128, FT])
        x2q = fsb("x2q", [128, 8 * FT], F8)
        hq = fsb("hq", [128, 32 * FT], F8)
        w1q = fsb("w1q", [128, 8 * DH], F8)
        w2q = fsb("w2q", [128, 32 * D], F8)
        msk8 = fsb("msk8", [128, 8 * FT], F8)
        wload = fsb("wload", [128, 1024])

        def q_weight(prev, srcT, nchunk, cols, dst_fp8):
            src3 = srcT.rearrange("(c p) n -> p c n", p=128)
            step = 1024
            for c in range(nchunk):
                for s0 in range(0, cols, step):
                    wdt = min(step, cols - s0)

                    def mk_ld(e, c=c, s0=s0, wdt=wdt):
                        e.dma_start(out=wload[:, 0:wdt],
                                    in_=src3[:, c, s0:s0 + wdt])
                    prev = stage(prev, [("sync", mk_ld)])
                    prev = run_serial(prev, q_ops(
                        wload[:, 0:wdt],
                        dst_fp8[:, c * cols + s0:c * cols + s0 + wdt],
                        M=3, clamp=6, signed=True, ties=True, width=wdt))
            return prev

        prev = q_weight(prev, d["w1T"], 8, DH, w1q)
        prev = q_weight(prev, d["w2T"], 32, D, w2q)
        prev = dump(prev, "dbg_b1t", b1t, 32)
        if debug:
            W_ = 32
            z_, out_ = b1t, b1q
            l_, f_, w_ = QSA[:, :W_], QSB[:, :W_], QSC[:, :W_]
            tt_, r_ = QSE[:, :W_], QSF[:, :W_]
            bs_ = QSG[:, :W_]
            gE_, g_ = QIA[:, :W_], QIB[:, :W_]
            bEi_, p_ = QIC[:, :W_], QIA[:, :W_]
            tie_, odd_ = QSG[:, :W_], QSD[:, :W_]
            prev = stage(prev, [("scalar", lambda e: e.activation(tt_, z_, AF.Square))])
            prev = stage(prev, [("scalar", lambda e: e.activation(l_, tt_, AF.Ln, bias=cba("tiny")))])
            prev = stage(prev, [("scalar", lambda e: e.activation(gE_, l_, AF.Copy, scale=1.0 / (2 * LN2)))])
            prev = stage(prev, [("vector", lambda e: e.scalar_tensor_tensor(bEi_, gE_, 2 * LN2, l_, ALU.mult, ALU.is_gt))])
            prev = stage(prev, [("vector", lambda e: e.scalar_tensor_tensor(f_, bEi_, -1.0, gE_, ALU.mult, ALU.add))])
            prev = stage(prev, [("scalar", lambda e: e.activation(w_, f_, AF.Relu, bias=cba("c6")))])
            prev = stage(prev, [("vector", lambda e: e.tensor_scalar(gE_, w_, -M23, 136.0 * M23, ALU.mult, ALU.add))])
            def mkbis(e):
                return e.tensor_copy(QSD[:, 0:W_], gE_)
            prev = stage(prev, [("vector", mkbis)])
            prev = dump(prev, "dbg_bis", QSD, 32)
            prev = stage(prev, [("vector", lambda e: e.tensor_tensor(tt_, z_, gE_.bitcast(F32), ALU.mult))])
            prev = stage(prev, [("vector", lambda e: e.tensor_copy(g_, tt_))])
            prev = stage(prev, [("vector", lambda e: e.scalar_tensor_tensor(r_, g_, 0.5, tt_, ALU.add, ALU.is_le))])
            prev = dump(prev, "dbg_bb1", QSF, 32)
            prev = stage(prev, [("vector", lambda e: e.scalar_tensor_tensor(bs_, g_, -0.5, tt_, ALU.add, ALU.is_gt))])
            prev = dump(prev, "dbg_bb2", QSG, 32)
            prev = stage(prev, [("vector", lambda e: e.scalar_tensor_tensor(r_, bs_, -1.0, r_, ALU.mult, ALU.add))])
            prev = dump(prev, "dbg_bcomb", QSF, 32)
            prev = stage(prev, [("vector", lambda e: e.tensor_tensor(r_, r_, g_, ALU.add))])
            prev = stage(prev, [("vector", lambda e: e.scalar_tensor_tensor(tie_, r_, -0.5, tt_, ALU.add, ALU.is_equal))])
            prev = stage(prev, [("vector", lambda e: e.tensor_scalar(odd_, r_, 0.5, None, ALU.mult))])
            prev = stage(prev, [("vector", lambda e: e.tensor_copy(p_, odd_))])
            prev = stage(prev, [("vector", lambda e: e.tensor_tensor(odd_, odd_, p_, ALU.not_equal))])
            prev = stage(prev, [("vector", lambda e: e.tensor_tensor(tie_, tie_, odd_, ALU.mult))])
            prev = stage(prev, [("vector", lambda e: e.tensor_tensor(r_, r_, tie_, ALU.subtract))])
            prev = stage(prev, [("vector", lambda e: e.tensor_scalar(gE_, w_, M23, 118.0 * M23, ALU.mult, ALU.add))])
            prev = stage(prev, [("vector", lambda e: e.tensor_tensor(out_, r_, gE_.bitcast(F32), ALU.mult))])
        else:
            prev = run_serial(prev, q_ops(b1t, b1q, M=3, clamp=6, signed=True,
                                          ties=True, width=32))
        if debug:
            prev = dump(prev, "dbg_bf", QSB, 32)
            prev = dump(prev, "dbg_bw", QSC, 32)
            prev = dump(prev, "dbg_br", QSF, 32)
            prev = dump(prev, "dbg_bt", QSE, 32)
            def mkg(e):
                return e.tensor_copy(QSD[:, 0:32], QIB[:, 0:32])
            prev = stage(prev, [("vector", mkg)])
            prev = dump(prev, "dbg_bg", QSD, 32)
        prev = run_serial(prev, q_ops(b2t, b2q, M=3, clamp=6, signed=True,
                                      ties=True, width=8))

        if debug:
            def mkdw(e):
                return e.tensor_copy(QSA[:, 0:1024], w1q[:, 0:1024])
            prev = stage(prev, [("vector", mkdw)])
            prev = dump(prev, "dbg_w1q", QSA, 1024)
            def mkdb(e):
                return e.tensor_copy(QSB[:, 0:32], b1q)
            prev = stage(prev, [("vector", mkdb)])
            prev = dump(prev, "dbg_b1q", QSB, 32)

        x2q3 = x2q.rearrange("p (c t) -> p c t", c=8)
        hq3 = hq.rearrange("p (c t) -> p c t", c=32)
        fx3 = fx.rearrange("p (c t) -> p c t", c=8)
        w1q3 = w1q.rearrange("p (c n) -> p c n", c=8)
        w2q3 = w2q.rearrange("p (c n) -> p c n", c=32)
        for ti in range(NFT):
            t0 = ti * FT
            prev = emit_ln(prev, d["xb"], t0, FT, fx, fxh, fxh)

            def mk_mld(e, t0=t0):
                e.dma_start(out=msk8.rearrange("p (c t) -> p c t", c=8),
                            in_=d["maskT"].rearrange("(c p) t -> p c t", p=128)
                            [:, :, t0:t0 + FT])
            prev = stage(prev, [("sync", mk_mld)])
            XW = min(1024, 8 * FT)
            for quar in range(8 * FT // XW):
                prev = run_serial(prev, q_ops(
                    fxh[:, quar * XW:(quar + 1) * XW],
                    x2q[:, quar * XW:(quar + 1) * XW],
                    M=3, clamp=6, signed=True, ties=False, width=XW))
            if debug and ti == 0:
                prev = dump(prev, "dbg_fxh", fxh, 8 * FT)
                def mkd(e):
                    return e.tensor_copy(QSA[:, 0:1024], x2q[:, 0:1024])
                prev = stage(prev, [("vector", mkd)])
                prev = dump(prev, "dbg_x2q", QSA, 1024)
            for grp in range(16):
                for sub in range(2):
                    dh = grp * 2 + sub

                    def mk_m(e, dh=dh):
                        for c in range(8):
                            ins = e.matmul(ps_p[:, 0:FT],
                                           w1q3[:, c, dh * 128:(dh + 1) * 128],
                                           x2q3[:, c, :],
                                           start=(c == 0), stop=(c == 7))
                        return ins
                    prev = stage(prev, [("tensor", mk_m)])

                    def mk_e(e, dh=dh, sub=sub):
                        e.activation(fz[:, sub * FT:(sub + 1) * FT],
                                     ps_p[:, 0:FT], AF.Relu,
                                     bias=b1q[:, dh:dh + 1])
                    prev = stage(prev, [("scalar", mk_e)])
                if debug and ti == 0 and grp == 0:
                    prev = dump(prev, "dbg_fz", fz, 2 * FT)
                prev = run_serial(prev, q_h(
                    fz[:, 0:2 * FT], hq[:, grp * 2 * FT:(grp + 1) * 2 * FT],
                    width=2 * FT))
                if debug and ti == 0 and grp == 0:
                    def mkd2(e):
                        return e.tensor_copy(QSA[:, 0:2 * FT],
                                             hq[:, 0:2 * FT])
                    prev = stage(prev, [("vector", mkd2)])
                    prev = dump(prev, "dbg_hq", QSA, 2 * FT)
            for dd in range(8):
                def mk_m2(e, dd=dd):
                    for c in range(32):
                        ins = e.matmul(ps_q[:, 0:FT],
                                       w2q3[:, c, dd * 128:(dd + 1) * 128],
                                       hq3[:, c, :],
                                       start=(c == 0), stop=(c == 31))
                    return ins
                prev = stage(prev, [("tensor", mk_m2)])

                def mk_e2(e, dd=dd):
                    return e.tensor_scalar(fy, ps_q[:, 0:FT],
                                           b2q[:, dd:dd + 1], None, ALU.add)
                prev = stage(prev, [("vector", mk_e2)])
                prev = run_serial(prev, q_ops(
                    fy, fz[:, 0:FT], M=10, clamp=14, signed=True, ties=False,
                    width=FT))

                def mk_dr1(e, dd=dd):
                    return e.tensor_tensor(
                        fy, fz[:, 0:FT],
                        msk8.rearrange("p (c t) -> p c t", c=8)[:, dd, :],
                        ALU.mult)
                prev = stage(prev, [("vector", mk_dr1)])

                def mk_dr2(e, dd=dd):
                    return e.tensor_tensor(fy, fy, fx3[:, dd, :], ALU.add)
                prev = stage(prev, [("vector", mk_dr2)])
                if debug and ti == 0 and dd == 0:
                    prev = dump(prev, "dbg_fy", fy, FT)

                def mk_out(e, dd=dd, t0=t0):
                    e.dma_start(
                        out=d["out_t"].rearrange("(c p) t -> p c t", p=128)
                        [:, dd, t0:t0 + FT], in_=fy)
                prev = stage(prev, [("gpsimd", mk_out)])

        ffn_stack.close()

        prog = sch.program
        final = sch.snap()

        class _EngProxy:
            def __init__(self, eng):
                self._e = eng
                self.last = None

            def __getattr__(self, nm):
                f = getattr(self._e, nm)
                if callable(f):
                    def g(*a, **k):
                        r = f(*a, **k)
                        if hasattr(r, "then_inc"):
                            self.last = r
                        return r
                    return g
                return f

        def engine_runner(ename):
            def run(eng):
                px = _EngProxy(eng)
                for entry in prog:
                    if entry[0] == "wait":
                        _, e2, pe, cnt = entry
                        if e2 == ename:
                            eng.wait_ge(sems[pe], cnt)
                    else:
                        _, e2, fn, inc = entry
                        if e2 == ename:
                            px.last = None
                            r = fn(px)
                            ins = r if hasattr(r, "then_inc") else px.last
                            if inc:
                                ins.then_inc(sems[ename], inc)
                if ename == "gpsimd":
                    for e2 in ENGINES:
                        if final[e2] > 0:
                            eng.wait_ge(sems[e2], final[e2])
            return run

        with nc.Block() as block:
            block.sync(engine_runner("sync"))
            block.scalar(engine_runner("scalar"))
            block.vector(engine_runner("vector"))
            block.tensor(engine_runner("tensor"))
            block.gpsimd(engine_runner("gpsimd"))

    return nc


def _dropout_mask_np():
    import jax
    cpu = jax.devices("cpu")[0]
    with jax.default_device(cpu):
        key = jax.random.key(42, impl="threefry2x32")
        keep = jax.random.bernoulli(key, 0.8, (4, 8192, 1024))
        keep = np.asarray(keep)
    return keep


def kernel(L0_attn_x, L0_attn_Wq, L0_attn_Wk, L0_attn_Wv, L0_attn_Wo,
           L0_attn_fmq_w, L0_attn_fmq_b, L0_attn_fmk_w, L0_attn_fmk_b,
           L0_attn_rms_w, L0_ffn_w1, L0_ffn_b1, L0_ffn_w2, L0_ffn_b2):
    x = np.ascontiguousarray(np.asarray(L0_attn_x, dtype=np.float32))
    B, N, _ = x.shape
    keep = _dropout_mask_np()
    maskmul = np.where(keep, np.float32(1.25), np.float32(0.0))
    maskmul8 = maskmul.astype(ml_dtypes.float8_e4m3)

    wqT = np.ascontiguousarray(np.asarray(L0_attn_Wq).T)
    wkT = np.ascontiguousarray(np.asarray(L0_attn_Wk).T)
    wvT = np.ascontiguousarray(np.asarray(L0_attn_Wv).T)
    woT = np.ascontiguousarray(np.asarray(L0_attn_Wo).T)
    fmqT = np.ascontiguousarray(np.asarray(L0_attn_fmq_w).T)
    fmkT = np.ascontiguousarray(np.asarray(L0_attn_fmk_w).T)
    w1T = np.ascontiguousarray(np.asarray(L0_ffn_w1).T)
    w2T = np.ascontiguousarray(np.asarray(L0_ffn_w2).T)

    triu = np.triu(np.ones((CH, CH), dtype=np.float32))
    fmqb_c = np.ascontiguousarray(
        np.asarray(L0_attn_fmq_b, dtype=np.float32).reshape(DK, 1))
    fmkb_c = np.ascontiguousarray(
        np.asarray(L0_attn_fmk_b, dtype=np.float32).reshape(DK, 1))
    fmkbr_c = np.ascontiguousarray(
        np.asarray(L0_attn_fmk_b, dtype=np.float32).reshape(1, DK))
    rmsw_c = np.ascontiguousarray(
        np.asarray(L0_attn_rms_w, dtype=np.float32).reshape(DK, 1))
    b1_c = np.ascontiguousarray(
        np.asarray(L0_ffn_b1, dtype=np.float32).reshape(32, 128).T)
    b2_c = np.ascontiguousarray(
        np.asarray(L0_ffn_b2, dtype=np.float32).reshape(8, 128).T)
    onesc = np.ones((128, 1), dtype=np.float32)
    zeros_pre = np.zeros((D, T), dtype=np.float32)

    nc = build_program()
    in_maps = []
    for c in range(8):
        b, half = c // 2, c % 2
        tok = slice(half * T, (half + 1) * T)
        in_maps.append({
            "x_t": np.ascontiguousarray(x[b, tok, :].T),
            "xpre_t": (zeros_pre if half == 0 else
                       np.ascontiguousarray(x[b, 0:T, :].T)),
            "wqT": wqT, "wkT": wkT, "wvT": wvT, "woT": woT,
            "fmqT": fmqT, "fmkT": fmkT,
            "fmq_b": fmqb_c, "fmk_b": fmkb_c, "fmk_br": fmkbr_c,
            "rms_w": rmsw_c,
            "w1T": w1T, "b1": b1_c,
            "w2T": w2T, "b2": b2_c,
            "maskT": np.ascontiguousarray(maskmul8[b, tok, :].T),
            "triu": triu, "ones": onesc,
        })
    res = run_bass_kernel_spmd(nc, in_maps, core_ids=list(range(8)))
    out = np.empty((B, N, D), dtype=np.float32)
    for c in range(8):
        b, half = c // 2, c % 2
        out[b, half * T:(half + 1) * T, :] = \
            np.asarray(res.results[c]["out_t"]).T
    return out


# version 04
# speedup vs baseline: 1.3050x; 1.3050x over previous
import contextlib
import numpy as np
import ml_dtypes

import concourse.bass as bass
import concourse.mybir as mybir
from concourse.bass_utils import run_bass_kernel_spmd

F32 = mybir.dt.float32
F8 = mybir.dt.float8e4
I32 = mybir.dt.int32
AF = mybir.ActivationFunctionType
ALU = mybir.AluOpType

LN2 = float(np.log(2.0))
M23 = 8388608.0

D = 1024
H = 8
DK = 128
T = 4096
CH = 128
NCH = T // CH
FT = 512
NFT = T // FT
DH = 4096
EPS = 1e-5

ENGINES = ("sync", "scalar", "vector", "tensor", "gpsimd")
DMA_ENG = ("sync", "gpsimd")


class OpRec:
    __slots__ = ("engine", "fnidx", "callidx", "reads", "writes", "inc_id",
                 "waits", "is_dma", "_deps")

    def __init__(self, engine, fnidx, callidx, reads, writes, is_dma):
        self.engine = engine
        self.fnidx = fnidx
        self.callidx = callidx
        self.reads = reads
        self.writes = writes
        self.is_dma = is_dma
        self.inc_id = None
        self.waits = []


_ROLES = {
    "matmul": ((0,), (1, 2)),
    "activation": ((0,), (1, 3, 4)),
    "tensor_copy": ((0,), (1,)),
    "tensor_tensor": ((0,), (1, 2)),
    "tensor_scalar": ((0,), (1, 2, 3)),
    "tensor_single_scalar": ((0,), (1, 2)),
    "scalar_tensor_tensor": ((0,), (1, 2, 3)),
    "memset": ((0,), ()),
    "dma_start": ((), ()),
    "tensor_add": ((0,), (1, 2)),
    "tensor_sub": ((0,), (1, 2)),
    "tensor_mul": ((0,), (1, 2)),
}


def _ap_name(a):
    t = getattr(a, "tensor", None)
    if t is None:
        return None
    return getattr(t, "name", None)


class _Recorder:

    class _Ins:
        def then_inc(self, *a, **k):
            return self

    def __init__(self, engine, sink):
        self._engine = engine
        self._sink = sink

    def __getattr__(self, nm):
        def call(*args, **kwargs):
            roles = _ROLES.get(nm)
            reads, writes = set(), set()
            if roles is None:
                for a in list(args) + list(kwargs.values()):
                    n = _ap_name(a)
                    if n:
                        reads.add(n)
                        writes.add(n)
            else:
                for i, a in enumerate(args):
                    n = _ap_name(a)
                    if not n:
                        continue
                    if i in roles[0]:
                        writes.add(n)
                    else:
                        reads.add(n)
                for k, a in kwargs.items():
                    n = _ap_name(a)
                    if not n:
                        continue
                    if k == "out":
                        writes.add(n)
                    else:
                        reads.add(n)
            self._sink.append((nm, reads, writes, nm == "dma_start"))
            return _Recorder._Ins()
        return call


class Sched:

    def __init__(self):
        self.fns = []
        self.ops = []
        self.per_engine_n = {e: 0 for e in ENGINES}
        self.done = {e: 0 for e in ENGINES}
        self._pending_barrier = False

    def stage(self, deps, items):
        for e, fn in items:
            self.fns.append((e, fn, self._pending_barrier))
            self._pending_barrier = False
        return dict(self.done)

    def snap(self):
        return dict(self.done)

    def barrier(self):
        self._pending_barrier = True

    def schedule(self):
        recs = []
        for e, fn, bar in self.fns:
            sink = []
            fn(_Recorder(e, sink))
            recs.append(sink)
        ops = []
        last_write = {}
        readers = {}
        barrier_join = {}
        last_op_eng = {e: None for e in ENGINES}
        for (e, fn, bar), sink in zip(self.fns, recs):
            if bar:
                barrier_join = {k: v for k, v in last_op_eng.items()
                                if v is not None}
            for callidx, (nm, reads, writes, is_dma) in enumerate(sink):
                op = OpRec(e, len(ops), callidx, reads, writes, is_dma)
                deps = list(barrier_join.values())
                for r in reads:
                    o = last_write.get(r)
                    if o is not None:
                        deps.append(o)
                for w in writes:
                    o = last_write.get(w)
                    if o is not None:
                        deps.append(o)
                    deps.extend(readers.get(w, ()))
                op._deps = deps
                for w in writes:
                    last_write[w] = op
                    readers[w] = []
                for r in reads:
                    readers.setdefault(r, []).append(op)
                ops.append(op)
                last_op_eng[e] = op
        needs_inc = [False] * len(ops)
        for op in ops:
            for d in op._deps:
                if d.engine == op.engine == "tensor":
                    continue
                needs_inc[d.fnidx] = True
        inc_total = {e: 0 for e in ENGINES}
        inc_cum = [0] * len(ops)
        for i, op in enumerate(ops):
            if needs_inc[i] or op.is_dma:
                inc_total[op.engine] += 16 if op.is_dma else 1
                op.inc_id = inc_total[op.engine]
            inc_cum[i] = inc_total[op.engine]
        wm = {e: {e2: 0 for e2 in ENGINES} for e in ENGINES}
        for op in ops:
            tgt = {}
            for d in op._deps:
                if d.engine == op.engine == "tensor":
                    continue
                v = d.inc_id if d.inc_id is not None else inc_cum[d.fnidx]
                if v and v > tgt.get(d.engine, 0):
                    tgt[d.engine] = v
            for pe, v in tgt.items():
                if v > wm[op.engine][pe]:
                    op.waits.append((pe, v))
                    wm[op.engine][pe] = v
        self.ops = ops
        self.inc_total = inc_total
        self.op_by_fn = {}
        idx = 0
        for fni, ((e, fn, bar), sink) in enumerate(zip(self.fns, recs)):
            self.op_by_fn[fni] = ops[idx: idx + len(sink)]
            idx += len(sink)


def build_program(debug=False):
    nc = bass.Bass("TRN2", target_bir_lowering=False)

    d = {}
    for nm, shape in [
        ("x_t", [D, T]), ("xpre_t", [D, T]),
        ("wqT", [D, D]), ("wkT", [D, D]), ("wvT", [D, D]), ("woT", [D, D]),
        ("fmqT", [DK, DK]), ("fmkT", [DK, DK]),
        ("fmq_b", [DK, 1]), ("fmk_b", [DK, 1]), ("fmk_br", [1, DK]),
        ("rms_w", [DK, 1]),
        ("w1T", [D, DH]), ("b1", [128, 32]), ("w2T", [DH, D]),
        ("b2", [128, 8]),
        ("triu", [CH, CH]), ("ones", [128, 1]),
    ]:
        d[nm] = nc.dram_tensor(nm, shape, F32, kind="ExternalInput").ap()
    d["maskT"] = nc.dram_tensor("maskT", [D, T], F8, kind="ExternalInput").ap()
    d["out_t"] = nc.dram_tensor("out_t", [D, T], F32, kind="ExternalOutput").ap()
    d["xb"] = nc.dram_tensor("xb", [D, T], F32,
                             kind="ExternalOutput" if debug else "Internal").ap()
    if debug:
        for nm, shape, dtp in [
            ("dbg_xh", [128, 1024], F32), ("dbg_q", [128, 1024], F32),
            ("dbg_qfa", [128, 1024], F32), ("dbg_kfa", [128, 1024], F32),
            ("dbg_kta", [128, 1024], F32), ("dbg_vt", [128, 1024], F32),
            ("dbg_sc", [128, 128], F32), ("dbg_on", [128, 1024], F32),
            ("dbg_Sa", [128, 1024], F32), ("dbg_Sb", [128, 1024], F32),
            ("dbg_fxh", [128, 4096], F32), ("dbg_x2q", [128, 4096], F32),
            ("dbg_fz", [128, 2048], F32), ("dbg_hq", [128, 4096], F32),
            ("dbg_fy", [128, 512], F32),
            ("dbg_opre", [128, 128], F32), ("dbg_osq", [128, 128], F32),
            ("dbg_stat", [1, 1024], F32),
            ("dbg_qfb", [128, 1024], F32), ("dbg_kfb", [128, 1024], F32),
            ("dbg_fmqb", [128, 2], F32),
            ("dbg_w1q", [128, 1024], F32), ("dbg_b1q", [128, 32], F32),
            ("dbg_b1t", [128, 32], F32), ("dbg_bf", [128, 32], F32),
            ("dbg_bw", [128, 32], F32), ("dbg_br", [128, 32], F32),
            ("dbg_bt", [128, 32], F32), ("dbg_bg", [128, 32], F32),
            ("dbg_bis", [128, 32], F32), ("dbg_bb1", [128, 32], F32),
            ("dbg_bb2", [128, 32], F32), ("dbg_bcomb", [128, 32], F32),
            ("dbg_sch", [128, 8 * 128], F32),
            ("dbg_oph", [128, 8 * 128], F32),
        ]:
            d[nm] = nc.dram_tensor(nm, shape, dtp,
                                   kind="ExternalOutput").ap()

    with contextlib.ExitStack() as st:
        def mksbuf(stack):
            def sbuf(name, shape, dt=F32):
                return stack.enter_context(nc.sbuf_tensor(name, shape, dt)).ap()
            return sbuf

        sbuf = mksbuf(st)

        def psum(name, shape):
            return st.enter_context(nc.psum_tensor(name, shape, F32)).ap()

        cb = sbuf("cb", [128, 8])
        CIDX = {"tiny": 0, "eps": 1, "c6": 2, "c14": 3}
        CVALS = {"tiny": 1e-37, "eps": EPS, "c6": 6.0, "c14": 14.0}

        def cba(nm, row128=True):
            i = CIDX[nm]
            return cb[:, i:i + 1] if row128 else cb[0:1, i:i + 1]

        fmq = sbuf("fmq", [128, DK])
        fmk = sbuf("fmk", [128, DK])
        fmqb = sbuf("fmqb", [128, 2])
        fmkb = sbuf("fmkb", [128, 2])
        fmkb_row = sbuf("fmkb_row", [1, DK])
        rmsw = sbuf("rmsw", [128, 1])
        b1t = sbuf("b1t", [128, 32])
        b2t = sbuf("b2t", [128, 8])
        b1q = sbuf("b1q", [128, 32])
        b2q = sbuf("b2q", [128, 8])
        triu = sbuf("triu_s", [128, CH])
        ones = sbuf("ones_s", [128, 1])
        ones_r = sbuf("ones_r", [1, 128])
        fmkb_bc = sbuf("fmkb_bc", [128, DK])
        S_a = sbuf("S_a", [128, H * DK])
        S_b = sbuf("S_b", [128, H * DK])
        stat = sbuf("stat", [1, 1024])
        stat2 = sbuf("stat2", [1, 1024])
        stat_i = sbuf("stat_i", [1, 256], I32)

        attn_stack = contextlib.ExitStack()
        asb = mksbuf(attn_stack)
        wq = asb("wq", [128, 8 * D])
        wk = asb("wk", [128, 8 * D])
        wv = asb("wv", [128, 8 * D])
        wo = asb("wo", [128, 8 * D])
        xc = asb("xc", [128, 8 * CH])
        sqc = asb("sqc", [128, 8 * CH])
        xh = asb("xh", [128, 8 * CH])
        qkv = asb("qkv", [128, 8 * CH])
        kfm = asb("kfm", [128, 8 * CH])
        qf_a = asb("qf_a", [128, H * CH])
        qf_b = asb("qf_b", [128, H * CH])
        kf_a = asb("kf_a", [128, H * CH])
        kf_b = asb("kf_b", [128, H * CH])
        kt_a = asb("kt_a", [128, H * DK])
        kt_b = asb("kt_b", [128, H * DK])
        vt = asb("vt", [128, D])
        sc = asb("sc", [128, CH])
        yt = asb("yt", [128, CH])
        o_n = asb("o_n", [128, H * CH])

        ps_p = psum("ps_p", [128, 512])
        ps_q = psum("ps_q", [128, 512])
        ps_v = psum("ps_v", [128, 1024])
        ps_s = psum("ps_s", [128, 512])
        ps_o = psum("ps_o", [128, 512])
        ps_d = psum("ps_d", [128, 512])

        sems = {e: st.enter_context(nc.semaphore("sem_" + e)) for e in ENGINES}

        nc.const_aps.register(F32, -0.125, cb[:, 7:8])
        nc.const_aps.register(F32, -4 * LN2, cb[:, 6:7])

        sch = Sched()

        def stage(deps, items):
            return sch.stage(deps, items)

        items = [("vector", (lambda i, v: lambda e: e.memset(
            cb[:, i:i + 1], v))(CIDX[nm], CVALS[nm])) for nm in CIDX]
        items.append(("vector", lambda e: e.memset(cb[:, 7:8], -0.125)))
        items.append(("vector", lambda e: e.memset(cb[:, 6:7], -4 * LN2)))
        items.append(("vector", lambda e: e.memset(ones_r, 1.0)))
        items.append(("vector", lambda e: e.memset(S_a, 0.0)))
        items.append(("vector", lambda e: e.memset(S_b, 0.0)))
        loads = [
            (fmq, d["fmqT"]), (fmk, d["fmkT"]),
            (triu, d["triu"]), (ones, d["ones"]),
            (rmsw, d["rms_w"]),
            (fmqb[:, 0:1], d["fmq_b"]),
            (fmkb[:, 0:1], d["fmk_b"]),
            (fmkb_row, d["fmk_br"]),
            (b1t, d["b1"]),
            (b2t, d["b2"]),
            (wq.rearrange("p (c n) -> p c n", c=8),
             d["wqT"].rearrange("(c p) n -> p c n", p=128)),
            (wk.rearrange("p (c n) -> p c n", c=8),
             d["wkT"].rearrange("(c p) n -> p c n", p=128)),
            (wv.rearrange("p (c n) -> p c n", c=8),
             d["wvT"].rearrange("(c p) n -> p c n", p=128)),
            (wo.rearrange("p (c n) -> p c n", c=8),
             d["woT"].rearrange("(c p) n -> p c n", p=128)),
        ]
        for dst, src in loads:
            items.append(("sync", (lambda dst, src: lambda e: e.dma_start(
                out=dst, in_=src))(dst, src)))
        prev = stage(sch.snap(), items)

        prev = stage(prev, [("scalar", lambda e: e.activation(
            fmqb[:, 1:2], fmqb[:, 0:1], AF.Copy, scale=-1.0,
            bias=-4 * LN2))])
        prev = stage(prev, [("scalar", lambda e: e.activation(
            fmqb[:, 0:1], fmqb[:, 0:1], AF.Copy, scale=1.0,
            bias=-4 * LN2))])
        prev = stage(prev, [("scalar", lambda e: e.activation(
            fmkb[:, 1:2], fmkb[:, 0:1], AF.Copy, scale=-1.0))])
        sch.barrier()
        prev = stage(prev, [("tensor", lambda e: e.matmul(
            ps_v[:, 0:DK], ones_r, fmkb_row, start=True, stop=True))])
        prev = stage(prev, [("scalar", lambda e: e.activation(
            fmkb_bc, ps_v[:, 0:DK], AF.Copy))])

        def wslice(wt, c, dd):
            return wt[:, c * D + dd * 128: c * D + (dd + 1) * 128]

        def q_ops(z, out_q, M, clamp, signed, ties, width):
            ops = []
            onv = lambda f: ops.append(("vector", f))
            ons = lambda f: ops.append(("scalar", f))
            l, f, w = QSA[:, :width], QSB[:, :width], QSC[:, :width]
            tt, r = QSE[:, :width], QSF[:, :width]
            bs = QSG[:, :width]
            isx, sx = QSD[:, :width], QSH[:, :width]
            gE, g = QIA[:, :width], QIB[:, :width]
            bEi, p = QIC[:, :width], QIA[:, :width]
            tie, odd = QSG[:, :width], QSD[:, :width]
            if signed:
                ons(lambda e: e.activation(tt, z, AF.Square))
                ons(lambda e: e.activation(l, tt, AF.Ln, bias=cba("tiny")))
                c, inv = 1.0 / (2 * LN2), 2 * LN2
            else:
                ons(lambda e: e.activation(l, z, AF.Ln, bias=cba("tiny")))
                c, inv = 1.0 / LN2, LN2
            ons((lambda c: lambda e: e.activation(gE, l, AF.Copy, scale=c))(c))
            onv((lambda inv: lambda e: e.scalar_tensor_tensor(
                bEi, gE, inv, l, ALU.mult, ALU.is_gt))(inv))
            onv(lambda e: e.scalar_tensor_tensor(
                f, bEi, -1.0, gE, ALU.mult, ALU.add))
            ons((lambda cl: lambda e: e.activation(
                w, f, AF.Relu, bias=cba("c%d" % cl)))(clamp))
            if ties:
                onv((lambda MF: lambda e: e.tensor_scalar(
                    gE, w, -M23, MF * M23, ALU.mult, ALU.add))(
                        float(127 + M + clamp)))
                onv(lambda e: e.tensor_tensor(tt, z, gE.bitcast(F32), ALU.mult))
            else:
                ons(lambda e: e.activation(isx, w, AF.Exp, scale=-LN2))
                onv((lambda s_: lambda e: e.scalar_tensor_tensor(
                    tt, z, s_, isx, ALU.mult, ALU.mult))(
                        float(2.0 ** (M + clamp))))
            if signed:
                onv(lambda e: e.tensor_copy(g, tt))
                onv(lambda e: e.scalar_tensor_tensor(
                    r, g, 0.5, tt, ALU.add, ALU.is_le))
                onv(lambda e: e.scalar_tensor_tensor(
                    bs, g, -0.5, tt, ALU.add, ALU.is_gt))
                onv(lambda e: e.scalar_tensor_tensor(
                    r, bs, -1.0, r, ALU.mult, ALU.add))
                onv(lambda e: e.tensor_tensor(r, r, g, ALU.add))
            else:
                ons(lambda e: e.activation(g, tt, AF.Copy, bias=-0.125))
                onv(lambda e: e.scalar_tensor_tensor(
                    r, g, 0.5, tt, ALU.add, ALU.is_le))
                onv(lambda e: e.tensor_tensor(r, r, g, ALU.add))
            if ties:
                onv(lambda e: e.scalar_tensor_tensor(
                    tie, r, -0.5, tt, ALU.add, ALU.is_equal))
                onv(lambda e: e.tensor_scalar(odd, r, 0.5, None, ALU.mult))
                onv(lambda e: e.tensor_copy(p, odd))
                onv(lambda e: e.tensor_tensor(odd, odd, p, ALU.not_equal))
                onv(lambda e: e.tensor_tensor(tie, tie, odd, ALU.mult))
                onv(lambda e: e.tensor_tensor(r, r, tie, ALU.subtract))
                onv((lambda MF: lambda e: e.tensor_scalar(
                    gE, w, M23, MF * M23, ALU.mult, ALU.add))(
                        float(127 - M - clamp)))
                onv(lambda e: e.tensor_tensor(
                    out_q, r, gE.bitcast(F32), ALU.mult))
            else:
                ons(lambda e: e.activation(sx, w, AF.Exp, scale=LN2))
                onv((lambda s_: lambda e: e.scalar_tensor_tensor(
                    out_q, r, s_, sx, ALU.mult, ALU.mult))(
                        float(2.0 ** (-(M + clamp)))))
            return ops

        def q_h(z, out_q, width):
            ops = []
            onv = lambda f: ops.append(("vector", f))
            ons = lambda f: ops.append(("scalar", f))
            l, f = QSA[:, :width], QSB[:, :width]
            w14, w6 = QSC[:, :width], QSD[:, :width]
            tt, r510 = QSF[:, :width], QSE[:, :width]
            r43 = QSG[:, :width]
            isx, sx = QSH[:, :width], QSH[:, :width]
            gE, g = QIA[:, :width], QIB[:, :width]
            bEi, p = QIC[:, :width], QIA[:, :width]
            tie, odd = QSA[:, :width], QSB[:, :width]
            ons(lambda e: e.activation(l, z, AF.Ln, bias=cba("tiny")))
            ons(lambda e: e.activation(gE, l, AF.Copy, scale=1.0 / LN2))
            onv(lambda e: e.scalar_tensor_tensor(
                bEi, gE, LN2, l, ALU.mult, ALU.is_gt))
            onv(lambda e: e.scalar_tensor_tensor(
                f, bEi, -1.0, gE, ALU.mult, ALU.add))
            ons(lambda e: e.activation(w14, f, AF.Relu, bias=cba("c14")))
            ons(lambda e: e.activation(w6, f, AF.Relu, bias=cba("c6")))
            ons(lambda e: e.activation(isx, w14, AF.Exp, scale=-LN2))
            onv(lambda e: e.scalar_tensor_tensor(
                tt, z, float(2.0 ** 24), isx, ALU.mult, ALU.mult))
            ons(lambda e: e.activation(g, tt, AF.Copy, bias=-0.125))
            onv(lambda e: e.scalar_tensor_tensor(
                r510, g, 0.5, tt, ALU.add, ALU.is_le))
            onv(lambda e: e.tensor_tensor(r510, r510, g, ALU.add))
            onv(lambda e: e.tensor_tensor(w14, w14, w6, ALU.subtract))
            onv(lambda e: e.tensor_scalar(
                gE, w14, M23, (127.0 - 15.0) * M23, ALU.mult, ALU.add))
            onv(lambda e: e.tensor_tensor(tt, r510, gE.bitcast(F32), ALU.mult))
            ons(lambda e: e.activation(g, tt, AF.Copy, bias=-0.125))
            onv(lambda e: e.scalar_tensor_tensor(
                r43, g, 0.5, tt, ALU.add, ALU.is_le))
            onv(lambda e: e.tensor_tensor(r43, r43, g, ALU.add))
            onv(lambda e: e.scalar_tensor_tensor(
                tie, r43, -0.5, tt, ALU.add, ALU.is_equal))
            onv(lambda e: e.tensor_scalar(odd, r43, 0.5, None, ALU.mult))
            onv(lambda e: e.tensor_copy(p, odd))
            onv(lambda e: e.tensor_tensor(odd, odd, p, ALU.not_equal))
            onv(lambda e: e.tensor_tensor(tie, tie, odd, ALU.mult))
            onv(lambda e: e.tensor_tensor(r43, r43, tie, ALU.subtract))
            ons(lambda e: e.activation(sx, w6, AF.Exp, scale=LN2))
            onv(lambda e: e.scalar_tensor_tensor(
                out_q, r43, float(2.0 ** -9), sx, ALU.mult, ALU.mult))
            return ops

        def run_serial(prev, ops):
            for eng, f in ops:
                prev = stage(prev, [(eng, f)])
            return prev

        def emit_ln(prev, src_ap, t0, width, x_tile, sq_tile, out_tile):
            x3 = x_tile.rearrange("p (c t) -> p c t", c=8)
            sq3 = sq_tile.rearrange("p (c t) -> p c t", c=8)
            o3 = out_tile.rearrange("p (c t) -> p c t", c=8)
            prev = stage(prev, [("sync", (lambda: lambda e: e.dma_start(
                out=x3, in_=src_ap.rearrange("(c p) t -> p c t", p=128)
                [:, :, t0:t0 + width]))())])
            prev = stage(prev, [("scalar", lambda e: e.activation(
                sq_tile, x_tile, AF.Square))])

            def mk_stats(e):
                for c in range(8):
                    e.matmul(ps_s[0:1, 0:width], ones, x3[:, c, :],
                             start=(c == 0), stop=(c == 7))
                for c in range(8):
                    ins = e.matmul(ps_d[0:1, 0:width], ones, sq3[:, c, :],
                                   start=(c == 0), stop=(c == 7))
                return ins
            prev = stage(prev, [("tensor", mk_stats)])

            def mk_sv1(e):
                e.tensor_scalar(stat[0:1, 0:width], ps_s[0:1, 0:width],
                                1.0 / D, None, ALU.mult)
                return e.tensor_scalar(stat2[0:1, 0:width],
                                       ps_d[0:1, 0:width],
                                       1.0 / D, None, ALU.mult)
            prev = stage(prev, [("vector", mk_sv1)])

            def mk_sv2(e):
                return e.tensor_tensor(stat[0:1, width:2 * width],
                                       stat[0:1, 0:width],
                                       stat[0:1, 0:width], ALU.mult)
            prev = stage(prev, [("vector", mk_sv2)])

            def mk_sv3(e):
                return e.tensor_tensor(stat2[0:1, 0:width],
                                       stat2[0:1, 0:width],
                                       stat[0:1, width:2 * width],
                                       ALU.subtract)
            prev = stage(prev, [("vector", mk_sv3)])

            def mk_ss(e):
                e.activation(stat2[0:1, width:2 * width],
                             stat2[0:1, 0:width], AF.Ln,
                             bias=cba("eps", False))
                return e.activation(stat2[0:1, 0:width],
                                    stat2[0:1, width:2 * width],
                                    AF.Exp, scale=-0.5)
            prev = stage(prev, [("scalar", mk_ss)])

            def mk_bc(e):
                e.matmul(ps_v[:, 0:width], ones_r, stat[0:1, 0:width],
                         start=True, stop=True)
                return e.matmul(ps_v[:, 512:512 + width], ones_r,
                                stat2[0:1, 0:width], start=True, stop=True)
            prev = stage(prev, [("tensor", mk_bc)])
            m_b = ps_v[:, 0:width].unsqueeze(1).broadcast_to([128, 8, width])
            r_b = ps_v[:, 512:512 + width].unsqueeze(1)\
                .broadcast_to([128, 8, width])

            def mk_ap(e):
                e.tensor_tensor(sq3, x3, m_b, ALU.subtract)
                return e.tensor_tensor(o3, sq3, r_b, ALU.mult)
            prev = stage(prev, [("vector", mk_ap)])
            return prev

        def emit_proj(prev, wt, out_sb, ps):
            for dd in range(8):
                def mk_mm(e, dd=dd):
                    for c in range(8):
                        ins = e.matmul(ps[:, 0:CH], wslice(wt, c, dd),
                                       xh[:, c * CH:(c + 1) * CH],
                                       start=(c == 0), stop=(c == 7))
                    return ins
                prev = stage(prev, [("tensor", mk_mm)])

                def mk_ev(e, dd=dd):
                    e.activation(out_sb[:, dd * CH:(dd + 1) * CH],
                                 ps[:, 0:CH], AF.Copy)
                prev = stage(prev, [("scalar", mk_ev)])
            return prev

        def emit_v_tm(prev):
            def mk_mm(e):
                wv3 = wv.rearrange("p (c n) -> p c n", c=8)
                for half in range(2):
                    for c in range(8):
                        ins = e.matmul(ps_v[:, half * 512:(half + 1) * 512],
                                       xh[:, c * CH:(c + 1) * CH],
                                       wv3[:, c, half * 512:(half + 1) * 512],
                                       start=(c == 0), stop=(c == 7))
                return ins
            prev = stage(prev, [("tensor", mk_mm)])

            def mk_ev(e):
                e.activation(vt[:, 0:512], ps_v[:, 0:512], AF.Copy)
                return e.activation(vt[:, 512:1024], ps_v[:, 512:1024],
                                    AF.Copy)
            return stage(prev, [("scalar", mk_ev)])

        def emit_fm(prev, src_fm, fmw, fmb, out_a, out_b):
            for h in range(H):
                def mk_mm(e, h=h):
                    e.matmul(ps_s[:, 0:CH], fmw,
                             src_fm[:, h * CH:(h + 1) * CH],
                             start=True, stop=True)
                prev = stage(prev, [("tensor", mk_mm)])

                def mk_ev(e, h=h):
                    e.activation(out_a[:, h * CH:(h + 1) * CH], ps_s[:, 0:CH],
                                 AF.Exp, bias=fmb[:, 0:1])
                    return e.activation(out_b[:, h * CH:(h + 1) * CH],
                                        ps_s[:, 0:CH], AF.Exp, scale=-1.0,
                                        bias=fmb[:, 1:2])
                prev = stage(prev, [("scalar", mk_ev)])
            return prev

        def emit_fm_tm(prev, src_fm, fmw, fmb_row, out_a, out_b):
            b_b = fmkb_bc
            for h in range(H):
                def mk_mm(e, h=h):
                    e.matmul(ps_s[:, 128:256],
                             src_fm[:, h * CH:(h + 1) * CH], fmw,
                             start=True, stop=True)
                prev = stage(prev, [("tensor", mk_mm)])

                def mk_add(e, h=h):
                    e.tensor_tensor(yt, ps_s[:, 128:256], b_b, ALU.add)
                prev = stage(prev, [("vector", mk_add)])

                def mk_ev(e, h=h):
                    e.activation(out_a[:, h * DK:(h + 1) * DK], yt, AF.Exp)
                    return e.activation(out_b[:, h * DK:(h + 1) * DK], yt,
                                        AF.Exp, scale=-1.0)
                prev = stage(prev, [("scalar", mk_ev)])
            return prev

        def emit_dS(prev):
            for h in range(H):
                hs = slice(h * CH, (h + 1) * CH)

                def mk_ds(e, hs=hs):
                    e.matmul(ps_d[:, 0:CH], kt_a[:, hs], vt[:, hs],
                             start=True, stop=True)
                    return e.matmul(ps_d[:, CH:2 * CH], kt_b[:, hs],
                                    vt[:, hs], start=True, stop=True)
                prev = stage(prev, [("tensor", mk_ds)])

                def mk_add(e, hs=hs):
                    e.tensor_tensor(S_a[:, hs], S_a[:, hs], ps_d[:, 0:CH],
                                    ALU.add)
                    return e.tensor_tensor(S_b[:, hs], S_b[:, hs],
                                           ps_d[:, CH:2 * CH], ALU.add)
                prev = stage(prev, [("vector", mk_add)])
            return prev

        def dump(prev, name, src_tile, width):
            if not debug:
                return prev
            def mk(e, name=name, src_tile=src_tile, width=width):
                return e.dma_start(out=d[name][:, 0:width],
                                   in_=src_tile[:, 0:width])
            return stage(prev, [("gpsimd", mk)])

        for j in range(NCH):
            t0 = j * CH
            prev = emit_ln(prev, d["xpre_t"], t0, CH, xc, sqc, xh)
            prev = emit_proj(prev, wk, kfm, ps_p)
            prev = emit_fm_tm(prev, kfm, fmk, fmkb_row, kt_a, kt_b)
            prev = emit_v_tm(prev)
            prev = emit_dS(prev)

        prev = dump(prev, "dbg_Sa", S_a, H * DK)
        prev = dump(prev, "dbg_Sb", S_b, H * DK)

        for j in range(NCH):
            t0 = j * CH
            prev = emit_ln(prev, d["x_t"], t0, CH, xc, sqc, xh)
            if debug and j == 0:
                prev = dump(prev, "dbg_xh", xh, 1024)
            prev = emit_proj(prev, wq, qkv, ps_p)
            if debug and j == 0:
                prev = dump(prev, "dbg_q", qkv, 1024)
            prev = emit_fm(prev, qkv, fmq, fmqb, qf_a, qf_b)
            if debug and j == 0:
                prev = dump(prev, "dbg_qfa", qf_a, 1024)
            prev = emit_proj(prev, wk, kfm, ps_q)
            prev = emit_fm(prev, kfm, fmk, fmkb, kf_a, kf_b)
            prev = emit_fm_tm(prev, kfm, fmk, fmkb_row, kt_a, kt_b)
            prev = emit_v_tm(prev)
            if debug and j == 0:
                prev = dump(prev, "dbg_fmqb", fmqb, 2)
                prev = dump(prev, "dbg_qfb", qf_b, 1024)
                prev = dump(prev, "dbg_kfb", kf_b, 1024)
                prev = dump(prev, "dbg_kfa", kf_a, 1024)
                prev = dump(prev, "dbg_kta", kt_a, 1024)
                prev = dump(prev, "dbg_vt", vt, 1024)
            for h in range(H):
                hs = slice(h * CH, (h + 1) * CH)

                def mk_sc(e, hs=hs):
                    e.matmul(ps_s[:, 0:CH], kf_a[:, hs], qf_a[:, hs],
                             start=True, stop=False)
                    return e.matmul(ps_s[:, 0:CH], kf_b[:, hs], qf_b[:, hs],
                                    start=False, stop=True)
                prev = stage(prev, [("tensor", mk_sc)])

                def mk_mask(e, hs=hs):
                    e.tensor_tensor(sc, ps_s[:, 0:CH], triu, ALU.mult)
                prev = stage(prev, [("vector", mk_mask)])

                if debug and j == 0:
                    def mkdsc(e, h=h):
                        return e.dma_start(out=d["dbg_sch"][:, h * 128:
                                                            (h + 1) * 128],
                                           in_=sc)
                    prev = stage(prev, [("gpsimd", mkdsc)])

                def mk_o(e, hs=hs):
                    e.matmul(ps_o[:, 0:CH], vt[:, hs], sc,
                             start=True, stop=False)
                    e.matmul(ps_o[:, 0:CH], S_a[:, hs], qf_a[:, hs],
                             start=False, stop=False)
                    return e.matmul(ps_o[:, 0:CH], S_b[:, hs], qf_b[:, hs],
                                    start=False, stop=True)
                prev = stage(prev, [("tensor", mk_o)])

                def mk_oev(e, hs=hs):
                    e.activation(o_n[:, hs], ps_o[:, 0:CH], AF.Copy)
                    return e.activation(yt, ps_o[:, 0:CH], AF.Square)
                prev = stage(prev, [("scalar", mk_oev)])

                if debug and j == 0:
                    def mkdop(e, h=h):
                        return e.dma_start(out=d["dbg_oph"][:, h * 128:
                                                            (h + 1) * 128],
                                           in_=o_n[:, h * CH:(h + 1) * CH])
                    prev = stage(prev, [("gpsimd", mkdop)])
                if debug and j == 0 and h == 0:
                    prev = dump(prev, "dbg_opre", o_n[:, 0:CH], 128)
                    prev = dump(prev, "dbg_osq", yt, 128)

                def mk_osum(e, hs=hs):
                    e.matmul(ps_s[0:1, 2 * CH:3 * CH], ones, yt,
                             start=True, stop=True)
                prev = stage(prev, [("tensor", mk_osum)])

                def mk_om1(e, hs=hs):
                    return e.tensor_scalar(stat[0:1, 3 * CH:4 * CH],
                                           ps_s[0:1, 2 * CH:3 * CH],
                                           1.0 / DK, EPS, ALU.mult, ALU.add)
                prev = stage(prev, [("vector", mk_om1)])

                def mk_om2(e, hs=hs):
                    return e.tensor_scalar(
                        stat_i[0:1, 0:CH],
                        stat[0:1, 3 * CH:4 * CH].bitcast(I32),
                        float(2.0 ** -23), -127.0, ALU.mult, ALU.add)
                prev = stage(prev, [("vector", mk_om2)])

                def mk_om3(e, hs=hs):
                    return e.tensor_scalar(stat_i[0:1, CH:2 * CH],
                                           stat_i[0:1, 0:CH], -M23,
                                           127.0 * M23, ALU.mult, ALU.add)
                prev = stage(prev, [("vector", mk_om3)])

                def mk_om4(e, hs=hs):
                    return e.tensor_tensor(
                        stat[0:1, 4 * CH:5 * CH], stat[0:1, 3 * CH:4 * CH],
                        stat_i[0:1, CH:2 * CH].bitcast(F32), ALU.mult)
                prev = stage(prev, [("vector", mk_om4)])

                def mk_or(e, hs=hs):
                    return e.activation(stat2[0:1, 0:CH],
                                        stat[0:1, 4 * CH:5 * CH], AF.Ln)
                prev = stage(prev, [("scalar", mk_or)])

                def mk_or2(e, hs=hs):
                    return e.scalar_tensor_tensor(
                        stat[0:1, 5 * CH:6 * CH], stat_i[0:1, 0:CH], LN2,
                        stat2[0:1, 0:CH], ALU.mult, ALU.add)
                prev = stage(prev, [("vector", mk_or2)])

                def mk_or3(e, hs=hs):
                    return e.activation(stat[0:1, 5 * CH:6 * CH],
                                        stat[0:1, 5 * CH:6 * CH],
                                        AF.Exp, scale=-0.5)
                prev = stage(prev, [("scalar", mk_or3)])

                if debug and j == 0 and h == 0:
                    def mkds(e):
                        return e.dma_start(out=d["dbg_stat"], in_=stat)
                    prev = stage(prev, [("gpsimd", mkds)])

                def mk_obc(e, hs=hs):
                    return e.matmul(ps_v[:, 0:CH], ones_r,
                                    stat[0:1, 5 * CH:6 * CH],
                                    start=True, stop=True)
                prev = stage(prev, [("tensor", mk_obc)])

                def mk_oap(e, hs=hs):
                    return e.scalar_tensor_tensor(
                        o_n[:, hs], o_n[:, hs], rmsw[:, 0:1],
                        ps_v[:, 0:CH], ALU.mult, ALU.mult)
                prev = stage(prev, [("vector", mk_oap)])

                def mk_ds(e, hs=hs):
                    e.matmul(ps_d[:, 0:CH], kt_a[:, hs], vt[:, hs],
                             start=True, stop=True)
                    return e.matmul(ps_d[:, CH:2 * CH], kt_b[:, hs],
                                    vt[:, hs], start=True, stop=True)
                prev = stage(prev, [("tensor", mk_ds)])

                def mk_sadd(e, hs=hs):
                    e.tensor_tensor(S_a[:, hs], S_a[:, hs], ps_d[:, 0:CH],
                                    ALU.add)
                    return e.tensor_tensor(S_b[:, hs], S_b[:, hs],
                                           ps_d[:, CH:2 * CH], ALU.add)
                prev = stage(prev, [("vector", mk_sadd)])
            if debug and j == 0:
                prev = dump(prev, "dbg_on", o_n, 1024)
                prev = dump(prev, "dbg_sc", sc, 128)
            for dd in range(8):
                def mk_wo(e, dd=dd):
                    for h in range(H):
                        ins = e.matmul(ps_o[:, 0:CH], wslice(wo, h, dd),
                                       o_n[:, h * CH:(h + 1) * CH],
                                       start=(h == 0), stop=(h == 7))
                    return ins
                prev = stage(prev, [("tensor", mk_wo)])

                def mk_res(e, dd=dd):
                    e.tensor_tensor(xc[:, dd * CH:(dd + 1) * CH],
                                    xc[:, dd * CH:(dd + 1) * CH],
                                    ps_o[:, 0:CH], ALU.add)
                prev = stage(prev, [("vector", mk_res)])

            def mk_st(e, t0=t0):
                e.dma_start(
                    out=d["xb"].rearrange("(c p) t -> p c t", p=128)
                    [:, :, t0:t0 + CH],
                    in_=xc.rearrange("p (c t) -> p c t", c=8))
            prev = stage(prev, [("gpsimd", mk_st)])

        sch.barrier()
        attn_stack.close()
        ffn_stack = contextlib.ExitStack()
        fsb = mksbuf(ffn_stack)
        QW = 1024
        QSA = fsb("QSA", [128, QW]); QSB = fsb("QSB", [128, QW])
        QSC = fsb("QSC", [128, QW]); QSD = fsb("QSD", [128, QW])
        QSE = fsb("QSE", [128, QW]); QSF = fsb("QSF", [128, QW])
        QSG = fsb("QSG", [128, QW]); QSH = fsb("QSH", [128, QW])
        QIA = fsb("QIA", [128, QW], I32)
        QIB = fsb("QIB", [128, QW], I32)
        QIC = fsb("QIC", [128, QW], I32)
        fx = fsb("fx", [128, 8 * FT])
        fxh = fsb("fxh", [128, 8 * FT])
        fz = fsb("fz", [128, 1024])
        fy = fsb("fy", [128, FT])
        x2q = fsb("x2q", [128, 8 * FT], F8)
        hq = fsb("hq", [128, 32 * FT], F8)
        w1q = fsb("w1q", [128, 8 * DH], F8)
        w2q = fsb("w2q", [128, 32 * D], F8)
        msk8 = fsb("msk8", [128, 8 * FT], F8)
        wload = fsb("wload", [128, 1024])

        def q_weight(prev, srcT, nchunk, cols, dst_fp8):
            src3 = srcT.rearrange("(c p) n -> p c n", p=128)
            step = 1024
            for c in range(nchunk):
                for s0 in range(0, cols, step):
                    wdt = min(step, cols - s0)

                    def mk_ld(e, c=c, s0=s0, wdt=wdt):
                        e.dma_start(out=wload[:, 0:wdt],
                                    in_=src3[:, c, s0:s0 + wdt])
                    prev = stage(prev, [("sync", mk_ld)])
                    prev = run_serial(prev, q_ops(
                        wload[:, 0:wdt],
                        dst_fp8[:, c * cols + s0:c * cols + s0 + wdt],
                        M=3, clamp=6, signed=True, ties=True, width=wdt))
            return prev

        prev = q_weight(prev, d["w1T"], 8, DH, w1q)
        prev = q_weight(prev, d["w2T"], 32, D, w2q)
        prev = dump(prev, "dbg_b1t", b1t, 32)
        if debug:
            W_ = 32
            z_, out_ = b1t, b1q
            l_, f_, w_ = QSA[:, :W_], QSB[:, :W_], QSC[:, :W_]
            tt_, r_ = QSE[:, :W_], QSF[:, :W_]
            bs_ = QSG[:, :W_]
            gE_, g_ = QIA[:, :W_], QIB[:, :W_]
            bEi_, p_ = QIC[:, :W_], QIA[:, :W_]
            tie_, odd_ = QSG[:, :W_], QSD[:, :W_]
            prev = stage(prev, [("scalar", lambda e: e.activation(tt_, z_, AF.Square))])
            prev = stage(prev, [("scalar", lambda e: e.activation(l_, tt_, AF.Ln, bias=cba("tiny")))])
            prev = stage(prev, [("scalar", lambda e: e.activation(gE_, l_, AF.Copy, scale=1.0 / (2 * LN2)))])
            prev = stage(prev, [("vector", lambda e: e.scalar_tensor_tensor(bEi_, gE_, 2 * LN2, l_, ALU.mult, ALU.is_gt))])
            prev = stage(prev, [("vector", lambda e: e.scalar_tensor_tensor(f_, bEi_, -1.0, gE_, ALU.mult, ALU.add))])
            prev = stage(prev, [("scalar", lambda e: e.activation(w_, f_, AF.Relu, bias=cba("c6")))])
            prev = stage(prev, [("vector", lambda e: e.tensor_scalar(gE_, w_, -M23, 136.0 * M23, ALU.mult, ALU.add))])
            def mkbis(e):
                return e.tensor_copy(QSD[:, 0:W_], gE_)
            prev = stage(prev, [("vector", mkbis)])
            prev = dump(prev, "dbg_bis", QSD, 32)
            prev = stage(prev, [("vector", lambda e: e.tensor_tensor(tt_, z_, gE_.bitcast(F32), ALU.mult))])
            prev = stage(prev, [("vector", lambda e: e.tensor_copy(g_, tt_))])
            prev = stage(prev, [("vector", lambda e: e.scalar_tensor_tensor(r_, g_, 0.5, tt_, ALU.add, ALU.is_le))])
            prev = dump(prev, "dbg_bb1", QSF, 32)
            prev = stage(prev, [("vector", lambda e: e.scalar_tensor_tensor(bs_, g_, -0.5, tt_, ALU.add, ALU.is_gt))])
            prev = dump(prev, "dbg_bb2", QSG, 32)
            prev = stage(prev, [("vector", lambda e: e.scalar_tensor_tensor(r_, bs_, -1.0, r_, ALU.mult, ALU.add))])
            prev = dump(prev, "dbg_bcomb", QSF, 32)
            prev = stage(prev, [("vector", lambda e: e.tensor_tensor(r_, r_, g_, ALU.add))])
            prev = stage(prev, [("vector", lambda e: e.scalar_tensor_tensor(tie_, r_, -0.5, tt_, ALU.add, ALU.is_equal))])
            prev = stage(prev, [("vector", lambda e: e.tensor_scalar(odd_, r_, 0.5, None, ALU.mult))])
            prev = stage(prev, [("vector", lambda e: e.tensor_copy(p_, odd_))])
            prev = stage(prev, [("vector", lambda e: e.tensor_tensor(odd_, odd_, p_, ALU.not_equal))])
            prev = stage(prev, [("vector", lambda e: e.tensor_tensor(tie_, tie_, odd_, ALU.mult))])
            prev = stage(prev, [("vector", lambda e: e.tensor_tensor(r_, r_, tie_, ALU.subtract))])
            prev = stage(prev, [("vector", lambda e: e.tensor_scalar(gE_, w_, M23, 118.0 * M23, ALU.mult, ALU.add))])
            prev = stage(prev, [("vector", lambda e: e.tensor_tensor(out_, r_, gE_.bitcast(F32), ALU.mult))])
        else:
            prev = run_serial(prev, q_ops(b1t, b1q, M=3, clamp=6, signed=True,
                                          ties=True, width=32))
        if debug:
            prev = dump(prev, "dbg_bf", QSB, 32)
            prev = dump(prev, "dbg_bw", QSC, 32)
            prev = dump(prev, "dbg_br", QSF, 32)
            prev = dump(prev, "dbg_bt", QSE, 32)
            def mkg(e):
                return e.tensor_copy(QSD[:, 0:32], QIB[:, 0:32])
            prev = stage(prev, [("vector", mkg)])
            prev = dump(prev, "dbg_bg", QSD, 32)
        prev = run_serial(prev, q_ops(b2t, b2q, M=3, clamp=6, signed=True,
                                      ties=True, width=8))

        if debug:
            def mkdw(e):
                return e.tensor_copy(QSA[:, 0:1024], w1q[:, 0:1024])
            prev = stage(prev, [("vector", mkdw)])
            prev = dump(prev, "dbg_w1q", QSA, 1024)
            def mkdb(e):
                return e.tensor_copy(QSB[:, 0:32], b1q)
            prev = stage(prev, [("vector", mkdb)])
            prev = dump(prev, "dbg_b1q", QSB, 32)

        x2q3 = x2q.rearrange("p (c t) -> p c t", c=8)
        hq3 = hq.rearrange("p (c t) -> p c t", c=32)
        fx3 = fx.rearrange("p (c t) -> p c t", c=8)
        w1q3 = w1q.rearrange("p (c n) -> p c n", c=8)
        w2q3 = w2q.rearrange("p (c n) -> p c n", c=32)
        for ti in range(NFT):
            t0 = ti * FT
            prev = emit_ln(prev, d["xb"], t0, FT, fx, fxh, fxh)

            def mk_mld(e, t0=t0):
                e.dma_start(out=msk8.rearrange("p (c t) -> p c t", c=8),
                            in_=d["maskT"].rearrange("(c p) t -> p c t", p=128)
                            [:, :, t0:t0 + FT])
            prev = stage(prev, [("sync", mk_mld)])
            XW = min(1024, 8 * FT)
            for quar in range(8 * FT // XW):
                prev = run_serial(prev, q_ops(
                    fxh[:, quar * XW:(quar + 1) * XW],
                    x2q[:, quar * XW:(quar + 1) * XW],
                    M=3, clamp=6, signed=True, ties=False, width=XW))
            if debug and ti == 0:
                prev = dump(prev, "dbg_fxh", fxh, 8 * FT)
                def mkd(e):
                    return e.tensor_copy(QSA[:, 0:1024], x2q[:, 0:1024])
                prev = stage(prev, [("vector", mkd)])
                prev = dump(prev, "dbg_x2q", QSA, 1024)
            for grp in range(16):
                for sub in range(2):
                    dh = grp * 2 + sub

                    def mk_m(e, dh=dh):
                        for c in range(8):
                            ins = e.matmul(ps_p[:, 0:FT],
                                           w1q3[:, c, dh * 128:(dh + 1) * 128],
                                           x2q3[:, c, :],
                                           start=(c == 0), stop=(c == 7))
                        return ins
                    prev = stage(prev, [("tensor", mk_m)])

                    def mk_e(e, dh=dh, sub=sub):
                        e.activation(fz[:, sub * FT:(sub + 1) * FT],
                                     ps_p[:, 0:FT], AF.Relu,
                                     bias=b1q[:, dh:dh + 1])
                    prev = stage(prev, [("scalar", mk_e)])
                if debug and ti == 0 and grp == 0:
                    prev = dump(prev, "dbg_fz", fz, 2 * FT)
                prev = run_serial(prev, q_h(
                    fz[:, 0:2 * FT], hq[:, grp * 2 * FT:(grp + 1) * 2 * FT],
                    width=2 * FT))
                if debug and ti == 0 and grp == 0:
                    def mkd2(e):
                        return e.tensor_copy(QSA[:, 0:2 * FT],
                                             hq[:, 0:2 * FT])
                    prev = stage(prev, [("vector", mkd2)])
                    prev = dump(prev, "dbg_hq", QSA, 2 * FT)
            for dd in range(8):
                def mk_m2(e, dd=dd):
                    for c in range(32):
                        ins = e.matmul(ps_q[:, 0:FT],
                                       w2q3[:, c, dd * 128:(dd + 1) * 128],
                                       hq3[:, c, :],
                                       start=(c == 0), stop=(c == 31))
                    return ins
                prev = stage(prev, [("tensor", mk_m2)])

                def mk_e2(e, dd=dd):
                    return e.tensor_scalar(fy, ps_q[:, 0:FT],
                                           b2q[:, dd:dd + 1], None, ALU.add)
                prev = stage(prev, [("vector", mk_e2)])
                prev = run_serial(prev, q_ops(
                    fy, fz[:, 0:FT], M=10, clamp=14, signed=True, ties=False,
                    width=FT))

                def mk_dr1(e, dd=dd):
                    return e.tensor_tensor(
                        fy, fz[:, 0:FT],
                        msk8.rearrange("p (c t) -> p c t", c=8)[:, dd, :],
                        ALU.mult)
                prev = stage(prev, [("vector", mk_dr1)])

                def mk_dr2(e, dd=dd):
                    return e.tensor_tensor(fy, fy, fx3[:, dd, :], ALU.add)
                prev = stage(prev, [("vector", mk_dr2)])
                if debug and ti == 0 and dd == 0:
                    prev = dump(prev, "dbg_fy", fy, FT)

                def mk_out(e, dd=dd, t0=t0):
                    e.dma_start(
                        out=d["out_t"].rearrange("(c p) t -> p c t", p=128)
                        [:, dd, t0:t0 + FT], in_=fy)
                prev = stage(prev, [("gpsimd", mk_out)])

        ffn_stack.close()

        sch.schedule()
        final = sch.inc_total

        class _Replay:
            def __init__(self, eng, oplist, sems):
                self._e = eng
                self._ops = oplist
                self._i = 0
                self._sems = sems

            def __getattr__(self, nm):
                f = getattr(self._e, nm)
                if not callable(f):
                    return f

                def g(*a, **k):
                    op = self._ops[self._i]
                    self._i += 1
                    for pe, v in op.waits:
                        self._e.wait_ge(self._sems[pe], v)
                    r = f(*a, **k)
                    if op.inc_id is not None and hasattr(r, "then_inc"):
                        r.then_inc(self._sems[op.engine],
                                   16 if op.is_dma else 1)
                    return r
                return g

        def engine_runner(ename):
            def run(eng):
                for fni, (e2, fn, bar) in enumerate(sch.fns):
                    if e2 != ename:
                        continue
                    fn(_Replay(eng, sch.op_by_fn[fni], sems))
                if ename == "gpsimd":
                    for e2 in ENGINES:
                        if final[e2] > 0:
                            eng.wait_ge(sems[e2], final[e2])
            return run

        with nc.Block() as block:
            block.sync(engine_runner("sync"))
            block.scalar(engine_runner("scalar"))
            block.vector(engine_runner("vector"))
            block.tensor(engine_runner("tensor"))
            block.gpsimd(engine_runner("gpsimd"))

    return nc


def _dropout_mask_np():
    import jax
    cpu = jax.devices("cpu")[0]
    with jax.default_device(cpu):
        key = jax.random.key(42, impl="threefry2x32")
        keep = jax.random.bernoulli(key, 0.8, (4, 8192, 1024))
        keep = np.asarray(keep)
    return keep


def kernel(L0_attn_x, L0_attn_Wq, L0_attn_Wk, L0_attn_Wv, L0_attn_Wo,
           L0_attn_fmq_w, L0_attn_fmq_b, L0_attn_fmk_w, L0_attn_fmk_b,
           L0_attn_rms_w, L0_ffn_w1, L0_ffn_b1, L0_ffn_w2, L0_ffn_b2):
    x = np.ascontiguousarray(np.asarray(L0_attn_x, dtype=np.float32))
    B, N, _ = x.shape
    keep = _dropout_mask_np()
    maskmul = np.where(keep, np.float32(1.25), np.float32(0.0))
    maskmul8 = maskmul.astype(ml_dtypes.float8_e4m3)

    wqT = np.ascontiguousarray(np.asarray(L0_attn_Wq).T)
    wkT = np.ascontiguousarray(np.asarray(L0_attn_Wk).T)
    wvT = np.ascontiguousarray(np.asarray(L0_attn_Wv).T)
    woT = np.ascontiguousarray(np.asarray(L0_attn_Wo).T)
    fmqT = np.ascontiguousarray(np.asarray(L0_attn_fmq_w).T)
    fmkT = np.ascontiguousarray(np.asarray(L0_attn_fmk_w).T)
    w1T = np.ascontiguousarray(np.asarray(L0_ffn_w1).T)
    w2T = np.ascontiguousarray(np.asarray(L0_ffn_w2).T)

    triu = np.triu(np.ones((CH, CH), dtype=np.float32))
    fmqb_c = np.ascontiguousarray(
        np.asarray(L0_attn_fmq_b, dtype=np.float32).reshape(DK, 1))
    fmkb_c = np.ascontiguousarray(
        np.asarray(L0_attn_fmk_b, dtype=np.float32).reshape(DK, 1))
    fmkbr_c = np.ascontiguousarray(
        np.asarray(L0_attn_fmk_b, dtype=np.float32).reshape(1, DK))
    rmsw_c = np.ascontiguousarray(
        np.asarray(L0_attn_rms_w, dtype=np.float32).reshape(DK, 1))
    b1_c = np.ascontiguousarray(
        np.asarray(L0_ffn_b1, dtype=np.float32).reshape(32, 128).T)
    b2_c = np.ascontiguousarray(
        np.asarray(L0_ffn_b2, dtype=np.float32).reshape(8, 128).T)
    onesc = np.ones((128, 1), dtype=np.float32)
    zeros_pre = np.zeros((D, T), dtype=np.float32)

    nc = build_program()
    in_maps = []
    for c in range(8):
        b, half = c // 2, c % 2
        tok = slice(half * T, (half + 1) * T)
        in_maps.append({
            "x_t": np.ascontiguousarray(x[b, tok, :].T),
            "xpre_t": (zeros_pre if half == 0 else
                       np.ascontiguousarray(x[b, 0:T, :].T)),
            "wqT": wqT, "wkT": wkT, "wvT": wvT, "woT": woT,
            "fmqT": fmqT, "fmkT": fmkT,
            "fmq_b": fmqb_c, "fmk_b": fmkb_c, "fmk_br": fmkbr_c,
            "rms_w": rmsw_c,
            "w1T": w1T, "b1": b1_c,
            "w2T": w2T, "b2": b2_c,
            "maskT": np.ascontiguousarray(maskmul8[b, tok, :].T),
            "triu": triu, "ones": onesc,
        })
    res = run_bass_kernel_spmd(nc, in_maps, core_ids=list(range(8)))
    out = np.empty((B, N, D), dtype=np.float32)
    for c in range(8):
        b, half = c // 2, c % 2
        out[b, half * T:(half + 1) * T, :] = \
            np.asarray(res.results[c]["out_t"]).T
    return out


# version 05
# speedup vs baseline: 1.6462x; 1.2615x over previous
import contextlib
import numpy as np
import ml_dtypes

import concourse.bass as bass
import concourse.mybir as mybir
from concourse.bass_utils import run_bass_kernel_spmd

F32 = mybir.dt.float32
F8 = mybir.dt.float8e4
I32 = mybir.dt.int32
AF = mybir.ActivationFunctionType
ALU = mybir.AluOpType

LN2 = float(np.log(2.0))
M23 = 8388608.0

D = 1024
H = 8
DK = 128
T = 4096
CH = 128
NCH = T // CH
TILE = 256
NTL = T // TILE
FT = 512
NFT = T // FT
DH = 4096
EPS = 1e-5

ENGINES = ("sync", "scalar", "vector", "tensor", "gpsimd")
DMA_ENG = ("sync", "gpsimd")


class OpRec:
    __slots__ = ("engine", "fnidx", "callidx", "reads", "writes", "inc_id",
                 "waits", "is_dma", "_deps")

    def __init__(self, engine, fnidx, callidx, reads, writes, is_dma):
        self.engine = engine
        self.fnidx = fnidx
        self.callidx = callidx
        self.reads = reads
        self.writes = writes
        self.is_dma = is_dma
        self.inc_id = None
        self.waits = []


_ROLES = {
    "matmul": ((0,), (1, 2)),
    "activation": ((0,), (1, 3, 4)),
    "tensor_copy": ((0,), (1,)),
    "tensor_tensor": ((0,), (1, 2)),
    "tensor_scalar": ((0,), (1, 2, 3)),
    "tensor_single_scalar": ((0,), (1, 2)),
    "scalar_tensor_tensor": ((0,), (1, 2, 3)),
    "memset": ((0,), ()),
    "dma_start": ((), ()),
    "tensor_add": ((0,), (1, 2)),
    "tensor_sub": ((0,), (1, 2)),
    "tensor_mul": ((0,), (1, 2)),
}


def _ap_name(a):
    t = getattr(a, "tensor", None)
    if t is None:
        return None
    return getattr(t, "name", None)


class _Recorder:

    class _Ins:
        def then_inc(self, *a, **k):
            return self

    def __init__(self, engine, sink):
        self._engine = engine
        self._sink = sink

    def __getattr__(self, nm):
        def call(*args, **kwargs):
            roles = _ROLES.get(nm)
            reads, writes = set(), set()
            if roles is None:
                for a in list(args) + list(kwargs.values()):
                    n = _ap_name(a)
                    if n:
                        reads.add(n)
                        writes.add(n)
            else:
                for i, a in enumerate(args):
                    n = _ap_name(a)
                    if not n:
                        continue
                    if i in roles[0]:
                        writes.add(n)
                    else:
                        reads.add(n)
                for k, a in kwargs.items():
                    n = _ap_name(a)
                    if not n:
                        continue
                    if k == "out":
                        writes.add(n)
                    else:
                        reads.add(n)
            self._sink.append((nm, reads, writes, nm == "dma_start"))
            return _Recorder._Ins()
        return call


class Sched:

    def __init__(self):
        self.fns = []
        self.ops = []
        self.per_engine_n = {e: 0 for e in ENGINES}
        self.done = {e: 0 for e in ENGINES}
        self._pending_barrier = False

    def stage(self, deps, items):
        for e, fn in items:
            self.fns.append((e, fn, self._pending_barrier))
            self._pending_barrier = False
        return dict(self.done)

    def snap(self):
        return dict(self.done)

    def barrier(self):
        self._pending_barrier = True

    def schedule(self):
        recs = []
        for e, fn, bar in self.fns:
            sink = []
            fn(_Recorder(e, sink))
            recs.append(sink)
        ops = []
        last_write = {}
        readers = {}
        barrier_join = {}
        last_op_eng = {e: None for e in ENGINES}
        for (e, fn, bar), sink in zip(self.fns, recs):
            if bar:
                barrier_join = {k: v for k, v in last_op_eng.items()
                                if v is not None}
            for callidx, (nm, reads, writes, is_dma) in enumerate(sink):
                op = OpRec(e, len(ops), callidx, reads, writes, is_dma)
                deps = list(barrier_join.values())
                for r in reads:
                    o = last_write.get(r)
                    if o is not None:
                        deps.append(o)
                for w in writes:
                    o = last_write.get(w)
                    if o is not None:
                        deps.append(o)
                    deps.extend(readers.get(w, ()))
                op._deps = deps
                for w in writes:
                    last_write[w] = op
                    readers[w] = []
                for r in reads:
                    readers.setdefault(r, []).append(op)
                ops.append(op)
                last_op_eng[e] = op
        needs_inc = [False] * len(ops)
        for op in ops:
            for d in op._deps:
                if d.engine == op.engine == "tensor":
                    continue
                needs_inc[d.fnidx] = True
        inc_total = {e: 0 for e in ENGINES}
        inc_cum = [0] * len(ops)
        for i, op in enumerate(ops):
            if needs_inc[i] or op.is_dma:
                inc_total[op.engine] += 16 if op.is_dma else 1
                op.inc_id = inc_total[op.engine]
            inc_cum[i] = inc_total[op.engine]
        wm = {e: {e2: 0 for e2 in ENGINES} for e in ENGINES}
        dma_ckpt = {e: 0 for e in ENGINES}
        for op in ops:
            tgt = {}
            if op.is_dma and dma_ckpt[op.engine] > 0:
                tgt[op.engine] = dma_ckpt[op.engine]
            for d in op._deps:
                if d.engine == op.engine == "tensor":
                    continue
                v = d.inc_id if d.inc_id is not None else inc_cum[d.fnidx]
                if v and v > tgt.get(d.engine, 0):
                    tgt[d.engine] = v
            for pe, v in tgt.items():
                if v > wm[op.engine][pe]:
                    op.waits.append((pe, v))
                    wm[op.engine][pe] = v
            if op.is_dma and needs_inc[op.fnidx]:
                dma_ckpt[op.engine] = op.inc_id
        self.ops = ops
        self.inc_total = inc_total
        self.op_by_fn = {}
        idx = 0
        for fni, ((e, fn, bar), sink) in enumerate(zip(self.fns, recs)):
            self.op_by_fn[fni] = ops[idx: idx + len(sink)]
            idx += len(sink)


_SCHED_HOOK = None


def build_program(debug=False):
    nc = bass.Bass("TRN2", target_bir_lowering=False)

    d = {}
    for nm, shape in [
        ("x_t", [D, T]), ("xpre_t", [D, T]),
        ("wqT", [D, D]), ("wkT", [D, D]), ("wvT", [D, D]), ("woT", [D, D]),
        ("fmqT", [DK, DK]), ("fmkT", [DK, DK]),
        ("fmq_b", [DK, 1]), ("fmk_b", [DK, 1]), ("fmk_br", [1, DK]),
        ("rms_w", [DK, 1]),
        ("w1T", [D, DH]), ("b1", [128, 32]), ("w2T", [DH, D]),
        ("b2", [128, 8]),
        ("triu", [CH, CH]), ("ones", [128, 1]),
    ]:
        d[nm] = nc.dram_tensor(nm, shape, F32, kind="ExternalInput").ap()
    d["maskT"] = nc.dram_tensor("maskT", [D, T], F8, kind="ExternalInput").ap()
    d["out_t"] = nc.dram_tensor("out_t", [D, T], F32, kind="ExternalOutput").ap()
    d["xb"] = nc.dram_tensor("xb", [D, T], F32,
                             kind="ExternalOutput" if debug else "Internal").ap()
    if debug:
        for nm, shape, dtp in [
            ("dbg_xh", [128, 1024], F32), ("dbg_q", [128, 1024], F32),
            ("dbg_qfa", [128, 1024], F32), ("dbg_kfa", [128, 1024], F32),
            ("dbg_kta", [128, 1024], F32), ("dbg_vt", [128, 1024], F32),
            ("dbg_sc", [128, 128], F32), ("dbg_on", [128, 1024], F32),
            ("dbg_Sa", [128, 1024], F32), ("dbg_Sb", [128, 1024], F32),
            ("dbg_fxh", [128, 4096], F32), ("dbg_x2q", [128, 4096], F32),
            ("dbg_fz", [128, 2048], F32), ("dbg_hq", [128, 4096], F32),
            ("dbg_fy", [128, 512], F32),
            ("dbg_opre", [128, 128], F32), ("dbg_osq", [128, 128], F32),
            ("dbg_stat", [1, 1024], F32),
            ("dbg_qfb", [128, 1024], F32), ("dbg_kfb", [128, 1024], F32),
            ("dbg_fmqb", [128, 2], F32),
            ("dbg_w1q", [128, 1024], F32), ("dbg_b1q", [128, 32], F32),
            ("dbg_w1qf", [128, 8 * DH], F8), ("dbg_w2qf", [128, 32 * D], F8),
            ("dbg_ffx", [128, 4096], F32), ("dbg_fstat", [1, 1024], F32),
            ("dbg_fstat2", [1, 1024], F32), ("dbg_fpsm", [128, 512], F32),
            ("dbg_fpsr", [128, 512], F32),
            ("dbg_b1t", [128, 32], F32), ("dbg_bf", [128, 32], F32),
            ("dbg_bw", [128, 32], F32), ("dbg_br", [128, 32], F32),
            ("dbg_bt", [128, 32], F32), ("dbg_bg", [128, 32], F32),
            ("dbg_bis", [128, 32], F32), ("dbg_bb1", [128, 32], F32),
            ("dbg_bb2", [128, 32], F32), ("dbg_bcomb", [128, 32], F32),
            ("dbg_sch", [128, 8 * 128], F32),
            ("dbg_oph", [128, 8 * 128], F32),
        ]:
            d[nm] = nc.dram_tensor(nm, shape, dtp,
                                   kind="ExternalOutput").ap()

    with contextlib.ExitStack() as st:
        def mksbuf(stack):
            def sbuf(name, shape, dt=F32):
                return stack.enter_context(nc.sbuf_tensor(name, shape, dt)).ap()
            return sbuf

        sbuf = mksbuf(st)

        def psum(name, shape):
            return st.enter_context(nc.psum_tensor(name, shape, F32)).ap()

        cb = sbuf("cb", [128, 8])
        CIDX = {"tiny": 0, "eps": 1, "c6": 2, "c14": 3}
        CVALS = {"tiny": 1e-37, "eps": EPS, "c6": 6.0, "c14": 14.0}

        def cba(nm, row128=True):
            i = CIDX[nm]
            return cb[:, i:i + 1] if row128 else cb[0:1, i:i + 1]

        fmq = sbuf("fmq", [128, DK])
        fmk = sbuf("fmk", [128, DK])
        fmqb = sbuf("fmqb", [128, 2])
        fmkb = sbuf("fmkb", [128, 2])
        fmkb_row = sbuf("fmkb_row", [1, DK])
        rmsw = sbuf("rmsw", [128, 1])
        b1t = sbuf("b1t", [128, 32])
        b2t = sbuf("b2t", [128, 8])
        b1q = sbuf("b1q", [128, 32])
        b2q = sbuf("b2q", [128, 8])
        triu = sbuf("triu_s", [128, CH])
        ones = sbuf("ones_s", [128, 1])
        ones_r = sbuf("ones_r", [1, 128])
        fmkb_bc = sbuf("fmkb_bc", [128, DK])
        S_a = sbuf("S_a", [128, H * DK])
        S_b = sbuf("S_b", [128, H * DK])
        stat = sbuf("stat", [1, 1024])
        stat2 = sbuf("stat2", [1, 1024])
        stat_i2 = sbuf("stat_i2", [1, 1024], I32)
        stat_i3 = sbuf("stat_i3", [1, 1024], I32)

        attn_stack = contextlib.ExitStack()
        asb = mksbuf(attn_stack)
        wbA = asb("wbA", [128, 8 * D])
        wbB = asb("wbB", [128, 8 * D])
        xcb = [asb("xc0", [128, 8 * TILE]), asb("xc1", [128, 8 * TILE])]
        sqc = asb("sqc", [128, 8 * TILE])
        xh = asb("xh", [128, 8 * TILE])
        qkv = asb("qkv", [128, 8 * TILE])
        qf_a = asb("qf_a", [128, H * TILE])
        qf_b = asb("qf_b", [128, H * TILE])
        kf_a = asb("kf_a", [128, H * TILE])
        kf_b = asb("kf_b", [128, H * TILE])
        kt_a = asb("kt_a", [128, H * DK])
        kt_b = asb("kt_b", [128, H * DK])
        vt = asb("vt", [128, D])
        sc_w = asb("sc_w", [128, H * CH])
        yt_w = asb("yt_w", [128, H * CH])
        o_n = asb("o_n", [128, H * TILE])
        ps_p = psum("ps_p", [128, 512])
        ps_q = psum("ps_q", [128, 512])
        ps_v = psum("ps_v", [128, 1024])
        ps_s = psum("ps_s", [128, 512])
        ps_o = psum("ps_o", [128, 512])
        ps_d = psum("ps_d", [128, 512])

        sems = {e: st.enter_context(nc.semaphore("sem_" + e)) for e in ENGINES}

        nc.const_aps.register(F32, -0.125, cb[:, 7:8])
        nc.const_aps.register(F32, -4 * LN2, cb[:, 6:7])

        sch = Sched()

        def stage(deps, items):
            return sch.stage(deps, items)

        items = [("vector", (lambda i, v: lambda e: e.memset(
            cb[:, i:i + 1], v))(CIDX[nm], CVALS[nm])) for nm in CIDX]
        items.append(("vector", lambda e: e.memset(cb[:, 7:8], -0.125)))
        items.append(("vector", lambda e: e.memset(cb[:, 6:7], -4 * LN2)))
        items.append(("vector", lambda e: e.memset(ones_r, 1.0)))
        items.append(("vector", lambda e: e.memset(S_a, 0.0)))
        items.append(("vector", lambda e: e.memset(S_b, 0.0)))
        loads = [
            (fmq, d["fmqT"]), (fmk, d["fmkT"]),
            (triu, d["triu"]), (ones, d["ones"]),
            (rmsw, d["rms_w"]),
            (fmqb[:, 0:1], d["fmq_b"]),
            (fmkb[:, 0:1], d["fmk_b"]),
            (fmkb_row, d["fmk_br"]),
            (b1t, d["b1"]),
            (b2t, d["b2"]),
            (wbA.rearrange("p (c n) -> p c n", c=8),
             d["wkT"].rearrange("(c p) n -> p c n", p=128)),
            (wbB.rearrange("p (c n) -> p c n", c=8),
             d["wvT"].rearrange("(c p) n -> p c n", p=128)),
        ]
        for dst, src in loads:
            items.append(("sync", (lambda dst, src: lambda e: e.dma_start(
                out=dst, in_=src))(dst, src)))
        prev = stage(sch.snap(), items)

        prev = stage(prev, [("scalar", lambda e: e.activation(
            fmqb[:, 1:2], fmqb[:, 0:1], AF.Copy, scale=-1.0,
            bias=-4 * LN2))])
        prev = stage(prev, [("scalar", lambda e: e.activation(
            fmqb[:, 0:1], fmqb[:, 0:1], AF.Copy, scale=1.0,
            bias=-4 * LN2))])
        prev = stage(prev, [("scalar", lambda e: e.activation(
            fmkb[:, 1:2], fmkb[:, 0:1], AF.Copy, scale=-1.0))])
        sch.barrier()
        prev = stage(prev, [("tensor", lambda e: e.matmul(
            ps_v[:, 0:DK], ones_r, fmkb_row, start=True, stop=True))])
        prev = stage(prev, [("scalar", lambda e: e.activation(
            fmkb_bc, ps_v[:, 0:DK], AF.Copy))])

        def wslice(wt, c, dd):
            return wt[:, c * D + dd * 128: c * D + (dd + 1) * 128]

        def dump(prev, name, src_tile, width):
            if not debug:
                return prev
            def mk(e, name=name, src_tile=src_tile, width=width):
                return e.dma_start(out=d[name][:, 0:width],
                                   in_=src_tile[:, 0:width])
            return stage(prev, [("gpsimd", mk)])

        def q_ops(z, out_q, M, clamp, signed, ties, width):
            ops = []
            onv = lambda f: ops.append(("vector", f))
            ons = lambda f: ops.append(("scalar", f))
            l, f, w = QSA[:, :width], QSB[:, :width], QSC[:, :width]
            tt, r = QSE[:, :width], QSF[:, :width]
            bs = QSG[:, :width]
            isx, sx = QSD[:, :width], QSH[:, :width]
            gE, g = QIA[:, :width], QIB[:, :width]
            bEi, p = QIC[:, :width], QIA[:, :width]
            tie, odd = QSG[:, :width], QSD[:, :width]
            if signed:
                ons(lambda e: e.activation(tt, z, AF.Square))
                ons(lambda e: e.activation(l, tt, AF.Ln, bias=cba("tiny")))
                c, inv = 1.0 / (2 * LN2), 2 * LN2
            else:
                ons(lambda e: e.activation(l, z, AF.Ln, bias=cba("tiny")))
                c, inv = 1.0 / LN2, LN2
            ons((lambda c: lambda e: e.activation(gE, l, AF.Copy, scale=c))(c))
            onv((lambda inv: lambda e: e.scalar_tensor_tensor(
                bEi, gE, inv, l, ALU.mult, ALU.is_gt))(inv))
            onv(lambda e: e.scalar_tensor_tensor(
                f, bEi, -1.0, gE, ALU.mult, ALU.add))
            ons((lambda cl: lambda e: e.activation(
                w, f, AF.Relu, bias=cba("c%d" % cl)))(clamp))
            if ties:
                onv((lambda MF: lambda e: e.tensor_scalar(
                    gE, w, -M23, MF * M23, ALU.mult, ALU.add))(
                        float(127 + M + clamp)))
                onv(lambda e: e.tensor_tensor(tt, z, gE.bitcast(F32), ALU.mult))
            else:
                ons(lambda e: e.activation(isx, w, AF.Exp, scale=-LN2))
                onv((lambda s_: lambda e: e.scalar_tensor_tensor(
                    tt, z, s_, isx, ALU.mult, ALU.mult))(
                        float(2.0 ** (M + clamp))))
            if signed:
                onv(lambda e: e.tensor_copy(g, tt))
                onv(lambda e: e.scalar_tensor_tensor(
                    r, g, 0.5, tt, ALU.add, ALU.is_le))
                onv(lambda e: e.scalar_tensor_tensor(
                    bs, g, -0.5, tt, ALU.add, ALU.is_gt))
                onv(lambda e: e.scalar_tensor_tensor(
                    r, bs, -1.0, r, ALU.mult, ALU.add))
                onv(lambda e: e.tensor_tensor(r, r, g, ALU.add))
            else:
                ons(lambda e: e.activation(g, tt, AF.Copy, bias=-0.125))
                onv(lambda e: e.scalar_tensor_tensor(
                    r, g, 0.5, tt, ALU.add, ALU.is_le))
                onv(lambda e: e.tensor_tensor(r, r, g, ALU.add))
            if ties:
                onv(lambda e: e.scalar_tensor_tensor(
                    tie, r, -0.5, tt, ALU.add, ALU.is_equal))
                onv(lambda e: e.tensor_scalar(odd, r, 0.5, None, ALU.mult))
                onv(lambda e: e.tensor_copy(p, odd))
                onv(lambda e: e.tensor_tensor(odd, odd, p, ALU.not_equal))
                onv(lambda e: e.tensor_tensor(tie, tie, odd, ALU.mult))
                onv(lambda e: e.tensor_tensor(r, r, tie, ALU.subtract))
                onv((lambda MF: lambda e: e.tensor_scalar(
                    gE, w, M23, MF * M23, ALU.mult, ALU.add))(
                        float(127 - M - clamp)))
                onv(lambda e: e.tensor_tensor(
                    out_q, r, gE.bitcast(F32), ALU.mult))
            else:
                ons(lambda e: e.activation(sx, w, AF.Exp, scale=LN2))
                onv((lambda s_: lambda e: e.scalar_tensor_tensor(
                    out_q, r, s_, sx, ALU.mult, ALU.mult))(
                        float(2.0 ** (-(M + clamp)))))
            return ops

        def q_h(z, out_q, width):
            ops = []
            onv = lambda f: ops.append(("vector", f))
            ons = lambda f: ops.append(("scalar", f))
            l, f = QSA[:, :width], QSB[:, :width]
            w14, w6 = QSC[:, :width], QSD[:, :width]
            tt, r510 = QSF[:, :width], QSE[:, :width]
            r43 = QSG[:, :width]
            isx, sx = QSH[:, :width], QSH[:, :width]
            gE, g = QIA[:, :width], QIB[:, :width]
            bEi, p = QIC[:, :width], QIA[:, :width]
            tie, odd = QSA[:, :width], QSB[:, :width]
            ons(lambda e: e.activation(l, z, AF.Ln, bias=cba("tiny")))
            ons(lambda e: e.activation(gE, l, AF.Copy, scale=1.0 / LN2))
            onv(lambda e: e.scalar_tensor_tensor(
                bEi, gE, LN2, l, ALU.mult, ALU.is_gt))
            onv(lambda e: e.scalar_tensor_tensor(
                f, bEi, -1.0, gE, ALU.mult, ALU.add))
            ons(lambda e: e.activation(w14, f, AF.Relu, bias=cba("c14")))
            ons(lambda e: e.activation(w6, f, AF.Relu, bias=cba("c6")))
            ons(lambda e: e.activation(isx, w14, AF.Exp, scale=-LN2))
            onv(lambda e: e.scalar_tensor_tensor(
                tt, z, float(2.0 ** 24), isx, ALU.mult, ALU.mult))
            ons(lambda e: e.activation(g, tt, AF.Copy, bias=-0.125))
            onv(lambda e: e.scalar_tensor_tensor(
                r510, g, 0.5, tt, ALU.add, ALU.is_le))
            onv(lambda e: e.tensor_tensor(r510, r510, g, ALU.add))
            onv(lambda e: e.tensor_tensor(w14, w14, w6, ALU.subtract))
            onv(lambda e: e.tensor_scalar(
                gE, w14, M23, (127.0 - 15.0) * M23, ALU.mult, ALU.add))
            onv(lambda e: e.tensor_tensor(tt, r510, gE.bitcast(F32), ALU.mult))
            ons(lambda e: e.activation(g, tt, AF.Copy, bias=-0.125))
            onv(lambda e: e.scalar_tensor_tensor(
                r43, g, 0.5, tt, ALU.add, ALU.is_le))
            onv(lambda e: e.tensor_tensor(r43, r43, g, ALU.add))
            onv(lambda e: e.scalar_tensor_tensor(
                tie, r43, -0.5, tt, ALU.add, ALU.is_equal))
            onv(lambda e: e.tensor_scalar(odd, r43, 0.5, None, ALU.mult))
            onv(lambda e: e.tensor_copy(p, odd))
            onv(lambda e: e.tensor_tensor(odd, odd, p, ALU.not_equal))
            onv(lambda e: e.tensor_tensor(tie, tie, odd, ALU.mult))
            onv(lambda e: e.tensor_tensor(r43, r43, tie, ALU.subtract))
            ons(lambda e: e.activation(sx, w6, AF.Exp, scale=LN2))
            onv(lambda e: e.scalar_tensor_tensor(
                out_q, r43, float(2.0 ** -9), sx, ALU.mult, ALU.mult))
            return ops

        def run_serial(prev, ops):
            for eng, f in ops:
                prev = stage(prev, [(eng, f)])
            return prev

        def emit_ln(prev, src_ap, t0, width, x_tile, sq_tile, out_tile,
                    do_load=True, dbg_ffn=False):
            x3 = x_tile.rearrange("p (c t) -> p c t", c=8)
            sq3 = sq_tile.rearrange("p (c t) -> p c t", c=8)
            o3 = out_tile.rearrange("p (c t) -> p c t", c=8)
            if do_load:
                prev = stage(prev, [("sync", (lambda: lambda e: e.dma_start(
                    out=x3, in_=src_ap.rearrange("(c p) t -> p c t", p=128)
                    [:, :, t0:t0 + width]))())])
            prev = stage(prev, [("scalar", lambda e: e.activation(
                sq_tile, x_tile, AF.Square))])

            def mk_stats(e):
                for c in range(8):
                    e.matmul(ps_s[0:1, 0:width], ones, x3[:, c, :],
                             start=(c == 0), stop=(c == 7))
                for c in range(8):
                    ins = e.matmul(ps_d[0:1, 0:width], ones, sq3[:, c, :],
                                   start=(c == 0), stop=(c == 7))
                return ins
            prev = stage(prev, [("tensor", mk_stats)])

            def mk_sv1(e):
                e.tensor_scalar(stat[0:1, 0:width], ps_s[0:1, 0:width],
                                1.0 / D, None, ALU.mult)
                return e.tensor_scalar(stat2[0:1, 0:width],
                                       ps_d[0:1, 0:width],
                                       1.0 / D, None, ALU.mult)
            prev = stage(prev, [("vector", mk_sv1)])

            def mk_sv2(e):
                return e.tensor_tensor(stat[0:1, width:2 * width],
                                       stat[0:1, 0:width],
                                       stat[0:1, 0:width], ALU.mult)
            prev = stage(prev, [("vector", mk_sv2)])

            def mk_sv3(e):
                return e.tensor_tensor(stat2[0:1, 0:width],
                                       stat2[0:1, 0:width],
                                       stat[0:1, width:2 * width],
                                       ALU.subtract)
            prev = stage(prev, [("vector", mk_sv3)])

            def mk_ss(e):
                e.activation(stat2[0:1, width:2 * width],
                             stat2[0:1, 0:width], AF.Ln,
                             bias=cba("eps", False))
                return e.activation(stat2[0:1, 0:width],
                                    stat2[0:1, width:2 * width],
                                    AF.Exp, scale=-0.5)
            prev = stage(prev, [("scalar", mk_ss)])

            def mk_bc(e):
                e.matmul(ps_v[:, 0:width], ones_r, stat[0:1, 0:width],
                         start=True, stop=True)
                return e.matmul(ps_v[:, 512:512 + width], ones_r,
                                stat2[0:1, 0:width], start=True, stop=True)
            prev = stage(prev, [("tensor", mk_bc)])
            m_b = ps_v[:, 0:width].unsqueeze(1).broadcast_to([128, 8, width])
            r_b = ps_v[:, 512:512 + width].unsqueeze(1)\
                .broadcast_to([128, 8, width])

            if debug and dbg_ffn:
                def mkd1(e):
                    e.dma_start(out=d["dbg_ffx"][:, 0:8 * width],
                                in_=x_tile[:, 0:8 * width])
                    e.dma_start(out=d["dbg_fstat"][0:1, 0:width],
                                in_=stat[0:1, 0:width])
                    return e.dma_start(out=d["dbg_fstat2"][0:1, 0:width],
                                       in_=stat2[0:1, 0:width])
                prev = stage(prev, [("gpsimd", mkd1)])

            def mk_ap(e):
                e.tensor_tensor(sq3, x3, m_b, ALU.subtract)
                return e.tensor_tensor(o3, sq3, r_b, ALU.mult)
            prev = stage(prev, [("vector", mk_ap)])
            return prev

        def emit_wload(prev, wsrc, wbuf):
            def mk(e):
                return e.dma_start(
                    out=wbuf.rearrange("p (c n) -> p c n", c=8),
                    in_=wsrc.rearrange("(c p) n -> p c n", p=128))
            return stage(prev, [("sync", mk)])

        def emit_proj_t(prev, wbuf, out_sb, ps2):
            for dd in range(8):
                ps = ps2[dd % 2]

                def mk_mm(e, dd=dd, ps=ps):
                    for c in range(8):
                        ins = e.matmul(ps[:, 0:TILE], wslice(wbuf, c, dd),
                                       xh[:, c * TILE:(c + 1) * TILE],
                                       start=(c == 0), stop=(c == 7))
                    return ins
                prev = stage(prev, [("tensor", mk_mm)])

                def mk_ev(e, dd=dd, ps=ps):
                    return e.activation(out_sb[:, dd * TILE:(dd + 1) * TILE],
                                        ps[:, 0:TILE], AF.Copy)
                prev = stage(prev, [("scalar", mk_ev)])
            return prev

        def emit_fm_t(prev, src_fm, fmw, fmb, out_a, out_b):
            for h in range(H):
                ps = ps_s if h % 2 == 0 else ps_d

                def mk_mm(e, h=h, ps=ps):
                    return e.matmul(ps[:, 0:TILE], fmw,
                                    src_fm[:, h * TILE:(h + 1) * TILE],
                                    start=True, stop=True)
                prev = stage(prev, [("tensor", mk_mm)])

                def mk_ev(e, h=h, ps=ps):
                    e.activation(out_a[:, h * TILE:(h + 1) * TILE],
                                 ps[:, 0:TILE], AF.Exp, bias=fmb[:, 0:1])
                    return e.activation(out_b[:, h * TILE:(h + 1) * TILE],
                                        ps[:, 0:TILE], AF.Exp, scale=-1.0,
                                        bias=fmb[:, 1:2])
                prev = stage(prev, [("scalar", mk_ev)])
            return prev

        def emit_kv_chunk(prev, wvbuf, cix):
            c0 = cix * CH
            b_b = fmkb_bc
            for h in range(H):
                def mk_mm(e, h=h):
                    return e.matmul(ps_s[:, 0:CH],
                                    qkv[:, h * TILE + c0:h * TILE + c0 + CH],
                                    fmk, start=True, stop=True)
                prev = stage(prev, [("tensor", mk_mm)])

                def mk_add(e, h=h):
                    return e.tensor_tensor(sc_w[:, 0:CH], ps_s[:, 0:CH], b_b,
                                           ALU.add)
                prev = stage(prev, [("vector", mk_add)])

                def mk_ev(e, h=h):
                    e.activation(kt_a[:, h * DK:(h + 1) * DK], sc_w[:, 0:CH],
                                 AF.Exp)
                    return e.activation(kt_b[:, h * DK:(h + 1) * DK],
                                        sc_w[:, 0:CH], AF.Exp, scale=-1.0)
                prev = stage(prev, [("scalar", mk_ev)])
            def mk_vm(e):
                wv3 = wvbuf.rearrange("p (c n) -> p c n", c=8)
                for half in range(2):
                    for c in range(8):
                        ins = e.matmul(
                            ps_v[:, half * 512:(half + 1) * 512],
                            xh[:, c * TILE + c0:c * TILE + c0 + CH],
                            wv3[:, c, half * 512:(half + 1) * 512],
                            start=(c == 0), stop=(c == 7))
                return ins
            prev = stage(prev, [("tensor", mk_vm)])

            def mk_vev(e):
                e.activation(vt[:, 0:512], ps_v[:, 0:512], AF.Copy)
                return e.activation(vt[:, 512:1024], ps_v[:, 512:1024],
                                    AF.Copy)
            prev = stage(prev, [("scalar", mk_vev)])
            return prev

        def emit_scan_chunk(prev, cix, with_output):
            c0 = cix * CH
            if with_output:
                def mk_sc(e):
                    for h in range(H):
                        ps = ps_s if h < 4 else ps_p
                        off = (h % 4) * CH
                        e.matmul(ps[:, off:off + CH],
                                 kf_a[:, h * TILE + c0:h * TILE + c0 + CH],
                                 qf_a[:, h * TILE + c0:h * TILE + c0 + CH],
                                 start=True, stop=False)
                        ins = e.matmul(
                            ps[:, off:off + CH],
                            kf_b[:, h * TILE + c0:h * TILE + c0 + CH],
                            qf_b[:, h * TILE + c0:h * TILE + c0 + CH],
                            start=False, stop=True)
                    return ins
                prev = stage(prev, [("tensor", mk_sc)])

                tri_b = triu.unsqueeze(1).broadcast_to([128, 4, CH])
                def mk_mask(e):
                    e.tensor_tensor(
                        sc_w.rearrange("p (h t) -> p h t", h=H)[:, 0:4, :],
                        ps_s.rearrange("p (h t) -> p h t", h=4), tri_b,
                        ALU.mult)
                    return e.tensor_tensor(
                        sc_w.rearrange("p (h t) -> p h t", h=H)[:, 4:8, :],
                        ps_p.rearrange("p (h t) -> p h t", h=4), tri_b,
                        ALU.mult)
                prev = stage(prev, [("vector", mk_mask)])

                def mk_o(e):
                    for h in range(H):
                        ps = ps_o if h < 4 else ps_q
                        off = (h % 4) * CH
                        e.matmul(ps[:, off:off + CH],
                                 vt[:, h * DK:(h + 1) * DK],
                                 sc_w[:, h * CH:(h + 1) * CH],
                                 start=True, stop=False)
                        e.matmul(ps[:, off:off + CH], S_a[:, h * DK:(h + 1) * DK],
                                 qf_a[:, h * TILE + c0:h * TILE + c0 + CH],
                                 start=False, stop=False)
                        ins = e.matmul(
                            ps[:, off:off + CH], S_b[:, h * DK:(h + 1) * DK],
                            qf_b[:, h * TILE + c0:h * TILE + c0 + CH],
                            start=False, stop=True)
                    return ins
                prev = stage(prev, [("tensor", mk_o)])

                on4 = o_n.rearrange("p (h t) -> p h t", h=H)
                o_n_c0a = on4[:, 0:4, c0:c0 + CH]
                o_n_c0b = on4[:, 4:8, c0:c0 + CH]

                def mk_oev2(e):
                    e.activation(o_n_c0a, ps_o.rearrange(
                        "p (h t) -> p h t", h=4), AF.Copy)
                    e.activation(o_n_c0b, ps_q.rearrange(
                        "p (h t) -> p h t", h=4), AF.Copy)
                    e.activation(yt_w[:, 0:512], ps_o[:, 0:512], AF.Square)
                    return e.activation(yt_w[:, 512:1024], ps_q[:, 0:512],
                                        AF.Square)
                prev = stage(prev, [("scalar", mk_oev2)])

                def mk_osum(e):
                    e.matmul(ps_d[0:1, 0:512], ones, yt_w[:, 0:512],
                             start=True, stop=True)
                    return e.matmul(ps_v[0:1, 0:512], ones, yt_w[:, 512:1024],
                                    start=True, stop=True)
                prev = stage(prev, [("tensor", mk_osum)])

                def mk_r1(e):
                    e.tensor_scalar(stat[0:1, 0:512], ps_d[0:1, 0:512],
                                    1.0 / DK, EPS, ALU.mult, ALU.add)
                    return e.tensor_scalar(stat[0:1, 512:1024],
                                           ps_v[0:1, 0:512],
                                           1.0 / DK, EPS, ALU.mult, ALU.add)
                prev = stage(prev, [("vector", mk_r1)])

                def mk_r2(e):
                    return e.tensor_scalar(
                        stat_i2[0:1, 0:1024], stat[0:1, 0:1024].bitcast(I32),
                        float(2.0 ** -23), -127.0, ALU.mult, ALU.add)
                prev = stage(prev, [("vector", mk_r2)])

                def mk_r3(e):
                    return e.tensor_scalar(stat_i3[0:1, 0:1024],
                                           stat_i2[0:1, 0:1024], -M23,
                                           127.0 * M23, ALU.mult, ALU.add)
                prev = stage(prev, [("vector", mk_r3)])

                def mk_r4(e):
                    return e.tensor_tensor(
                        stat2[0:1, 0:1024], stat[0:1, 0:1024],
                        stat_i3[0:1, 0:1024].bitcast(F32), ALU.mult)
                prev = stage(prev, [("vector", mk_r4)])

                def mk_r5(e):
                    return e.activation(stat2[0:1, 0:1024], stat2[0:1, 0:1024],
                                        AF.Ln)
                prev = stage(prev, [("scalar", mk_r5)])

                def mk_r6(e):
                    return e.scalar_tensor_tensor(
                        stat[0:1, 0:1024], stat_i2[0:1, 0:1024], LN2,
                        stat2[0:1, 0:1024], ALU.mult, ALU.add)
                prev = stage(prev, [("vector", mk_r6)])

                def mk_r7(e):
                    return e.activation(stat[0:1, 0:1024], stat[0:1, 0:1024],
                                        AF.Exp, scale=-0.5)
                prev = stage(prev, [("scalar", mk_r7)])

                def mk_rb(e):
                    e.matmul(ps_o[:, 0:512], ones_r, stat[0:1, 0:512],
                             start=True, stop=True)
                    return e.matmul(ps_q[:, 0:512], ones_r,
                                    stat[0:1, 512:1024], start=True, stop=True)
                prev = stage(prev, [("tensor", mk_rb)])

                def mk_oap(e):
                    e.scalar_tensor_tensor(
                        on4[:, 0:4, c0:c0 + CH], on4[:, 0:4, c0:c0 + CH],
                        rmsw[:, 0:1],
                        ps_o.rearrange("p (h t) -> p h t", h=4),
                        ALU.mult, ALU.mult)
                    return e.scalar_tensor_tensor(
                        on4[:, 4:8, c0:c0 + CH], on4[:, 4:8, c0:c0 + CH],
                        rmsw[:, 0:1],
                        ps_q.rearrange("p (h t) -> p h t", h=4),
                        ALU.mult, ALU.mult)
                prev = stage(prev, [("vector", mk_oap)])

            def mk_ds(e):
                for h in range(H):
                    ps = ps_s if h < 4 else ps_p
                    off = (h % 4) * CH
                    ins = e.matmul(ps[:, off:off + CH],
                                   kt_a[:, h * DK:(h + 1) * DK],
                                   vt[:, h * DK:(h + 1) * DK],
                                   start=True, stop=True)
                for h in range(H):
                    ps = ps_o if h < 4 else ps_q
                    off = (h % 4) * CH
                    ins = e.matmul(ps[:, off:off + CH],
                                   kt_b[:, h * DK:(h + 1) * DK],
                                   vt[:, h * DK:(h + 1) * DK],
                                   start=True, stop=True)
                return ins
            prev = stage(prev, [("tensor", mk_ds)])

            def mk_sadd(e):
                e.tensor_tensor(S_a[:, 0:512], S_a[:, 0:512], ps_s[:, 0:512],
                                ALU.add)
                e.tensor_tensor(S_a[:, 512:1024], S_a[:, 512:1024],
                                ps_p[:, 0:512], ALU.add)
                e.tensor_tensor(S_b[:, 0:512], S_b[:, 0:512], ps_o[:, 0:512],
                                ALU.add)
                return e.tensor_tensor(S_b[:, 512:1024], S_b[:, 512:1024],
                                       ps_q[:, 0:512], ALU.add)
            prev = stage(prev, [("vector", mk_sadd)])
            return prev

        NCHT = TILE // CH
        for ti in range(NTL):
            t0 = ti * TILE
            buf = xcb[ti % 2]
            def mk_ldx(e, t0=t0, buf=buf):
                return e.dma_start(
                    out=buf.rearrange("p (c t) -> p c t", c=8),
                    in_=d["xpre_t"].rearrange("(c p) t -> p c t", p=128)
                    [:, :, t0:t0 + TILE])
            prev = stage(prev, [("sync", mk_ldx)])
            prev = emit_ln(prev, None, t0, TILE, buf, sqc, xh, do_load=False)
            prev = emit_proj_t(prev, wbA, qkv, (ps_p, ps_q))
            for cix in range(NCHT):
                prev = emit_kv_chunk(prev, wbB, cix)
                prev = emit_scan_chunk(prev, cix, with_output=False)
        prev = dump(prev, "dbg_Sa", S_a, 1024)
        prev = dump(prev, "dbg_Sb", S_b, 1024)

        for ti in range(NTL):
            t0 = ti * TILE
            buf = xcb[ti % 2]
            def mk_ldx(e, t0=t0, buf=buf):
                return e.dma_start(
                    out=buf.rearrange("p (c t) -> p c t", c=8),
                    in_=d["x_t"].rearrange("(c p) t -> p c t", p=128)
                    [:, :, t0:t0 + TILE])
            prev = stage(prev, [("sync", mk_ldx)])
            prev = emit_ln(prev, None, t0, TILE, buf, sqc, xh, do_load=False)
            prev = emit_wload(prev, d["wqT"], wbA)
            prev = emit_proj_t(prev, wbA, qkv, (ps_p, ps_q))
            prev = emit_fm_t(prev, qkv, fmq, fmqb, qf_a, qf_b)
            prev = emit_wload(prev, d["wkT"], wbB)
            prev = emit_proj_t(prev, wbB, qkv, (ps_p, ps_q))
            prev = emit_fm_t(prev, qkv, fmk, fmkb, kf_a, kf_b)
            prev = emit_wload(prev, d["wvT"], wbA)
            for cix in range(NCHT):
                prev = emit_kv_chunk(prev, wbA, cix)
                prev = emit_scan_chunk(prev, cix, with_output=True)
            prev = emit_wload(prev, d["woT"], wbB)
            on4 = o_n.rearrange("p (h t) -> p h t", h=H)
            for dd in range(8):
                ps = ps_o if dd % 2 == 0 else ps_q

                def mk_wo(e, dd=dd, ps=ps):
                    for h in range(H):
                        ins = e.matmul(
                            ps[:, 0:TILE], wslice(wbB, h, dd),
                            o_n[:, h * TILE:(h + 1) * TILE],
                            start=(h == 0), stop=(h == 7))
                    return ins
                prev = stage(prev, [("tensor", mk_wo)])

                def mk_res(e, dd=dd, ps=ps, buf=buf):
                    return e.tensor_tensor(
                        buf[:, dd * TILE:(dd + 1) * TILE],
                        buf[:, dd * TILE:(dd + 1) * TILE],
                        ps[:, 0:TILE], ALU.add)
                prev = stage(prev, [("vector", mk_res)])

            def mk_st(e, t0=t0, buf=buf):
                return e.dma_start(
                    out=d["xb"].rearrange("(c p) t -> p c t", p=128)
                    [:, :, t0:t0 + TILE],
                    in_=buf.rearrange("p (c t) -> p c t", c=8))
            prev = stage(prev, [("gpsimd", mk_st)])

        sch.barrier()
        attn_stack.close()
        ffn_stack = contextlib.ExitStack()
        fsb = mksbuf(ffn_stack)
        QW = 1024
        QSA = fsb("QSA", [128, QW]); QSB = fsb("QSB", [128, QW])
        QSC = fsb("QSC", [128, QW]); QSD = fsb("QSD", [128, QW])
        QSE = fsb("QSE", [128, QW]); QSF = fsb("QSF", [128, QW])
        QSG = fsb("QSG", [128, QW]); QSH = fsb("QSH", [128, QW])
        QIA = fsb("QIA", [128, QW], I32)
        QIB = fsb("QIB", [128, QW], I32)
        QIC = fsb("QIC", [128, QW], I32)
        fx = fsb("fx", [128, 8 * FT])
        fxh = fsb("fxh", [128, 8 * FT])
        fz = fsb("fz", [128, 1024])
        fy = fsb("fy", [128, FT])
        x2q = fsb("x2q", [128, 8 * FT], F8)
        hq = fsb("hq", [128, 32 * FT], F8)
        w1q = fsb("w1q", [128, 8 * DH], F8)
        w2q = fsb("w2q", [128, 32 * D], F8)
        msk8 = fsb("msk8", [128, 8 * FT], F8)
        wload = fsb("wload", [128, 1024])

        def q_weight(prev, srcT, nchunk, cols, dst_fp8):
            src3 = srcT.rearrange("(c p) n -> p c n", p=128)
            step = 1024
            for c in range(nchunk):
                for s0 in range(0, cols, step):
                    wdt = min(step, cols - s0)

                    def mk_ld(e, c=c, s0=s0, wdt=wdt):
                        e.dma_start(out=wload[:, 0:wdt],
                                    in_=src3[:, c, s0:s0 + wdt])
                    prev = stage(prev, [("sync", mk_ld)])
                    prev = run_serial(prev, q_ops(
                        wload[:, 0:wdt],
                        dst_fp8[:, c * cols + s0:c * cols + s0 + wdt],
                        M=3, clamp=6, signed=True, ties=True, width=wdt))
            return prev

        prev = q_weight(prev, d["w1T"], 8, DH, w1q)
        prev = q_weight(prev, d["w2T"], 32, D, w2q)
        if debug:
            prev = stage(prev, [("gpsimd", lambda e: e.dma_start(
                out=d["dbg_w1qf"], in_=w1q))])
            prev = stage(prev, [("gpsimd", lambda e: e.dma_start(
                out=d["dbg_w2qf"], in_=w2q))])
        prev = dump(prev, "dbg_b1t", b1t, 32)
        if debug:
            W_ = 32
            z_, out_ = b1t, b1q
            l_, f_, w_ = QSA[:, :W_], QSB[:, :W_], QSC[:, :W_]
            tt_, r_ = QSE[:, :W_], QSF[:, :W_]
            bs_ = QSG[:, :W_]
            gE_, g_ = QIA[:, :W_], QIB[:, :W_]
            bEi_, p_ = QIC[:, :W_], QIA[:, :W_]
            tie_, odd_ = QSG[:, :W_], QSD[:, :W_]
            prev = stage(prev, [("scalar", lambda e: e.activation(tt_, z_, AF.Square))])
            prev = stage(prev, [("scalar", lambda e: e.activation(l_, tt_, AF.Ln, bias=cba("tiny")))])
            prev = stage(prev, [("scalar", lambda e: e.activation(gE_, l_, AF.Copy, scale=1.0 / (2 * LN2)))])
            prev = stage(prev, [("vector", lambda e: e.scalar_tensor_tensor(bEi_, gE_, 2 * LN2, l_, ALU.mult, ALU.is_gt))])
            prev = stage(prev, [("vector", lambda e: e.scalar_tensor_tensor(f_, bEi_, -1.0, gE_, ALU.mult, ALU.add))])
            prev = stage(prev, [("scalar", lambda e: e.activation(w_, f_, AF.Relu, bias=cba("c6")))])
            prev = stage(prev, [("vector", lambda e: e.tensor_scalar(gE_, w_, -M23, 136.0 * M23, ALU.mult, ALU.add))])
            def mkbis(e):
                return e.tensor_copy(QSD[:, 0:W_], gE_)
            prev = stage(prev, [("vector", mkbis)])
            prev = dump(prev, "dbg_bis", QSD, 32)
            prev = stage(prev, [("vector", lambda e: e.tensor_tensor(tt_, z_, gE_.bitcast(F32), ALU.mult))])
            prev = stage(prev, [("vector", lambda e: e.tensor_copy(g_, tt_))])
            prev = stage(prev, [("vector", lambda e: e.scalar_tensor_tensor(r_, g_, 0.5, tt_, ALU.add, ALU.is_le))])
            prev = dump(prev, "dbg_bb1", QSF, 32)
            prev = stage(prev, [("vector", lambda e: e.scalar_tensor_tensor(bs_, g_, -0.5, tt_, ALU.add, ALU.is_gt))])
            prev = dump(prev, "dbg_bb2", QSG, 32)
            prev = stage(prev, [("vector", lambda e: e.scalar_tensor_tensor(r_, bs_, -1.0, r_, ALU.mult, ALU.add))])
            prev = dump(prev, "dbg_bcomb", QSF, 32)
            prev = stage(prev, [("vector", lambda e: e.tensor_tensor(r_, r_, g_, ALU.add))])
            prev = stage(prev, [("vector", lambda e: e.scalar_tensor_tensor(tie_, r_, -0.5, tt_, ALU.add, ALU.is_equal))])
            prev = stage(prev, [("vector", lambda e: e.tensor_scalar(odd_, r_, 0.5, None, ALU.mult))])
            prev = stage(prev, [("vector", lambda e: e.tensor_copy(p_, odd_))])
            prev = stage(prev, [("vector", lambda e: e.tensor_tensor(odd_, odd_, p_, ALU.not_equal))])
            prev = stage(prev, [("vector", lambda e: e.tensor_tensor(tie_, tie_, odd_, ALU.mult))])
            prev = stage(prev, [("vector", lambda e: e.tensor_tensor(r_, r_, tie_, ALU.subtract))])
            prev = stage(prev, [("vector", lambda e: e.tensor_scalar(gE_, w_, M23, 118.0 * M23, ALU.mult, ALU.add))])
            prev = stage(prev, [("vector", lambda e: e.tensor_tensor(out_, r_, gE_.bitcast(F32), ALU.mult))])
        else:
            prev = run_serial(prev, q_ops(b1t, b1q, M=3, clamp=6, signed=True,
                                          ties=True, width=32))
        if debug:
            prev = dump(prev, "dbg_bf", QSB, 32)
            prev = dump(prev, "dbg_bw", QSC, 32)
            prev = dump(prev, "dbg_br", QSF, 32)
            prev = dump(prev, "dbg_bt", QSE, 32)
            def mkg(e):
                return e.tensor_copy(QSD[:, 0:32], QIB[:, 0:32])
            prev = stage(prev, [("vector", mkg)])
            prev = dump(prev, "dbg_bg", QSD, 32)
        prev = run_serial(prev, q_ops(b2t, b2q, M=3, clamp=6, signed=True,
                                      ties=True, width=8))

        if debug:
            def mkdw(e):
                return e.tensor_copy(QSA[:, 0:1024], w1q[:, 0:1024])
            prev = stage(prev, [("vector", mkdw)])
            prev = dump(prev, "dbg_w1q", QSA, 1024)
            def mkdb(e):
                return e.tensor_copy(QSB[:, 0:32], b1q)
            prev = stage(prev, [("vector", mkdb)])
            prev = dump(prev, "dbg_b1q", QSB, 32)

        x2q3 = x2q.rearrange("p (c t) -> p c t", c=8)
        hq3 = hq.rearrange("p (c t) -> p c t", c=32)
        fx3 = fx.rearrange("p (c t) -> p c t", c=8)
        w1q3 = w1q.rearrange("p (c n) -> p c n", c=8)
        w2q3 = w2q.rearrange("p (c n) -> p c n", c=32)
        for ti in range(NFT):
            t0 = ti * FT
            prev = emit_ln(prev, d["xb"], t0, FT, fx, fxh, fxh,
                           dbg_ffn=(ti == 0))

            def mk_mld(e, t0=t0):
                e.dma_start(out=msk8.rearrange("p (c t) -> p c t", c=8),
                            in_=d["maskT"].rearrange("(c p) t -> p c t", p=128)
                            [:, :, t0:t0 + FT])
            prev = stage(prev, [("sync", mk_mld)])
            XW = min(1024, 8 * FT)
            for quar in range(8 * FT // XW):
                prev = run_serial(prev, q_ops(
                    fxh[:, quar * XW:(quar + 1) * XW],
                    x2q[:, quar * XW:(quar + 1) * XW],
                    M=3, clamp=6, signed=True, ties=False, width=XW))
            if debug and ti == 0:
                prev = dump(prev, "dbg_fxh", fxh, 8 * FT)
                def mkd(e):
                    return e.tensor_copy(QSA[:, 0:1024], x2q[:, 0:1024])
                prev = stage(prev, [("vector", mkd)])
                prev = dump(prev, "dbg_x2q", QSA, 1024)
            for grp in range(16):
                for sub in range(2):
                    dh = grp * 2 + sub

                    def mk_m(e, dh=dh):
                        for c in range(8):
                            ins = e.matmul(ps_p[:, 0:FT],
                                           w1q3[:, c, dh * 128:(dh + 1) * 128],
                                           x2q3[:, c, :],
                                           start=(c == 0), stop=(c == 7))
                        return ins
                    prev = stage(prev, [("tensor", mk_m)])

                    def mk_e(e, dh=dh, sub=sub):
                        e.activation(fz[:, sub * FT:(sub + 1) * FT],
                                     ps_p[:, 0:FT], AF.Relu,
                                     bias=b1q[:, dh:dh + 1])
                    prev = stage(prev, [("scalar", mk_e)])
                if debug and ti == 0 and grp == 0:
                    prev = dump(prev, "dbg_fz", fz, 2 * FT)
                prev = run_serial(prev, q_h(
                    fz[:, 0:2 * FT], hq[:, grp * 2 * FT:(grp + 1) * 2 * FT],
                    width=2 * FT))
                if debug and ti == 0 and grp == 0:
                    def mkd2(e):
                        return e.tensor_copy(QSA[:, 0:2 * FT],
                                             hq[:, 0:2 * FT])
                    prev = stage(prev, [("vector", mkd2)])
                    prev = dump(prev, "dbg_hq", QSA, 2 * FT)
            for dd in range(8):
                def mk_m2(e, dd=dd):
                    for c in range(32):
                        ins = e.matmul(ps_q[:, 0:FT],
                                       w2q3[:, c, dd * 128:(dd + 1) * 128],
                                       hq3[:, c, :],
                                       start=(c == 0), stop=(c == 31))
                    return ins
                prev = stage(prev, [("tensor", mk_m2)])

                def mk_e2(e, dd=dd):
                    return e.tensor_scalar(fy, ps_q[:, 0:FT],
                                           b2q[:, dd:dd + 1], None, ALU.add)
                prev = stage(prev, [("vector", mk_e2)])
                prev = run_serial(prev, q_ops(
                    fy, fz[:, 0:FT], M=10, clamp=14, signed=True, ties=False,
                    width=FT))

                def mk_dr1(e, dd=dd):
                    return e.tensor_tensor(
                        fy, fz[:, 0:FT],
                        msk8.rearrange("p (c t) -> p c t", c=8)[:, dd, :],
                        ALU.mult)
                prev = stage(prev, [("vector", mk_dr1)])

                def mk_dr2(e, dd=dd):
                    return e.tensor_tensor(fy, fy, fx3[:, dd, :], ALU.add)
                prev = stage(prev, [("vector", mk_dr2)])
                if debug and ti == 0 and dd == 0:
                    prev = dump(prev, "dbg_fy", fy, FT)

                def mk_out(e, dd=dd, t0=t0):
                    e.dma_start(
                        out=d["out_t"].rearrange("(c p) t -> p c t", p=128)
                        [:, dd, t0:t0 + FT], in_=fy)
                prev = stage(prev, [("gpsimd", mk_out)])

        ffn_stack.close()

        sch.schedule()
        if _SCHED_HOOK is not None:
            _SCHED_HOOK(sch)
        final = sch.inc_total

        class _Replay:
            def __init__(self, eng, oplist, sems):
                self._e = eng
                self._ops = oplist
                self._i = 0
                self._sems = sems

            def __getattr__(self, nm):
                f = getattr(self._e, nm)
                if not callable(f):
                    return f

                def g(*a, **k):
                    op = self._ops[self._i]
                    self._i += 1
                    for pe, v in op.waits:
                        self._e.wait_ge(self._sems[pe], v)
                    r = f(*a, **k)
                    if op.inc_id is not None and hasattr(r, "then_inc"):
                        r.then_inc(self._sems[op.engine],
                                   16 if op.is_dma else 1)
                    return r
                return g

        def engine_runner(ename):
            def run(eng):
                for fni, (e2, fn, bar) in enumerate(sch.fns):
                    if e2 != ename:
                        continue
                    fn(_Replay(eng, sch.op_by_fn[fni], sems))
                if ename == "gpsimd":
                    for e2 in ENGINES:
                        if final[e2] > 0:
                            eng.wait_ge(sems[e2], final[e2])
            return run

        with nc.Block() as block:
            block.sync(engine_runner("sync"))
            block.scalar(engine_runner("scalar"))
            block.vector(engine_runner("vector"))
            block.tensor(engine_runner("tensor"))
            block.gpsimd(engine_runner("gpsimd"))

    return nc


def _dropout_mask_np():
    import jax
    cpu = jax.devices("cpu")[0]
    with jax.default_device(cpu):
        key = jax.random.key(42, impl="threefry2x32")
        keep = jax.random.bernoulli(key, 0.8, (4, 8192, 1024))
        keep = np.asarray(keep)
    return keep


def kernel(L0_attn_x, L0_attn_Wq, L0_attn_Wk, L0_attn_Wv, L0_attn_Wo,
           L0_attn_fmq_w, L0_attn_fmq_b, L0_attn_fmk_w, L0_attn_fmk_b,
           L0_attn_rms_w, L0_ffn_w1, L0_ffn_b1, L0_ffn_w2, L0_ffn_b2):
    x = np.ascontiguousarray(np.asarray(L0_attn_x, dtype=np.float32))
    B, N, _ = x.shape
    keep = _dropout_mask_np()
    maskmul = np.where(keep, np.float32(1.25), np.float32(0.0))
    maskmul8 = maskmul.astype(ml_dtypes.float8_e4m3)

    wqT = np.ascontiguousarray(np.asarray(L0_attn_Wq).T)
    wkT = np.ascontiguousarray(np.asarray(L0_attn_Wk).T)
    wvT = np.ascontiguousarray(np.asarray(L0_attn_Wv).T)
    woT = np.ascontiguousarray(np.asarray(L0_attn_Wo).T)
    fmqT = np.ascontiguousarray(np.asarray(L0_attn_fmq_w).T)
    fmkT = np.ascontiguousarray(np.asarray(L0_attn_fmk_w).T)
    w1T = np.ascontiguousarray(np.asarray(L0_ffn_w1).T)
    w2T = np.ascontiguousarray(np.asarray(L0_ffn_w2).T)

    triu = np.triu(np.ones((CH, CH), dtype=np.float32))
    fmqb_c = np.ascontiguousarray(
        np.asarray(L0_attn_fmq_b, dtype=np.float32).reshape(DK, 1))
    fmkb_c = np.ascontiguousarray(
        np.asarray(L0_attn_fmk_b, dtype=np.float32).reshape(DK, 1))
    fmkbr_c = np.ascontiguousarray(
        np.asarray(L0_attn_fmk_b, dtype=np.float32).reshape(1, DK))
    rmsw_c = np.ascontiguousarray(
        np.asarray(L0_attn_rms_w, dtype=np.float32).reshape(DK, 1))
    b1_c = np.ascontiguousarray(
        np.asarray(L0_ffn_b1, dtype=np.float32).reshape(32, 128).T)
    b2_c = np.ascontiguousarray(
        np.asarray(L0_ffn_b2, dtype=np.float32).reshape(8, 128).T)
    onesc = np.ones((128, 1), dtype=np.float32)
    zeros_pre = np.zeros((D, T), dtype=np.float32)

    nc = build_program()
    in_maps = []
    for c in range(8):
        b, half = c // 2, c % 2
        tok = slice(half * T, (half + 1) * T)
        in_maps.append({
            "x_t": np.ascontiguousarray(x[b, tok, :].T),
            "xpre_t": (zeros_pre if half == 0 else
                       np.ascontiguousarray(x[b, 0:T, :].T)),
            "wqT": wqT, "wkT": wkT, "wvT": wvT, "woT": woT,
            "fmqT": fmqT, "fmkT": fmkT,
            "fmq_b": fmqb_c, "fmk_b": fmkb_c, "fmk_br": fmkbr_c,
            "rms_w": rmsw_c,
            "w1T": w1T, "b1": b1_c,
            "w2T": w2T, "b2": b2_c,
            "maskT": np.ascontiguousarray(maskmul8[b, tok, :].T),
            "triu": triu, "ones": onesc,
        })
    res = run_bass_kernel_spmd(nc, in_maps, core_ids=list(range(8)))
    out = np.empty((B, N, D), dtype=np.float32)
    for c in range(8):
        b, half = c // 2, c % 2
        out[b, half * T:(half + 1) * T, :] = \
            np.asarray(res.results[c]["out_t"]).T
    return out


# version 06
# speedup vs baseline: 1.7450x; 1.0600x over previous
import contextlib
import numpy as np
import ml_dtypes

import concourse.bass as bass
import concourse.mybir as mybir
from concourse.bass_utils import run_bass_kernel_spmd

F32 = mybir.dt.float32
F8 = mybir.dt.float8e4
I32 = mybir.dt.int32
AF = mybir.ActivationFunctionType
ALU = mybir.AluOpType

LN2 = float(np.log(2.0))
M23 = 8388608.0

D = 1024
H = 8
DK = 128
T = 4096
CH = 128
NCH = T // CH
TILE = 512
NTL = T // TILE
FT = 512
NFT = T // FT
DH = 4096
EPS = 1e-5

ENGINES = ("sync", "scalar", "vector", "tensor", "gpsimd")
DMA_ENG = ("sync", "gpsimd")


class OpRec:
    __slots__ = ("engine", "fnidx", "callidx", "reads", "writes", "inc_id",
                 "waits", "is_dma", "_deps")

    def __init__(self, engine, fnidx, callidx, reads, writes, is_dma):
        self.engine = engine
        self.fnidx = fnidx
        self.callidx = callidx
        self.reads = reads
        self.writes = writes
        self.is_dma = is_dma
        self.inc_id = None
        self.waits = []


_ROLES = {
    "matmul": ((0,), (1, 2)),
    "transpose": ((0,), (1, 2)),
    "activation": ((0,), (1, 3, 4)),
    "tensor_copy": ((0,), (1,)),
    "tensor_tensor": ((0,), (1, 2)),
    "tensor_scalar": ((0,), (1, 2, 3)),
    "tensor_single_scalar": ((0,), (1, 2)),
    "scalar_tensor_tensor": ((0,), (1, 2, 3)),
    "memset": ((0,), ()),
    "dma_start": ((), ()),
    "tensor_add": ((0,), (1, 2)),
    "tensor_sub": ((0,), (1, 2)),
    "tensor_mul": ((0,), (1, 2)),
}


def _ap_name(a):
    t = getattr(a, "tensor", None)
    if t is None:
        return None
    return getattr(t, "name", None)


class _Recorder:

    class _Ins:
        def then_inc(self, *a, **k):
            return self

    def __init__(self, engine, sink):
        self._engine = engine
        self._sink = sink

    def __getattr__(self, nm):
        def call(*args, **kwargs):
            roles = _ROLES.get(nm)
            reads, writes = set(), set()
            if roles is None:
                for a in list(args) + list(kwargs.values()):
                    n = _ap_name(a)
                    if n:
                        reads.add(n)
                        writes.add(n)
            else:
                for i, a in enumerate(args):
                    n = _ap_name(a)
                    if not n:
                        continue
                    if i in roles[0]:
                        writes.add(n)
                    else:
                        reads.add(n)
                for k, a in kwargs.items():
                    n = _ap_name(a)
                    if not n:
                        continue
                    if k == "out":
                        writes.add(n)
                    else:
                        reads.add(n)
            self._sink.append((nm, reads, writes, nm == "dma_start"))
            return _Recorder._Ins()
        return call


class Sched:

    def __init__(self):
        self.fns = []
        self.ops = []
        self.per_engine_n = {e: 0 for e in ENGINES}
        self.done = {e: 0 for e in ENGINES}
        self._pending_barrier = False

    def stage(self, deps, items):
        for e, fn in items:
            self.fns.append((e, fn, self._pending_barrier))
            self._pending_barrier = False
        return dict(self.done)

    def snap(self):
        return dict(self.done)

    def barrier(self):
        self._pending_barrier = True

    def schedule(self):
        recs = []
        for e, fn, bar in self.fns:
            sink = []
            fn(_Recorder(e, sink))
            recs.append(sink)
        ops = []
        last_write = {}
        readers = {}
        barrier_join = {}
        last_op_eng = {e: None for e in ENGINES}
        for (e, fn, bar), sink in zip(self.fns, recs):
            if bar:
                barrier_join = {k: v for k, v in last_op_eng.items()
                                if v is not None}
            for callidx, (nm, reads, writes, is_dma) in enumerate(sink):
                op = OpRec(e, len(ops), callidx, reads, writes, is_dma)
                deps = list(barrier_join.values())
                for r in reads:
                    o = last_write.get(r)
                    if o is not None:
                        deps.append(o)
                for w in writes:
                    o = last_write.get(w)
                    if o is not None:
                        deps.append(o)
                    deps.extend(readers.get(w, ()))
                op._deps = deps
                for w in writes:
                    last_write[w] = op
                    readers[w] = []
                for r in reads:
                    readers.setdefault(r, []).append(op)
                ops.append(op)
                last_op_eng[e] = op
        needs_inc = [False] * len(ops)
        for op in ops:
            for d in op._deps:
                if d.engine == op.engine == "tensor":
                    continue
                needs_inc[d.fnidx] = True
        inc_total = {e: 0 for e in ENGINES}
        inc_cum = [0] * len(ops)
        for i, op in enumerate(ops):
            if needs_inc[i] or op.is_dma:
                inc_total[op.engine] += 16 if op.is_dma else 1
                op.inc_id = inc_total[op.engine]
            inc_cum[i] = inc_total[op.engine]
        wm = {e: {e2: 0 for e2 in ENGINES} for e in ENGINES}
        dma_ckpt = {e: 0 for e in ENGINES}
        for op in ops:
            tgt = {}
            if op.is_dma and dma_ckpt[op.engine] > 0:
                tgt[op.engine] = dma_ckpt[op.engine]
            for d in op._deps:
                if d.engine == op.engine == "tensor":
                    continue
                v = d.inc_id if d.inc_id is not None else inc_cum[d.fnidx]
                if v and v > tgt.get(d.engine, 0):
                    tgt[d.engine] = v
            for pe, v in tgt.items():
                if v > wm[op.engine][pe]:
                    op.waits.append((pe, v))
                    wm[op.engine][pe] = v
            if op.is_dma and needs_inc[op.fnidx]:
                dma_ckpt[op.engine] = op.inc_id
        self.ops = ops
        self.inc_total = inc_total
        self.op_by_fn = {}
        idx = 0
        for fni, ((e, fn, bar), sink) in enumerate(zip(self.fns, recs)):
            self.op_by_fn[fni] = ops[idx: idx + len(sink)]
            idx += len(sink)


_SCHED_HOOK = None


def build_program(debug=False):
    nc = bass.Bass("TRN2", target_bir_lowering=False)

    d = {}
    for nm, shape in [
        ("x_t", [D, T]), ("xpre_t", [D, T]),
        ("wqT", [D, D]), ("wkT", [D, D]), ("wvT", [D, D]), ("woT", [D, D]),
        ("fmqT", [DK, DK]), ("fmkT", [DK, DK]),
        ("fmq_b", [DK, 1]), ("fmk_b", [DK, 1]), ("fmk_br", [1, DK]),
        ("rms_w", [DK, 1]),
        ("w1T", [D, DH]), ("b1", [128, 32]), ("w2T", [DH, D]),
        ("b2", [128, 8]),
        ("triu", [CH, CH]), ("ones", [128, 1]),
        ("wq_raw", [D, D]), ("wk_raw", [D, D]), ("ident", [128, 128]),
    ]:
        d[nm] = nc.dram_tensor(nm, shape, F32, kind="ExternalInput").ap()
    for nm in ("wqf", "wkf"):
        d[nm] = nc.dram_tensor(nm, [D, D], F32, kind="Internal").ap()
    d["maskT"] = nc.dram_tensor("maskT", [D, T], F8, kind="ExternalInput").ap()
    d["out_t"] = nc.dram_tensor("out_t", [D, T], F32, kind="ExternalOutput").ap()
    d["xb"] = nc.dram_tensor("xb", [D, T], F32,
                             kind="ExternalOutput" if debug else "Internal").ap()
    if debug:
        for nm, shape, dtp in [
            ("dbg_xh", [128, 1024], F32), ("dbg_q", [128, 1024], F32),
            ("dbg_qfa", [128, 1024], F32), ("dbg_kfa", [128, 1024], F32),
            ("dbg_kta", [128, 1024], F32), ("dbg_vt", [128, 1024], F32),
            ("dbg_sc", [128, 128], F32), ("dbg_on", [128, 1024], F32),
            ("dbg_Sa", [128, 1024], F32), ("dbg_Sb", [128, 1024], F32),
            ("dbg_fxh", [128, 4096], F32), ("dbg_x2q", [128, 4096], F32),
            ("dbg_fz", [128, 2048], F32), ("dbg_hq", [128, 4096], F32),
            ("dbg_fy", [128, 512], F32),
            ("dbg_opre", [128, 128], F32), ("dbg_osq", [128, 128], F32),
            ("dbg_stat", [1, 1024], F32),
            ("dbg_qfb", [128, 1024], F32), ("dbg_kfb", [128, 1024], F32),
            ("dbg_fmqb", [128, 2], F32),
            ("dbg_w1q", [128, 1024], F32), ("dbg_b1q", [128, 32], F32),
            ("dbg_w1qf", [128, 8 * DH], F8), ("dbg_w2qf", [128, 32 * D], F8),
            ("dbg_ffx", [128, 4096], F32), ("dbg_fstat", [1, 1024], F32),
            ("dbg_fstat2", [1, 1024], F32), ("dbg_fpsm", [128, 512], F32),
            ("dbg_fpsr", [128, 512], F32),
            ("dbg_b1t", [128, 32], F32), ("dbg_bf", [128, 32], F32),
            ("dbg_bw", [128, 32], F32), ("dbg_br", [128, 32], F32),
            ("dbg_bt", [128, 32], F32), ("dbg_bg", [128, 32], F32),
            ("dbg_bis", [128, 32], F32), ("dbg_bb1", [128, 32], F32),
            ("dbg_bb2", [128, 32], F32), ("dbg_bcomb", [128, 32], F32),
            ("dbg_sch", [128, 8 * 128], F32),
            ("dbg_oph", [128, 8 * 128], F32),
        ]:
            d[nm] = nc.dram_tensor(nm, shape, dtp,
                                   kind="ExternalOutput").ap()

    with contextlib.ExitStack() as st:
        def mksbuf(stack):
            def sbuf(name, shape, dt=F32):
                return stack.enter_context(nc.sbuf_tensor(name, shape, dt)).ap()
            return sbuf

        sbuf = mksbuf(st)

        def psum(name, shape):
            return st.enter_context(nc.psum_tensor(name, shape, F32)).ap()

        cb = sbuf("cb", [128, 8])
        CIDX = {"tiny": 0, "eps": 1, "c6": 2, "c14": 3}
        CVALS = {"tiny": 1e-37, "eps": EPS, "c6": 6.0, "c14": 14.0}

        def cba(nm, row128=True):
            i = CIDX[nm]
            return cb[:, i:i + 1] if row128 else cb[0:1, i:i + 1]

        fmq = sbuf("fmq", [128, DK])
        fmk = sbuf("fmk", [128, DK])
        fmqb = sbuf("fmqb", [128, 2])
        fmkb = sbuf("fmkb", [128, 2])
        fmkb_row = sbuf("fmkb_row", [1, DK])
        rmsw = sbuf("rmsw", [128, 1])
        b1t = sbuf("b1t", [128, 32])
        b2t = sbuf("b2t", [128, 8])
        b1q = sbuf("b1q", [128, 32])
        b2q = sbuf("b2q", [128, 8])
        triu = sbuf("triu_s", [128, CH])
        ones = sbuf("ones_s", [128, 1])
        ones_r = sbuf("ones_r", [1, 128])
        fmkb_bc = sbuf("fmkb_bc", [128, DK])
        S_a = sbuf("S_a", [128, H * DK])
        S_b = sbuf("S_b", [128, H * DK])
        stat = sbuf("stat", [1, 1024])
        stat2 = sbuf("stat2", [1, 1024])
        stat_i2 = sbuf("stat_i2", [1, 1024], I32)
        stat_i3 = sbuf("stat_i3", [1, 1024], I32)

        attn_stack = contextlib.ExitStack()
        asb = mksbuf(attn_stack)
        wbA = asb("wbA", [128, 8 * D])
        wbB = asb("wbB", [128, 8 * D])
        xc = asb("xc", [128, 8 * TILE])
        xh = asb("xh", [128, 8 * TILE])
        qp = asb("qp", [128, 8 * TILE])
        kp = asb("kp", [128, 8 * TILE])
        qfc_a = asb("qfc_a", [128, H * CH])
        qfc_b = asb("qfc_b", [128, H * CH])
        yt_w = qfc_a
        kfc_a = asb("kfc_a", [128, H * CH])
        kfc_b = asb("kfc_b", [128, H * CH])
        kt_a = asb("kt_a", [128, H * DK])
        kt_b = asb("kt_b", [128, H * DK])
        vt = asb("vt", [128, D])
        sc_w = asb("sc_w", [128, H * CH])
        o_n = asb("o_n", [128, H * TILE])
        idn = asb("idn", [128, 128])
        wpre = [asb("wpre0", [128, 128]), asb("wpre1", [128, 128])]
        wout = [asb("wout0", [128, 128]), asb("wout1", [128, 128])]
        ps_p = psum("ps_p", [128, 512])
        ps_q = psum("ps_q", [128, 512])
        ps_v = psum("ps_v", [128, 1024])
        ps_s = psum("ps_s", [128, 512])
        ps_o = psum("ps_o", [128, 512])
        ps_d = psum("ps_d", [128, 512])

        sems = {e: st.enter_context(nc.semaphore("sem_" + e)) for e in ENGINES}

        nc.const_aps.register(F32, -0.125, cb[:, 7:8])
        nc.const_aps.register(F32, -4 * LN2, cb[:, 6:7])

        sch = Sched()

        def stage(deps, items):
            return sch.stage(deps, items)

        items = [("vector", (lambda i, v: lambda e: e.memset(
            cb[:, i:i + 1], v))(CIDX[nm], CVALS[nm])) for nm in CIDX]
        items.append(("vector", lambda e: e.memset(cb[:, 7:8], -0.125)))
        items.append(("vector", lambda e: e.memset(cb[:, 6:7], -4 * LN2)))
        items.append(("vector", lambda e: e.memset(ones_r, 1.0)))
        items.append(("vector", lambda e: e.memset(S_a, 0.0)))
        items.append(("vector", lambda e: e.memset(S_b, 0.0)))
        loads = [
            (fmq, d["fmqT"]), (fmk, d["fmkT"]),
            (triu, d["triu"]), (ones, d["ones"]),
            (rmsw, d["rms_w"]),
            (fmqb[:, 0:1], d["fmq_b"]),
            (fmkb[:, 0:1], d["fmk_b"]),
            (fmkb_row, d["fmk_br"]),
            (b1t, d["b1"]),
            (b2t, d["b2"]),
            (idn, d["ident"]),
            (wbB.rearrange("p (c n) -> p c n", c=8),
             d["wvT"].rearrange("(c p) n -> p c n", p=128)),
        ]
        for dst, src in loads:
            items.append(("sync", (lambda dst, src: lambda e: e.dma_start(
                out=dst, in_=src))(dst, src)))
        prev = stage(sch.snap(), items)

        prev = stage(prev, [("scalar", lambda e: e.activation(
            fmqb[:, 1:2], fmqb[:, 0:1], AF.Copy, scale=-1.0,
            bias=-4 * LN2))])
        prev = stage(prev, [("scalar", lambda e: e.activation(
            fmqb[:, 0:1], fmqb[:, 0:1], AF.Copy, scale=1.0,
            bias=-4 * LN2))])
        prev = stage(prev, [("scalar", lambda e: e.activation(
            fmkb[:, 1:2], fmkb[:, 0:1], AF.Copy, scale=-1.0))])
        sch.barrier()
        prev = stage(prev, [("tensor", lambda e: e.matmul(
            ps_v[:, 0:DK], ones_r, fmkb_row, start=True, stop=True))])
        prev = stage(prev, [("scalar", lambda e: e.activation(
            fmkb_bc, ps_v[:, 0:DK], AF.Copy))])

        for raw, fmw, dst in ((d["wq_raw"], fmq, d["wqf"]),
                              (d["wk_raw"], fmk, d["wkf"])):
            for h in range(H):
                for jc in range(8):
                    pi = (h * 8 + jc) % 2
                    wp, wo_, psx = wpre[pi], wout[pi], (ps_s if pi == 0
                                                        else ps_d)

                    def mk_ld(e, raw=raw, h=h, jc=jc, wp=wp):
                        return e.dma_start(
                            out=wp, in_=raw[h * DK:(h + 1) * DK,
                                            jc * 128:(jc + 1) * 128])
                    prev = stage(prev, [("sync", mk_ld)])

                    def mk_mm(e, wp=wp, fmw=fmw, psx=psx):
                        return e.matmul(psx[:, 0:128], wp, fmw,
                                        start=True, stop=True)
                    prev = stage(prev, [("tensor", mk_mm)])

                    def mk_ev(e, wo_=wo_, psx=psx):
                        return e.activation(wo_, psx[:, 0:128], AF.Copy)
                    prev = stage(prev, [("scalar", mk_ev)])

                    def mk_st(e, dst=dst, h=h, jc=jc, wo_=wo_):
                        return e.dma_start(
                            out=dst[jc * 128:(jc + 1) * 128,
                                    h * DK:(h + 1) * DK], in_=wo_)
                    prev = stage(prev, [("gpsimd", mk_st)])
        def mk_wkf(e):
            return e.dma_start(
                out=wbA.rearrange("p (c n) -> p c n", c=8),
                in_=d["wkf"].rearrange("(c p) n -> p c n", p=128))
        prev = stage(prev, [("sync", mk_wkf)])

        def wslice(wt, c, dd):
            return wt[:, c * D + dd * 128: c * D + (dd + 1) * 128]

        def dump(prev, name, src_tile, width):
            if not debug:
                return prev
            def mk(e, name=name, src_tile=src_tile, width=width):
                return e.dma_start(out=d[name][:, 0:width],
                                   in_=src_tile[:, 0:width])
            return stage(prev, [("gpsimd", mk)])

        def q_ops(z, out_q, M, clamp, signed, ties, width):
            ops = []
            onv = lambda f: ops.append(("vector", f))
            ons = lambda f: ops.append(("scalar", f))
            l, f, w = QSA[:, :width], QSB[:, :width], QSC[:, :width]
            tt, r = QSE[:, :width], QSF[:, :width]
            bs = QSG[:, :width]
            isx, sx = QSD[:, :width], QSH[:, :width]
            gE, g = QIA[:, :width], QIB[:, :width]
            bEi, p = QIC[:, :width], QIA[:, :width]
            tie, odd = QSG[:, :width], QSD[:, :width]
            if signed:
                ons(lambda e: e.activation(tt, z, AF.Square))
                ons(lambda e: e.activation(l, tt, AF.Ln, bias=cba("tiny")))
                c, inv = 1.0 / (2 * LN2), 2 * LN2
            else:
                ons(lambda e: e.activation(l, z, AF.Ln, bias=cba("tiny")))
                c, inv = 1.0 / LN2, LN2
            ons((lambda c: lambda e: e.activation(gE, l, AF.Copy, scale=c))(c))
            onv((lambda inv: lambda e: e.scalar_tensor_tensor(
                bEi, gE, inv, l, ALU.mult, ALU.is_gt))(inv))
            onv(lambda e: e.scalar_tensor_tensor(
                f, bEi, -1.0, gE, ALU.mult, ALU.add))
            ons((lambda cl: lambda e: e.activation(
                w, f, AF.Relu, bias=cba("c%d" % cl)))(clamp))
            if ties:
                onv((lambda MF: lambda e: e.tensor_scalar(
                    gE, w, -M23, MF * M23, ALU.mult, ALU.add))(
                        float(127 + M + clamp)))
                onv(lambda e: e.tensor_tensor(tt, z, gE.bitcast(F32), ALU.mult))
            else:
                ons(lambda e: e.activation(isx, w, AF.Exp, scale=-LN2))
                onv((lambda s_: lambda e: e.scalar_tensor_tensor(
                    tt, z, s_, isx, ALU.mult, ALU.mult))(
                        float(2.0 ** (M + clamp))))
            if signed:
                onv(lambda e: e.tensor_copy(g, tt))
                onv(lambda e: e.scalar_tensor_tensor(
                    r, g, 0.5, tt, ALU.add, ALU.is_le))
                onv(lambda e: e.scalar_tensor_tensor(
                    bs, g, -0.5, tt, ALU.add, ALU.is_gt))
                onv(lambda e: e.scalar_tensor_tensor(
                    r, bs, -1.0, r, ALU.mult, ALU.add))
                onv(lambda e: e.tensor_tensor(r, r, g, ALU.add))
            else:
                ons(lambda e: e.activation(g, tt, AF.Copy, bias=-0.125))
                onv(lambda e: e.scalar_tensor_tensor(
                    r, g, 0.5, tt, ALU.add, ALU.is_le))
                onv(lambda e: e.tensor_tensor(r, r, g, ALU.add))
            if ties:
                onv(lambda e: e.scalar_tensor_tensor(
                    tie, r, -0.5, tt, ALU.add, ALU.is_equal))
                onv(lambda e: e.tensor_scalar(odd, r, 0.5, None, ALU.mult))
                onv(lambda e: e.tensor_copy(p, odd))
                onv(lambda e: e.tensor_tensor(odd, odd, p, ALU.not_equal))
                onv(lambda e: e.tensor_tensor(tie, tie, odd, ALU.mult))
                onv(lambda e: e.tensor_tensor(r, r, tie, ALU.subtract))
                onv((lambda MF: lambda e: e.tensor_scalar(
                    gE, w, M23, MF * M23, ALU.mult, ALU.add))(
                        float(127 - M - clamp)))
                onv(lambda e: e.tensor_tensor(
                    out_q, r, gE.bitcast(F32), ALU.mult))
            else:
                ons(lambda e: e.activation(sx, w, AF.Exp, scale=LN2))
                onv((lambda s_: lambda e: e.scalar_tensor_tensor(
                    out_q, r, s_, sx, ALU.mult, ALU.mult))(
                        float(2.0 ** (-(M + clamp)))))
            return ops

        def q_h(z, out_q, width):
            ops = []
            onv = lambda f: ops.append(("vector", f))
            ons = lambda f: ops.append(("scalar", f))
            l, f = QSA[:, :width], QSB[:, :width]
            w14, w6 = QSC[:, :width], QSD[:, :width]
            tt, r510 = QSF[:, :width], QSE[:, :width]
            r43 = QSG[:, :width]
            isx, sx = QSH[:, :width], QSH[:, :width]
            gE, g = QIA[:, :width], QIB[:, :width]
            bEi, p = QIC[:, :width], QIA[:, :width]
            tie, odd = QSA[:, :width], QSB[:, :width]
            ons(lambda e: e.activation(l, z, AF.Ln, bias=cba("tiny")))
            ons(lambda e: e.activation(gE, l, AF.Copy, scale=1.0 / LN2))
            onv(lambda e: e.scalar_tensor_tensor(
                bEi, gE, LN2, l, ALU.mult, ALU.is_gt))
            onv(lambda e: e.scalar_tensor_tensor(
                f, bEi, -1.0, gE, ALU.mult, ALU.add))
            ons(lambda e: e.activation(w14, f, AF.Relu, bias=cba("c14")))
            ons(lambda e: e.activation(w6, f, AF.Relu, bias=cba("c6")))
            ons(lambda e: e.activation(isx, w14, AF.Exp, scale=-LN2))
            onv(lambda e: e.scalar_tensor_tensor(
                tt, z, float(2.0 ** 24), isx, ALU.mult, ALU.mult))
            ons(lambda e: e.activation(g, tt, AF.Copy, bias=-0.125))
            onv(lambda e: e.scalar_tensor_tensor(
                r510, g, 0.5, tt, ALU.add, ALU.is_le))
            onv(lambda e: e.tensor_tensor(r510, r510, g, ALU.add))
            onv(lambda e: e.tensor_tensor(w14, w14, w6, ALU.subtract))
            onv(lambda e: e.tensor_scalar(
                gE, w14, M23, (127.0 - 15.0) * M23, ALU.mult, ALU.add))
            onv(lambda e: e.tensor_tensor(tt, r510, gE.bitcast(F32), ALU.mult))
            ons(lambda e: e.activation(g, tt, AF.Copy, bias=-0.125))
            onv(lambda e: e.scalar_tensor_tensor(
                r43, g, 0.5, tt, ALU.add, ALU.is_le))
            onv(lambda e: e.tensor_tensor(r43, r43, g, ALU.add))
            onv(lambda e: e.scalar_tensor_tensor(
                tie, r43, -0.5, tt, ALU.add, ALU.is_equal))
            onv(lambda e: e.tensor_scalar(odd, r43, 0.5, None, ALU.mult))
            onv(lambda e: e.tensor_copy(p, odd))
            onv(lambda e: e.tensor_tensor(odd, odd, p, ALU.not_equal))
            onv(lambda e: e.tensor_tensor(tie, tie, odd, ALU.mult))
            onv(lambda e: e.tensor_tensor(r43, r43, tie, ALU.subtract))
            ons(lambda e: e.activation(sx, w6, AF.Exp, scale=LN2))
            onv(lambda e: e.scalar_tensor_tensor(
                out_q, r43, float(2.0 ** -9), sx, ALU.mult, ALU.mult))
            return ops

        def run_serial(prev, ops):
            for eng, f in ops:
                prev = stage(prev, [(eng, f)])
            return prev

        def emit_ln(prev, src_ap, t0, width, x_tile, sq_tile, out_tile,
                    do_load=True, dbg_ffn=False):
            x3 = x_tile.rearrange("p (c t) -> p c t", c=8)
            sq3 = sq_tile.rearrange("p (c t) -> p c t", c=8)
            o3 = out_tile.rearrange("p (c t) -> p c t", c=8)
            if do_load:
                prev = stage(prev, [("sync", (lambda: lambda e: e.dma_start(
                    out=x3, in_=src_ap.rearrange("(c p) t -> p c t", p=128)
                    [:, :, t0:t0 + width]))())])
            prev = stage(prev, [("scalar", lambda e: e.activation(
                sq_tile, x_tile, AF.Square))])

            def mk_stats(e):
                for c in range(8):
                    e.matmul(ps_s[0:1, 0:width], ones, x3[:, c, :],
                             start=(c == 0), stop=(c == 7))
                for c in range(8):
                    ins = e.matmul(ps_d[0:1, 0:width], ones, sq3[:, c, :],
                                   start=(c == 0), stop=(c == 7))
                return ins
            prev = stage(prev, [("tensor", mk_stats)])

            def mk_sv1(e):
                e.tensor_scalar(stat[0:1, 0:width], ps_s[0:1, 0:width],
                                1.0 / D, None, ALU.mult)
                return e.tensor_scalar(stat2[0:1, 0:width],
                                       ps_d[0:1, 0:width],
                                       1.0 / D, None, ALU.mult)
            prev = stage(prev, [("vector", mk_sv1)])

            def mk_sv2(e):
                return e.tensor_tensor(stat[0:1, width:2 * width],
                                       stat[0:1, 0:width],
                                       stat[0:1, 0:width], ALU.mult)
            prev = stage(prev, [("vector", mk_sv2)])

            def mk_sv3(e):
                return e.tensor_tensor(stat2[0:1, 0:width],
                                       stat2[0:1, 0:width],
                                       stat[0:1, width:2 * width],
                                       ALU.subtract)
            prev = stage(prev, [("vector", mk_sv3)])

            def mk_ss(e):
                e.activation(stat2[0:1, width:2 * width],
                             stat2[0:1, 0:width], AF.Ln,
                             bias=cba("eps", False))
                return e.activation(stat2[0:1, 0:width],
                                    stat2[0:1, width:2 * width],
                                    AF.Exp, scale=-0.5)
            prev = stage(prev, [("scalar", mk_ss)])

            def mk_bc(e):
                e.matmul(ps_v[:, 0:width], ones_r, stat[0:1, 0:width],
                         start=True, stop=True)
                return e.matmul(ps_v[:, 512:512 + width], ones_r,
                                stat2[0:1, 0:width], start=True, stop=True)
            prev = stage(prev, [("tensor", mk_bc)])
            m_b = ps_v[:, 0:width].unsqueeze(1).broadcast_to([128, 8, width])
            r_b = ps_v[:, 512:512 + width].unsqueeze(1)\
                .broadcast_to([128, 8, width])

            if debug and dbg_ffn:
                def mkd1(e):
                    e.dma_start(out=d["dbg_ffx"][:, 0:8 * width],
                                in_=x_tile[:, 0:8 * width])
                    e.dma_start(out=d["dbg_fstat"][0:1, 0:width],
                                in_=stat[0:1, 0:width])
                    return e.dma_start(out=d["dbg_fstat2"][0:1, 0:width],
                                       in_=stat2[0:1, 0:width])
                prev = stage(prev, [("gpsimd", mkd1)])

            def mk_ap(e):
                e.tensor_tensor(sq3, x3, m_b, ALU.subtract)
                return e.tensor_tensor(o3, sq3, r_b, ALU.mult)
            prev = stage(prev, [("vector", mk_ap)])
            return prev

        def emit_wload(prev, wsrc, wbuf):
            def mk(e):
                return e.dma_start(
                    out=wbuf.rearrange("p (c n) -> p c n", c=8),
                    in_=wsrc.rearrange("(c p) n -> p c n", p=128))
            return stage(prev, [("sync", mk)])

        def emit_proj_t(prev, wbuf, out_sb):
            for dd in range(8):
                ps = ps_p if dd % 2 == 0 else ps_q

                def mk_mm(e, dd=dd, ps=ps):
                    for c in range(8):
                        ins = e.matmul(ps[:, 0:TILE], wslice(wbuf, c, dd),
                                       xh[:, c * TILE:(c + 1) * TILE],
                                       start=(c == 0), stop=(c == 7))
                    return ins
                prev = stage(prev, [("tensor", mk_mm)])

                def mk_ev(e, dd=dd, ps=ps):
                    return e.activation(out_sb[:, dd * TILE:(dd + 1) * TILE],
                                        ps[:, 0:TILE], AF.Copy)
                prev = stage(prev, [("scalar", mk_ev)])
            return prev

        qp4 = qp.rearrange("p (h t) -> p h t", h=H)
        kp4 = kp.rearrange("p (h t) -> p h t", h=H)
        qfc_a4 = qfc_a.rearrange("p (h t) -> p h t", h=H)
        qfc_b4 = qfc_b.rearrange("p (h t) -> p h t", h=H)
        kfc_a4 = kfc_a.rearrange("p (h t) -> p h t", h=H)
        kfc_b4 = kfc_b.rearrange("p (h t) -> p h t", h=H)
        scw4 = sc_w.rearrange("p (h t) -> p h t", h=H)
        on4 = o_n.rearrange("p (h t) -> p h t", h=H)
        kb_b = fmkb_bc.unsqueeze(1).broadcast_to([128, 4, DK])

        def emit_qk_chunk(prev, cix, with_scores):
            c0 = cix * CH
            if with_scores:
                def mk_qf(e, c0=c0):
                    e.activation(qfc_a4, qp4[:, :, c0:c0 + CH], AF.Exp,
                                 bias=fmqb[:, 0:1])
                    e.activation(qfc_b4, qp4[:, :, c0:c0 + CH], AF.Exp,
                                 scale=-1.0, bias=fmqb[:, 1:2])
                    e.activation(kfc_a4, kp4[:, :, c0:c0 + CH], AF.Exp,
                                 bias=fmkb[:, 0:1])
                    return e.activation(kfc_b4, kp4[:, :, c0:c0 + CH],
                                        AF.Exp, scale=-1.0,
                                        bias=fmkb[:, 1:2])
                prev = stage(prev, [("scalar", mk_qf)])

            def mk_tr(e, c0=c0):
                for h in range(H):
                    ps = ps_s if h < 4 else ps_d
                    off = (h % 4) * DK
                    ins = e.transpose(ps[:, off:off + DK],
                                      kp[:, h * TILE + c0:h * TILE + c0 + CH],
                                      idn)
                return ins
            prev = stage(prev, [("tensor", mk_tr)])

            def mk_ktb(e):
                e.tensor_tensor(scw4[:, 0:4, :],
                                ps_s.rearrange("p (h t) -> p h t", h=4),
                                kb_b, ALU.add)
                return e.tensor_tensor(scw4[:, 4:8, :],
                                       ps_d.rearrange("p (h t) -> p h t", h=4),
                                       kb_b, ALU.add)
            prev = stage(prev, [("vector", mk_ktb)])

            def mk_kte(e):
                e.activation(kt_a, sc_w, AF.Exp)
                return e.activation(kt_b, sc_w, AF.Exp, scale=-1.0)
            prev = stage(prev, [("scalar", mk_kte)])
            return prev

        def emit_v_chunk(prev, wvbuf, cix):
            c0 = cix * CH

            def mk_vm(e, c0=c0):
                wv3 = wvbuf.rearrange("p (c n) -> p c n", c=8)
                for half in range(2):
                    for c in range(8):
                        ins = e.matmul(
                            ps_v[:, half * 512:(half + 1) * 512],
                            xh[:, c * TILE + c0:c * TILE + c0 + CH],
                            wv3[:, c, half * 512:(half + 1) * 512],
                            start=(c == 0), stop=(c == 7))
                return ins
            prev = stage(prev, [("tensor", mk_vm)])

            def mk_vev(e):
                e.activation(vt[:, 0:512], ps_v[:, 0:512], AF.Copy)
                return e.activation(vt[:, 512:1024], ps_v[:, 512:1024],
                                    AF.Copy)
            prev = stage(prev, [("scalar", mk_vev)])
            return prev

        tri_b = triu.unsqueeze(1).broadcast_to([128, 4, CH])

        def emit_scan_chunk(prev, cix, with_output):
            c0 = cix * CH
            if with_output:
                def mk_sc(e):
                    for h in range(H):
                        ps = ps_s if h < 4 else ps_p
                        off = (h % 4) * CH
                        e.matmul(ps[:, off:off + CH],
                                 kfc_a[:, h * CH:(h + 1) * CH],
                                 qfc_a[:, h * CH:(h + 1) * CH],
                                 start=True, stop=False)
                        ins = e.matmul(ps[:, off:off + CH],
                                       kfc_b[:, h * CH:(h + 1) * CH],
                                       qfc_b[:, h * CH:(h + 1) * CH],
                                       start=False, stop=True)
                    return ins
                prev = stage(prev, [("tensor", mk_sc)])

                def mk_mask(e):
                    e.tensor_tensor(
                        scw4[:, 0:4, :],
                        ps_s.rearrange("p (h t) -> p h t", h=4), tri_b,
                        ALU.mult)
                    return e.tensor_tensor(
                        scw4[:, 4:8, :],
                        ps_p.rearrange("p (h t) -> p h t", h=4), tri_b,
                        ALU.mult)
                prev = stage(prev, [("vector", mk_mask)])

                def mk_o(e):
                    for h in range(H):
                        ps = ps_o if h < 4 else ps_q
                        off = (h % 4) * CH
                        e.matmul(ps[:, off:off + CH],
                                 vt[:, h * DK:(h + 1) * DK],
                                 sc_w[:, h * CH:(h + 1) * CH],
                                 start=True, stop=False)
                        e.matmul(ps[:, off:off + CH],
                                 S_a[:, h * DK:(h + 1) * DK],
                                 qfc_a[:, h * CH:(h + 1) * CH],
                                 start=False, stop=False)
                        ins = e.matmul(ps[:, off:off + CH],
                                       S_b[:, h * DK:(h + 1) * DK],
                                       qfc_b[:, h * CH:(h + 1) * CH],
                                       start=False, stop=True)
                    return ins
                prev = stage(prev, [("tensor", mk_o)])

                def mk_oev2(e, c0=c0):
                    e.activation(on4[:, 0:4, c0:c0 + CH], ps_o.rearrange(
                        "p (h t) -> p h t", h=4), AF.Copy)
                    e.activation(on4[:, 4:8, c0:c0 + CH], ps_q.rearrange(
                        "p (h t) -> p h t", h=4), AF.Copy)
                    e.activation(yt_w[:, 0:512], ps_o[:, 0:512], AF.Square)
                    return e.activation(yt_w[:, 512:1024], ps_q[:, 0:512],
                                        AF.Square)
                prev = stage(prev, [("scalar", mk_oev2)])

                def mk_osum(e):
                    e.matmul(ps_d[0:1, 0:512], ones, yt_w[:, 0:512],
                             start=True, stop=True)
                    return e.matmul(ps_v[0:1, 0:512], ones, yt_w[:, 512:1024],
                                    start=True, stop=True)
                prev = stage(prev, [("tensor", mk_osum)])

                def mk_r1(e):
                    e.tensor_scalar(stat[0:1, 0:512], ps_d[0:1, 0:512],
                                    1.0 / DK, EPS, ALU.mult, ALU.add)
                    return e.tensor_scalar(stat[0:1, 512:1024],
                                           ps_v[0:1, 0:512],
                                           1.0 / DK, EPS, ALU.mult, ALU.add)
                prev = stage(prev, [("vector", mk_r1)])

                def mk_r2(e):
                    return e.tensor_scalar(
                        stat_i2[0:1, 0:1024], stat[0:1, 0:1024].bitcast(I32),
                        float(2.0 ** -23), -127.0, ALU.mult, ALU.add)
                prev = stage(prev, [("vector", mk_r2)])

                def mk_r3(e):
                    return e.tensor_scalar(stat_i3[0:1, 0:1024],
                                           stat_i2[0:1, 0:1024], -M23,
                                           127.0 * M23, ALU.mult, ALU.add)
                prev = stage(prev, [("vector", mk_r3)])

                def mk_r4(e):
                    return e.tensor_tensor(
                        stat2[0:1, 0:1024], stat[0:1, 0:1024],
                        stat_i3[0:1, 0:1024].bitcast(F32), ALU.mult)
                prev = stage(prev, [("vector", mk_r4)])

                def mk_r5(e):
                    return e.activation(stat2[0:1, 0:1024],
                                        stat2[0:1, 0:1024], AF.Ln)
                prev = stage(prev, [("scalar", mk_r5)])

                def mk_r6(e):
                    return e.scalar_tensor_tensor(
                        stat[0:1, 0:1024], stat_i2[0:1, 0:1024], LN2,
                        stat2[0:1, 0:1024], ALU.mult, ALU.add)
                prev = stage(prev, [("vector", mk_r6)])

                def mk_r7(e):
                    return e.activation(stat[0:1, 0:1024], stat[0:1, 0:1024],
                                        AF.Exp, scale=-0.5)
                prev = stage(prev, [("scalar", mk_r7)])

                def mk_rb(e):
                    e.matmul(ps_o[:, 0:512], ones_r, stat[0:1, 0:512],
                             start=True, stop=True)
                    return e.matmul(ps_q[:, 0:512], ones_r,
                                    stat[0:1, 512:1024], start=True,
                                    stop=True)
                prev = stage(prev, [("tensor", mk_rb)])

                def mk_oap(e, c0=c0):
                    e.scalar_tensor_tensor(
                        on4[:, 0:4, c0:c0 + CH], on4[:, 0:4, c0:c0 + CH],
                        rmsw[:, 0:1],
                        ps_o.rearrange("p (h t) -> p h t", h=4),
                        ALU.mult, ALU.mult)
                    return e.scalar_tensor_tensor(
                        on4[:, 4:8, c0:c0 + CH], on4[:, 4:8, c0:c0 + CH],
                        rmsw[:, 0:1],
                        ps_q.rearrange("p (h t) -> p h t", h=4),
                        ALU.mult, ALU.mult)
                prev = stage(prev, [("vector", mk_oap)])

            def mk_ds(e):
                for h in range(H):
                    ps = ps_s if h < 4 else ps_p
                    off = (h % 4) * CH
                    ins = e.matmul(ps[:, off:off + CH],
                                   kt_a[:, h * DK:(h + 1) * DK],
                                   vt[:, h * DK:(h + 1) * DK],
                                   start=True, stop=True)
                for h in range(H):
                    ps = ps_o if h < 4 else ps_q
                    off = (h % 4) * CH
                    ins = e.matmul(ps[:, off:off + CH],
                                   kt_b[:, h * DK:(h + 1) * DK],
                                   vt[:, h * DK:(h + 1) * DK],
                                   start=True, stop=True)
                return ins
            prev = stage(prev, [("tensor", mk_ds)])

            def mk_sadd(e):
                e.tensor_tensor(S_a[:, 0:512], S_a[:, 0:512], ps_s[:, 0:512],
                                ALU.add)
                e.tensor_tensor(S_a[:, 512:1024], S_a[:, 512:1024],
                                ps_p[:, 0:512], ALU.add)
                e.tensor_tensor(S_b[:, 0:512], S_b[:, 0:512], ps_o[:, 0:512],
                                ALU.add)
                return e.tensor_tensor(S_b[:, 512:1024], S_b[:, 512:1024],
                                       ps_q[:, 0:512], ALU.add)
            prev = stage(prev, [("vector", mk_sadd)])
            return prev

        NCHT = TILE // CH
        for ti in range(NTL):
            t0 = ti * TILE

            def mk_ldx(e, t0=t0):
                return e.dma_start(
                    out=xc.rearrange("p (c t) -> p c t", c=8),
                    in_=d["xpre_t"].rearrange("(c p) t -> p c t", p=128)
                    [:, :, t0:t0 + TILE])
            prev = stage(prev, [("sync", mk_ldx)])
            prev = emit_ln(prev, None, t0, TILE, xc, o_n, xh, do_load=False)
            prev = emit_proj_t(prev, wbA, kp)
            for cix in range(NCHT):
                prev = emit_qk_chunk(prev, cix, with_scores=False)
                prev = emit_v_chunk(prev, wbB, cix)
                prev = emit_scan_chunk(prev, cix, with_output=False)
        prev = dump(prev, "dbg_Sa", S_a, 1024)
        prev = dump(prev, "dbg_Sb", S_b, 1024)

        for ti in range(NTL):
            t0 = ti * TILE

            def mk_ldx(e, t0=t0):
                return e.dma_start(
                    out=xc.rearrange("p (c t) -> p c t", c=8),
                    in_=d["x_t"].rearrange("(c p) t -> p c t", p=128)
                    [:, :, t0:t0 + TILE])
            prev = stage(prev, [("sync", mk_ldx)])
            prev = emit_ln(prev, None, t0, TILE, xc, o_n, xh, do_load=False)
            prev = emit_wload(prev, d["wqf"], wbA)
            prev = emit_proj_t(prev, wbA, qp)
            prev = emit_wload(prev, d["wkf"], wbB)
            prev = emit_proj_t(prev, wbB, kp)
            prev = emit_wload(prev, d["wvT"], wbA)
            for cix in range(NCHT):
                prev = emit_qk_chunk(prev, cix, with_scores=True)
                prev = emit_v_chunk(prev, wbA, cix)
                prev = emit_scan_chunk(prev, cix, with_output=True)
            prev = emit_wload(prev, d["woT"], wbB)
            for dd in range(8):
                ps = ps_o if dd % 2 == 0 else ps_q

                def mk_wo(e, dd=dd, ps=ps):
                    for h in range(H):
                        ins = e.matmul(
                            ps[:, 0:TILE], wslice(wbB, h, dd),
                            o_n[:, h * TILE:(h + 1) * TILE],
                            start=(h == 0), stop=(h == 7))
                    return ins
                prev = stage(prev, [("tensor", mk_wo)])

                def mk_res(e, dd=dd, ps=ps):
                    return e.tensor_tensor(
                        xc[:, dd * TILE:(dd + 1) * TILE],
                        xc[:, dd * TILE:(dd + 1) * TILE],
                        ps[:, 0:TILE], ALU.add)
                prev = stage(prev, [("vector", mk_res)])

            def mk_st(e, t0=t0):
                return e.dma_start(
                    out=d["xb"].rearrange("(c p) t -> p c t", p=128)
                    [:, :, t0:t0 + TILE],
                    in_=xc.rearrange("p (c t) -> p c t", c=8))
            prev = stage(prev, [("gpsimd", mk_st)])

        sch.barrier()
        attn_stack.close()
        ffn_stack = contextlib.ExitStack()
        fsb = mksbuf(ffn_stack)
        QW = 1024
        QSA = fsb("QSA", [128, QW]); QSB = fsb("QSB", [128, QW])
        QSC = fsb("QSC", [128, QW]); QSD = fsb("QSD", [128, QW])
        QSE = fsb("QSE", [128, QW]); QSF = fsb("QSF", [128, QW])
        QSG = fsb("QSG", [128, QW]); QSH = fsb("QSH", [128, QW])
        QIA = fsb("QIA", [128, QW], I32)
        QIB = fsb("QIB", [128, QW], I32)
        QIC = fsb("QIC", [128, QW], I32)
        fx = fsb("fx", [128, 8 * FT])
        fxh = fsb("fxh", [128, 8 * FT])
        fz = fsb("fz", [128, 1024])
        fy = fsb("fy", [128, FT])
        x2q = fsb("x2q", [128, 8 * FT], F8)
        hq = fsb("hq", [128, 32 * FT], F8)
        w1q = fsb("w1q", [128, 8 * DH], F8)
        w2q = fsb("w2q", [128, 32 * D], F8)
        msk8 = fsb("msk8", [128, 8 * FT], F8)
        wload = fsb("wload", [128, 1024])

        def q_weight(prev, srcT, nchunk, cols, dst_fp8):
            src3 = srcT.rearrange("(c p) n -> p c n", p=128)
            step = 1024
            for c in range(nchunk):
                for s0 in range(0, cols, step):
                    wdt = min(step, cols - s0)

                    def mk_ld(e, c=c, s0=s0, wdt=wdt):
                        e.dma_start(out=wload[:, 0:wdt],
                                    in_=src3[:, c, s0:s0 + wdt])
                    prev = stage(prev, [("sync", mk_ld)])
                    prev = run_serial(prev, q_ops(
                        wload[:, 0:wdt],
                        dst_fp8[:, c * cols + s0:c * cols + s0 + wdt],
                        M=3, clamp=6, signed=True, ties=True, width=wdt))
            return prev

        prev = q_weight(prev, d["w1T"], 8, DH, w1q)
        prev = q_weight(prev, d["w2T"], 32, D, w2q)
        if debug:
            prev = stage(prev, [("gpsimd", lambda e: e.dma_start(
                out=d["dbg_w1qf"], in_=w1q))])
            prev = stage(prev, [("gpsimd", lambda e: e.dma_start(
                out=d["dbg_w2qf"], in_=w2q))])
        prev = dump(prev, "dbg_b1t", b1t, 32)
        if debug:
            W_ = 32
            z_, out_ = b1t, b1q
            l_, f_, w_ = QSA[:, :W_], QSB[:, :W_], QSC[:, :W_]
            tt_, r_ = QSE[:, :W_], QSF[:, :W_]
            bs_ = QSG[:, :W_]
            gE_, g_ = QIA[:, :W_], QIB[:, :W_]
            bEi_, p_ = QIC[:, :W_], QIA[:, :W_]
            tie_, odd_ = QSG[:, :W_], QSD[:, :W_]
            prev = stage(prev, [("scalar", lambda e: e.activation(tt_, z_, AF.Square))])
            prev = stage(prev, [("scalar", lambda e: e.activation(l_, tt_, AF.Ln, bias=cba("tiny")))])
            prev = stage(prev, [("scalar", lambda e: e.activation(gE_, l_, AF.Copy, scale=1.0 / (2 * LN2)))])
            prev = stage(prev, [("vector", lambda e: e.scalar_tensor_tensor(bEi_, gE_, 2 * LN2, l_, ALU.mult, ALU.is_gt))])
            prev = stage(prev, [("vector", lambda e: e.scalar_tensor_tensor(f_, bEi_, -1.0, gE_, ALU.mult, ALU.add))])
            prev = stage(prev, [("scalar", lambda e: e.activation(w_, f_, AF.Relu, bias=cba("c6")))])
            prev = stage(prev, [("vector", lambda e: e.tensor_scalar(gE_, w_, -M23, 136.0 * M23, ALU.mult, ALU.add))])
            def mkbis(e):
                return e.tensor_copy(QSD[:, 0:W_], gE_)
            prev = stage(prev, [("vector", mkbis)])
            prev = dump(prev, "dbg_bis", QSD, 32)
            prev = stage(prev, [("vector", lambda e: e.tensor_tensor(tt_, z_, gE_.bitcast(F32), ALU.mult))])
            prev = stage(prev, [("vector", lambda e: e.tensor_copy(g_, tt_))])
            prev = stage(prev, [("vector", lambda e: e.scalar_tensor_tensor(r_, g_, 0.5, tt_, ALU.add, ALU.is_le))])
            prev = dump(prev, "dbg_bb1", QSF, 32)
            prev = stage(prev, [("vector", lambda e: e.scalar_tensor_tensor(bs_, g_, -0.5, tt_, ALU.add, ALU.is_gt))])
            prev = dump(prev, "dbg_bb2", QSG, 32)
            prev = stage(prev, [("vector", lambda e: e.scalar_tensor_tensor(r_, bs_, -1.0, r_, ALU.mult, ALU.add))])
            prev = dump(prev, "dbg_bcomb", QSF, 32)
            prev = stage(prev, [("vector", lambda e: e.tensor_tensor(r_, r_, g_, ALU.add))])
            prev = stage(prev, [("vector", lambda e: e.scalar_tensor_tensor(tie_, r_, -0.5, tt_, ALU.add, ALU.is_equal))])
            prev = stage(prev, [("vector", lambda e: e.tensor_scalar(odd_, r_, 0.5, None, ALU.mult))])
            prev = stage(prev, [("vector", lambda e: e.tensor_copy(p_, odd_))])
            prev = stage(prev, [("vector", lambda e: e.tensor_tensor(odd_, odd_, p_, ALU.not_equal))])
            prev = stage(prev, [("vector", lambda e: e.tensor_tensor(tie_, tie_, odd_, ALU.mult))])
            prev = stage(prev, [("vector", lambda e: e.tensor_tensor(r_, r_, tie_, ALU.subtract))])
            prev = stage(prev, [("vector", lambda e: e.tensor_scalar(gE_, w_, M23, 118.0 * M23, ALU.mult, ALU.add))])
            prev = stage(prev, [("vector", lambda e: e.tensor_tensor(out_, r_, gE_.bitcast(F32), ALU.mult))])
        else:
            prev = run_serial(prev, q_ops(b1t, b1q, M=3, clamp=6, signed=True,
                                          ties=True, width=32))
        if debug:
            prev = dump(prev, "dbg_bf", QSB, 32)
            prev = dump(prev, "dbg_bw", QSC, 32)
            prev = dump(prev, "dbg_br", QSF, 32)
            prev = dump(prev, "dbg_bt", QSE, 32)
            def mkg(e):
                return e.tensor_copy(QSD[:, 0:32], QIB[:, 0:32])
            prev = stage(prev, [("vector", mkg)])
            prev = dump(prev, "dbg_bg", QSD, 32)
        prev = run_serial(prev, q_ops(b2t, b2q, M=3, clamp=6, signed=True,
                                      ties=True, width=8))

        if debug:
            def mkdw(e):
                return e.tensor_copy(QSA[:, 0:1024], w1q[:, 0:1024])
            prev = stage(prev, [("vector", mkdw)])
            prev = dump(prev, "dbg_w1q", QSA, 1024)
            def mkdb(e):
                return e.tensor_copy(QSB[:, 0:32], b1q)
            prev = stage(prev, [("vector", mkdb)])
            prev = dump(prev, "dbg_b1q", QSB, 32)

        x2q3 = x2q.rearrange("p (c t) -> p c t", c=8)
        hq3 = hq.rearrange("p (c t) -> p c t", c=32)
        fx3 = fx.rearrange("p (c t) -> p c t", c=8)
        w1q3 = w1q.rearrange("p (c n) -> p c n", c=8)
        w2q3 = w2q.rearrange("p (c n) -> p c n", c=32)
        for ti in range(NFT):
            t0 = ti * FT
            prev = emit_ln(prev, d["xb"], t0, FT, fx, fxh, fxh,
                           dbg_ffn=(ti == 0))

            def mk_mld(e, t0=t0):
                e.dma_start(out=msk8.rearrange("p (c t) -> p c t", c=8),
                            in_=d["maskT"].rearrange("(c p) t -> p c t", p=128)
                            [:, :, t0:t0 + FT])
            prev = stage(prev, [("sync", mk_mld)])
            XW = min(1024, 8 * FT)
            for quar in range(8 * FT // XW):
                prev = run_serial(prev, q_ops(
                    fxh[:, quar * XW:(quar + 1) * XW],
                    x2q[:, quar * XW:(quar + 1) * XW],
                    M=3, clamp=6, signed=True, ties=False, width=XW))
            if debug and ti == 0:
                prev = dump(prev, "dbg_fxh", fxh, 8 * FT)
                def mkd(e):
                    return e.tensor_copy(QSA[:, 0:1024], x2q[:, 0:1024])
                prev = stage(prev, [("vector", mkd)])
                prev = dump(prev, "dbg_x2q", QSA, 1024)
            for grp in range(16):
                for sub in range(2):
                    dh = grp * 2 + sub

                    def mk_m(e, dh=dh):
                        for c in range(8):
                            ins = e.matmul(ps_p[:, 0:FT],
                                           w1q3[:, c, dh * 128:(dh + 1) * 128],
                                           x2q3[:, c, :],
                                           start=(c == 0), stop=(c == 7))
                        return ins
                    prev = stage(prev, [("tensor", mk_m)])

                    def mk_e(e, dh=dh, sub=sub):
                        e.activation(fz[:, sub * FT:(sub + 1) * FT],
                                     ps_p[:, 0:FT], AF.Relu,
                                     bias=b1q[:, dh:dh + 1])
                    prev = stage(prev, [("scalar", mk_e)])
                if debug and ti == 0 and grp == 0:
                    prev = dump(prev, "dbg_fz", fz, 2 * FT)
                prev = run_serial(prev, q_h(
                    fz[:, 0:2 * FT], hq[:, grp * 2 * FT:(grp + 1) * 2 * FT],
                    width=2 * FT))
                if debug and ti == 0 and grp == 0:
                    def mkd2(e):
                        return e.tensor_copy(QSA[:, 0:2 * FT],
                                             hq[:, 0:2 * FT])
                    prev = stage(prev, [("vector", mkd2)])
                    prev = dump(prev, "dbg_hq", QSA, 2 * FT)
            for dd in range(8):
                def mk_m2(e, dd=dd):
                    for c in range(32):
                        ins = e.matmul(ps_q[:, 0:FT],
                                       w2q3[:, c, dd * 128:(dd + 1) * 128],
                                       hq3[:, c, :],
                                       start=(c == 0), stop=(c == 31))
                    return ins
                prev = stage(prev, [("tensor", mk_m2)])

                def mk_e2(e, dd=dd):
                    return e.tensor_scalar(fy, ps_q[:, 0:FT],
                                           b2q[:, dd:dd + 1], None, ALU.add)
                prev = stage(prev, [("vector", mk_e2)])
                prev = run_serial(prev, q_ops(
                    fy, fz[:, 0:FT], M=10, clamp=14, signed=True, ties=False,
                    width=FT))

                def mk_dr1(e, dd=dd):
                    return e.tensor_tensor(
                        fy, fz[:, 0:FT],
                        msk8.rearrange("p (c t) -> p c t", c=8)[:, dd, :],
                        ALU.mult)
                prev = stage(prev, [("vector", mk_dr1)])

                def mk_dr2(e, dd=dd):
                    return e.tensor_tensor(fy, fy, fx3[:, dd, :], ALU.add)
                prev = stage(prev, [("vector", mk_dr2)])
                if debug and ti == 0 and dd == 0:
                    prev = dump(prev, "dbg_fy", fy, FT)

                def mk_out(e, dd=dd, t0=t0):
                    e.dma_start(
                        out=d["out_t"].rearrange("(c p) t -> p c t", p=128)
                        [:, dd, t0:t0 + FT], in_=fy)
                prev = stage(prev, [("gpsimd", mk_out)])

        ffn_stack.close()

        sch.schedule()
        if _SCHED_HOOK is not None:
            _SCHED_HOOK(sch)
        final = sch.inc_total

        class _Replay:
            def __init__(self, eng, oplist, sems):
                self._e = eng
                self._ops = oplist
                self._i = 0
                self._sems = sems

            def __getattr__(self, nm):
                f = getattr(self._e, nm)
                if not callable(f):
                    return f

                def g(*a, **k):
                    op = self._ops[self._i]
                    self._i += 1
                    for pe, v in op.waits:
                        self._e.wait_ge(self._sems[pe], v)
                    r = f(*a, **k)
                    if op.inc_id is not None and hasattr(r, "then_inc"):
                        r.then_inc(self._sems[op.engine],
                                   16 if op.is_dma else 1)
                    return r
                return g

        def engine_runner(ename):
            def run(eng):
                for fni, (e2, fn, bar) in enumerate(sch.fns):
                    if e2 != ename:
                        continue
                    fn(_Replay(eng, sch.op_by_fn[fni], sems))
                if ename == "gpsimd":
                    for e2 in ENGINES:
                        if final[e2] > 0:
                            eng.wait_ge(sems[e2], final[e2])
            return run

        with nc.Block() as block:
            block.sync(engine_runner("sync"))
            block.scalar(engine_runner("scalar"))
            block.vector(engine_runner("vector"))
            block.tensor(engine_runner("tensor"))
            block.gpsimd(engine_runner("gpsimd"))

    return nc


def _dropout_mask_np():
    import jax
    cpu = jax.devices("cpu")[0]
    with jax.default_device(cpu):
        key = jax.random.key(42, impl="threefry2x32")
        keep = jax.random.bernoulli(key, 0.8, (4, 8192, 1024))
        keep = np.asarray(keep)
    return keep


def kernel(L0_attn_x, L0_attn_Wq, L0_attn_Wk, L0_attn_Wv, L0_attn_Wo,
           L0_attn_fmq_w, L0_attn_fmq_b, L0_attn_fmk_w, L0_attn_fmk_b,
           L0_attn_rms_w, L0_ffn_w1, L0_ffn_b1, L0_ffn_w2, L0_ffn_b2):
    x = np.ascontiguousarray(np.asarray(L0_attn_x, dtype=np.float32))
    B, N, _ = x.shape
    keep = _dropout_mask_np()
    maskmul = np.where(keep, np.float32(1.25), np.float32(0.0))
    maskmul8 = maskmul.astype(ml_dtypes.float8_e4m3)

    wqT = np.ascontiguousarray(np.asarray(L0_attn_Wq).T)
    wkT = np.ascontiguousarray(np.asarray(L0_attn_Wk).T)
    wvT = np.ascontiguousarray(np.asarray(L0_attn_Wv).T)
    woT = np.ascontiguousarray(np.asarray(L0_attn_Wo).T)
    fmqT = np.ascontiguousarray(np.asarray(L0_attn_fmq_w).T)
    fmkT = np.ascontiguousarray(np.asarray(L0_attn_fmk_w).T)
    w1T = np.ascontiguousarray(np.asarray(L0_ffn_w1).T)
    w2T = np.ascontiguousarray(np.asarray(L0_ffn_w2).T)

    triu = np.triu(np.ones((CH, CH), dtype=np.float32))
    ident = np.eye(128, dtype=np.float32)
    wq_raw = np.ascontiguousarray(np.asarray(L0_attn_Wq, dtype=np.float32))
    wk_raw = np.ascontiguousarray(np.asarray(L0_attn_Wk, dtype=np.float32))
    fmqb_c = np.ascontiguousarray(
        np.asarray(L0_attn_fmq_b, dtype=np.float32).reshape(DK, 1))
    fmkb_c = np.ascontiguousarray(
        np.asarray(L0_attn_fmk_b, dtype=np.float32).reshape(DK, 1))
    fmkbr_c = np.ascontiguousarray(
        np.asarray(L0_attn_fmk_b, dtype=np.float32).reshape(1, DK))
    rmsw_c = np.ascontiguousarray(
        np.asarray(L0_attn_rms_w, dtype=np.float32).reshape(DK, 1))
    b1_c = np.ascontiguousarray(
        np.asarray(L0_ffn_b1, dtype=np.float32).reshape(32, 128).T)
    b2_c = np.ascontiguousarray(
        np.asarray(L0_ffn_b2, dtype=np.float32).reshape(8, 128).T)
    onesc = np.ones((128, 1), dtype=np.float32)
    zeros_pre = np.zeros((D, T), dtype=np.float32)

    nc = build_program()
    in_maps = []
    for c in range(8):
        b, half = c // 2, c % 2
        tok = slice(half * T, (half + 1) * T)
        in_maps.append({
            "x_t": np.ascontiguousarray(x[b, tok, :].T),
            "xpre_t": (zeros_pre if half == 0 else
                       np.ascontiguousarray(x[b, 0:T, :].T)),
            "wqT": wqT, "wkT": wkT, "wvT": wvT, "woT": woT,
            "fmqT": fmqT, "fmkT": fmkT,
            "fmq_b": fmqb_c, "fmk_b": fmkb_c, "fmk_br": fmkbr_c,
            "rms_w": rmsw_c,
            "w1T": w1T, "b1": b1_c,
            "w2T": w2T, "b2": b2_c,
            "maskT": np.ascontiguousarray(maskmul8[b, tok, :].T),
            "triu": triu, "ones": onesc, "ident": ident,
            "wq_raw": wq_raw, "wk_raw": wk_raw,
        })
    res = run_bass_kernel_spmd(nc, in_maps, core_ids=list(range(8)))
    out = np.empty((B, N, D), dtype=np.float32)
    for c in range(8):
        b, half = c // 2, c % 2
        out[b, half * T:(half + 1) * T, :] = \
            np.asarray(res.results[c]["out_t"]).T
    return out


# version 07
# speedup vs baseline: 1.7675x; 1.0128x over previous
import contextlib
import numpy as np
import ml_dtypes

import concourse.bass as bass
import concourse.mybir as mybir
from concourse.bass_utils import run_bass_kernel_spmd

F32 = mybir.dt.float32
F8 = mybir.dt.float8e4
I32 = mybir.dt.int32
AF = mybir.ActivationFunctionType
ALU = mybir.AluOpType

LN2 = float(np.log(2.0))
M23 = 8388608.0

D = 1024
H = 8
DK = 128
T = 4096
CH = 128
NCH = T // CH
TILE = 512
NTL = T // TILE
FT = 512
NFT = T // FT
DH = 4096
EPS = 1e-5

ENGINES = ("sync", "scalar", "vector", "tensor", "gpsimd")
DMA_ENG = ("sync", "gpsimd")


class OpRec:
    __slots__ = ("engine", "fnidx", "callidx", "reads", "writes", "inc_id",
                 "waits", "is_dma", "_deps")

    def __init__(self, engine, fnidx, callidx, reads, writes, is_dma):
        self.engine = engine
        self.fnidx = fnidx
        self.callidx = callidx
        self.reads = reads
        self.writes = writes
        self.is_dma = is_dma
        self.inc_id = None
        self.waits = []


_ROLES = {
    "matmul": ((0,), (1, 2)),
    "transpose": ((0,), (1, 2)),
    "activation": ((0,), (1, 3, 4)),
    "tensor_copy": ((0,), (1,)),
    "tensor_tensor": ((0,), (1, 2)),
    "tensor_scalar": ((0,), (1, 2, 3)),
    "tensor_single_scalar": ((0,), (1, 2)),
    "scalar_tensor_tensor": ((0,), (1, 2, 3)),
    "memset": ((0,), ()),
    "dma_start": ((), ()),
    "tensor_add": ((0,), (1, 2)),
    "tensor_sub": ((0,), (1, 2)),
    "tensor_mul": ((0,), (1, 2)),
}


def _ap_name(a):
    t = getattr(a, "tensor", None)
    if t is None:
        return None
    return getattr(t, "name", None)


class _Recorder:

    class _Ins:
        def then_inc(self, *a, **k):
            return self

    def __init__(self, engine, sink):
        self._engine = engine
        self._sink = sink

    def __getattr__(self, nm):
        def call(*args, **kwargs):
            roles = _ROLES.get(nm)
            reads, writes = set(), set()
            if roles is None:
                for a in list(args) + list(kwargs.values()):
                    n = _ap_name(a)
                    if n:
                        reads.add(n)
                        writes.add(n)
            else:
                for i, a in enumerate(args):
                    n = _ap_name(a)
                    if not n:
                        continue
                    if i in roles[0]:
                        writes.add(n)
                    else:
                        reads.add(n)
                for k, a in kwargs.items():
                    n = _ap_name(a)
                    if not n:
                        continue
                    if k == "out":
                        writes.add(n)
                    else:
                        reads.add(n)
            self._sink.append((nm, reads, writes, nm == "dma_start"))
            return _Recorder._Ins()
        return call


class Sched:

    def __init__(self):
        self.fns = []
        self.ops = []
        self.per_engine_n = {e: 0 for e in ENGINES}
        self.done = {e: 0 for e in ENGINES}
        self._pending_barrier = False

    def stage(self, deps, items):
        for e, fn in items:
            self.fns.append((e, fn, self._pending_barrier))
            self._pending_barrier = False
        return dict(self.done)

    def snap(self):
        return dict(self.done)

    def barrier(self):
        self._pending_barrier = True

    def schedule(self):
        recs = []
        for e, fn, bar in self.fns:
            sink = []
            fn(_Recorder(e, sink))
            recs.append(sink)
        ops = []
        last_write = {}
        readers = {}
        barrier_join = {}
        last_op_eng = {e: None for e in ENGINES}
        for (e, fn, bar), sink in zip(self.fns, recs):
            if bar:
                barrier_join = {k: v for k, v in last_op_eng.items()
                                if v is not None}
            for callidx, (nm, reads, writes, is_dma) in enumerate(sink):
                op = OpRec(e, len(ops), callidx, reads, writes, is_dma)
                deps = list(barrier_join.values())
                for r in reads:
                    o = last_write.get(r)
                    if o is not None:
                        deps.append(o)
                for w in writes:
                    o = last_write.get(w)
                    if o is not None:
                        deps.append(o)
                    deps.extend(readers.get(w, ()))
                op._deps = deps
                for w in writes:
                    last_write[w] = op
                    readers[w] = []
                for r in reads:
                    readers.setdefault(r, []).append(op)
                ops.append(op)
                last_op_eng[e] = op
        needs_inc = [False] * len(ops)
        for op in ops:
            for d in op._deps:
                if d.engine == op.engine == "tensor":
                    continue
                needs_inc[d.fnidx] = True
        inc_total = {e: 0 for e in ENGINES}
        inc_cum = [0] * len(ops)
        for i, op in enumerate(ops):
            if needs_inc[i] or op.is_dma:
                inc_total[op.engine] += 16 if op.is_dma else 1
                op.inc_id = inc_total[op.engine]
            inc_cum[i] = inc_total[op.engine]
        wm = {e: {e2: 0 for e2 in ENGINES} for e in ENGINES}
        dma_ckpt = {e: 0 for e in ENGINES}
        for op in ops:
            tgt = {}
            if op.is_dma and dma_ckpt[op.engine] > 0:
                tgt[op.engine] = dma_ckpt[op.engine]
            for d in op._deps:
                if d.engine == op.engine == "tensor":
                    continue
                v = d.inc_id if d.inc_id is not None else inc_cum[d.fnidx]
                if v and v > tgt.get(d.engine, 0):
                    tgt[d.engine] = v
            for pe, v in tgt.items():
                if v > wm[op.engine][pe]:
                    op.waits.append((pe, v))
                    wm[op.engine][pe] = v
            if op.is_dma and needs_inc[op.fnidx]:
                dma_ckpt[op.engine] = op.inc_id
        self.ops = ops
        self.inc_total = inc_total
        self.op_by_fn = {}
        idx = 0
        for fni, ((e, fn, bar), sink) in enumerate(zip(self.fns, recs)):
            self.op_by_fn[fni] = ops[idx: idx + len(sink)]
            idx += len(sink)


_SCHED_HOOK = None


def build_program(debug=False):
    nc = bass.Bass("TRN2", target_bir_lowering=False)

    d = {}
    for nm, shape in [
        ("x_t", [D, T]), ("xpre_t", [D, T]),
        ("wqT", [D, D]), ("wkT", [D, D]), ("wvT", [D, D]), ("woT", [D, D]),
        ("fmqT", [DK, DK]), ("fmkT", [DK, DK]),
        ("fmq_b", [DK, 1]), ("fmk_b", [DK, 1]), ("fmk_br", [1, DK]),
        ("rms_w", [DK, 1]),
        ("w1T", [D, DH]), ("b1", [128, 32]), ("w2T", [DH, D]),
        ("b2", [128, 8]),
        ("triu", [CH, CH]), ("ones", [128, 1]),
        ("wq_raw", [D, D]), ("wk_raw", [D, D]), ("ident", [128, 128]),
    ]:
        d[nm] = nc.dram_tensor(nm, shape, F32, kind="ExternalInput").ap()
    for nm in ("wqf", "wkf"):
        d[nm] = nc.dram_tensor(nm, [D, D], F32, kind="Internal").ap()
    d["maskT"] = nc.dram_tensor("maskT", [D, T], F8, kind="ExternalInput").ap()
    d["out_t"] = nc.dram_tensor("out_t", [D, T], F32, kind="ExternalOutput").ap()
    d["xb"] = nc.dram_tensor("xb", [D, T], F32,
                             kind="ExternalOutput" if debug else "Internal").ap()
    if debug:
        for nm, shape, dtp in [
            ("dbg_xh", [128, 1024], F32), ("dbg_q", [128, 1024], F32),
            ("dbg_qfa", [128, 1024], F32), ("dbg_kfa", [128, 1024], F32),
            ("dbg_kta", [128, 1024], F32), ("dbg_vt", [128, 1024], F32),
            ("dbg_sc", [128, 128], F32), ("dbg_on", [128, 1024], F32),
            ("dbg_Sa", [128, 1024], F32), ("dbg_Sb", [128, 1024], F32),
            ("dbg_fxh", [128, 4096], F32), ("dbg_x2q", [128, 4096], F32),
            ("dbg_fz", [128, 2048], F32), ("dbg_hq", [128, 4096], F32),
            ("dbg_fy", [128, 512], F32),
            ("dbg_opre", [128, 128], F32), ("dbg_osq", [128, 128], F32),
            ("dbg_stat", [1, 1024], F32),
            ("dbg_qfb", [128, 1024], F32), ("dbg_kfb", [128, 1024], F32),
            ("dbg_fmqb", [128, 2], F32),
            ("dbg_w1q", [128, 1024], F32), ("dbg_b1q", [128, 32], F32),
            ("dbg_w1qf", [128, 8 * DH], F8), ("dbg_w2qf", [128, 32 * D], F8),
            ("dbg_ffx", [128, 4096], F32), ("dbg_fstat", [1, 1024], F32),
            ("dbg_fstat2", [1, 1024], F32), ("dbg_fpsm", [128, 512], F32),
            ("dbg_fpsr", [128, 512], F32),
            ("dbg_b1t", [128, 32], F32), ("dbg_bf", [128, 32], F32),
            ("dbg_bw", [128, 32], F32), ("dbg_br", [128, 32], F32),
            ("dbg_bt", [128, 32], F32), ("dbg_bg", [128, 32], F32),
            ("dbg_bis", [128, 32], F32), ("dbg_bb1", [128, 32], F32),
            ("dbg_bb2", [128, 32], F32), ("dbg_bcomb", [128, 32], F32),
            ("dbg_sch", [128, 8 * 128], F32),
            ("dbg_oph", [128, 8 * 128], F32),
        ]:
            d[nm] = nc.dram_tensor(nm, shape, dtp,
                                   kind="ExternalOutput").ap()

    with contextlib.ExitStack() as st:
        def mksbuf(stack):
            def sbuf(name, shape, dt=F32):
                return stack.enter_context(nc.sbuf_tensor(name, shape, dt)).ap()
            return sbuf

        sbuf = mksbuf(st)

        def psum(name, shape):
            return st.enter_context(nc.psum_tensor(name, shape, F32)).ap()

        cb = sbuf("cb", [128, 16])
        CIDX = {"tiny": 0, "eps": 1, "c6": 2, "c14": 3}
        CVALS = {"tiny": 1e-37, "eps": EPS, "c6": 6.0, "c14": 14.0}

        def cba(nm, row128=True):
            i = CIDX[nm]
            return cb[:, i:i + 1] if row128 else cb[0:1, i:i + 1]

        fmq = sbuf("fmq", [128, DK])
        fmk = sbuf("fmk", [128, DK])
        fmqb = sbuf("fmqb", [128, 2])
        fmkb = sbuf("fmkb", [128, 2])
        fmkb_row = sbuf("fmkb_row", [1, DK])
        rmsw = sbuf("rmsw", [128, 1])
        b1t = sbuf("b1t", [128, 32])
        b2t = sbuf("b2t", [128, 8])
        b1q = sbuf("b1q", [128, 32])
        b2q = sbuf("b2q", [128, 8])
        triu = sbuf("triu_s", [128, CH])
        ones = sbuf("ones_s", [128, 1])
        ones_r = sbuf("ones_r", [1, 128])
        fmkb_bc = sbuf("fmkb_bc", [128, DK])
        S_a = sbuf("S_a", [128, H * DK])
        S_b = sbuf("S_b", [128, H * DK])
        stat = sbuf("stat", [1, 1024])
        stat2 = sbuf("stat2", [1, 1024])
        stat_i2 = sbuf("stat_i2", [1, 1024], I32)
        stat_i3 = sbuf("stat_i3", [1, 1024], I32)

        attn_stack = contextlib.ExitStack()
        asb = mksbuf(attn_stack)
        wbA = asb("wbA", [128, 8 * D])
        wbB = asb("wbB", [128, 8 * D])
        xc = asb("xc", [128, 8 * TILE])
        xh = asb("xh", [128, 8 * TILE])
        qp = asb("qp", [128, 8 * TILE])
        kp = asb("kp", [128, 8 * TILE])
        qfc_a = asb("qfc_a", [128, H * CH])
        qfc_b = asb("qfc_b", [128, H * CH])
        yt_w = qfc_a
        kfc_a = asb("kfc_a", [128, H * CH])
        kfc_b = asb("kfc_b", [128, H * CH])
        kt_a = asb("kt_a", [128, H * DK])
        kt_b = asb("kt_b", [128, H * DK])
        vt = asb("vt", [128, D])
        sc_w = asb("sc_w", [128, H * CH])
        o_n = asb("o_n", [128, H * TILE])
        idn = asb("idn", [128, 128])
        wpre = [asb("wpre0", [128, 128]), asb("wpre1", [128, 128])]
        wout = [asb("wout0", [128, 128]), asb("wout1", [128, 128])]
        ps_p = psum("ps_p", [128, 512])
        ps_q = psum("ps_q", [128, 512])
        ps_v = psum("ps_v", [128, 1024])
        ps_s = psum("ps_s", [128, 512])
        ps_o = psum("ps_o", [128, 512])
        ps_d = psum("ps_d", [128, 512])

        sems = {e: st.enter_context(nc.semaphore("sem_" + e)) for e in ENGINES}

        nc.const_aps.register(F32, -0.125, cb[:, 7:8])
        nc.const_aps.register(F32, -4 * LN2, cb[:, 6:7])
        nc.const_aps.register(F32, 136.0 * M23, cb[:, 8:9])
        nc.const_aps.register(F32, 118.0 * M23, cb[:, 9:10])
        nc.const_aps.register(F32, 112.0 * M23, cb[:, 10:11])

        sch = Sched()

        def stage(deps, items):
            return sch.stage(deps, items)

        items = [("vector", (lambda i, v: lambda e: e.memset(
            cb[:, i:i + 1], v))(CIDX[nm], CVALS[nm])) for nm in CIDX]
        items.append(("vector", lambda e: e.memset(cb[:, 7:8], -0.125)))
        items.append(("vector", lambda e: e.memset(cb[:, 6:7], -4 * LN2)))
        items.append(("vector", lambda e: e.memset(cb[:, 8:9], 136.0 * M23)))
        items.append(("vector", lambda e: e.memset(cb[:, 9:10], 118.0 * M23)))
        items.append(("vector", lambda e: e.memset(cb[:, 10:11],
                                                   112.0 * M23)))
        items.append(("vector", lambda e: e.memset(ones_r, 1.0)))
        items.append(("vector", lambda e: e.memset(S_a, 0.0)))
        items.append(("vector", lambda e: e.memset(S_b, 0.0)))
        loads = [
            (fmq, d["fmqT"]), (fmk, d["fmkT"]),
            (triu, d["triu"]), (ones, d["ones"]),
            (rmsw, d["rms_w"]),
            (fmqb[:, 0:1], d["fmq_b"]),
            (fmkb[:, 0:1], d["fmk_b"]),
            (fmkb_row, d["fmk_br"]),
            (b1t, d["b1"]),
            (b2t, d["b2"]),
            (idn, d["ident"]),
            (wbB.rearrange("p (c n) -> p c n", c=8),
             d["wvT"].rearrange("(c p) n -> p c n", p=128)),
        ]
        for dst, src in loads:
            items.append(("sync", (lambda dst, src: lambda e: e.dma_start(
                out=dst, in_=src))(dst, src)))
        prev = stage(sch.snap(), items)

        prev = stage(prev, [("scalar", lambda e: e.activation(
            fmqb[:, 1:2], fmqb[:, 0:1], AF.Copy, scale=-1.0,
            bias=-4 * LN2))])
        prev = stage(prev, [("scalar", lambda e: e.activation(
            fmqb[:, 0:1], fmqb[:, 0:1], AF.Copy, scale=1.0,
            bias=-4 * LN2))])
        prev = stage(prev, [("scalar", lambda e: e.activation(
            fmkb[:, 1:2], fmkb[:, 0:1], AF.Copy, scale=-1.0))])
        sch.barrier()
        prev = stage(prev, [("tensor", lambda e: e.matmul(
            ps_v[:, 0:DK], ones_r, fmkb_row, start=True, stop=True))])
        prev = stage(prev, [("scalar", lambda e: e.activation(
            fmkb_bc, ps_v[:, 0:DK], AF.Copy))])

        for raw, fmw, dst in ((d["wq_raw"], fmq, d["wqf"]),
                              (d["wk_raw"], fmk, d["wkf"])):
            for h in range(H):
                for jc in range(8):
                    pi = (h * 8 + jc) % 2
                    wp, wo_, psx = wpre[pi], wout[pi], (ps_s if pi == 0
                                                        else ps_d)

                    def mk_ld(e, raw=raw, h=h, jc=jc, wp=wp):
                        return e.dma_start(
                            out=wp, in_=raw[h * DK:(h + 1) * DK,
                                            jc * 128:(jc + 1) * 128])
                    prev = stage(prev, [("sync", mk_ld)])

                    def mk_mm(e, wp=wp, fmw=fmw, psx=psx):
                        return e.matmul(psx[:, 0:128], wp, fmw,
                                        start=True, stop=True)
                    prev = stage(prev, [("tensor", mk_mm)])

                    def mk_ev(e, wo_=wo_, psx=psx):
                        return e.activation(wo_, psx[:, 0:128], AF.Copy)
                    prev = stage(prev, [("scalar", mk_ev)])

                    def mk_st(e, dst=dst, h=h, jc=jc, wo_=wo_):
                        return e.dma_start(
                            out=dst[jc * 128:(jc + 1) * 128,
                                    h * DK:(h + 1) * DK], in_=wo_)
                    prev = stage(prev, [("gpsimd", mk_st)])
        def mk_wkf(e):
            return e.dma_start(
                out=wbA.rearrange("p (c n) -> p c n", c=8),
                in_=d["wkf"].rearrange("(c p) n -> p c n", p=128))
        prev = stage(prev, [("sync", mk_wkf)])

        def wslice(wt, c, dd):
            return wt[:, c * D + dd * 128: c * D + (dd + 1) * 128]

        def dump(prev, name, src_tile, width):
            if not debug:
                return prev
            def mk(e, name=name, src_tile=src_tile, width=width):
                return e.dma_start(out=d[name][:, 0:width],
                                   in_=src_tile[:, 0:width])
            return stage(prev, [("gpsimd", mk)])

        def q_ops(z, out_q, M, clamp, signed, ties, width, S=None):
            S = S if S is not None else QS0
            ops = []
            onv = lambda f: ops.append(("vector", f))
            ons = lambda f: ops.append(("scalar", f))
            l, f, w = S["A"][:, :width], S["B"][:, :width], S["C"][:, :width]
            tt, r = S["E"][:, :width], S["F"][:, :width]
            bs = S["G"][:, :width]
            isx, sx = S["D"][:, :width], S["H"][:, :width]
            gE, g = S["IA"][:, :width], S["IB"][:, :width]
            bEi, p = S["IC"][:, :width], S["IA"][:, :width]
            tie, odd = S["G"][:, :width], S["D"][:, :width]
            if signed:
                ons(lambda e: e.activation(tt, z, AF.Square))
                ons(lambda e: e.activation(l, tt, AF.Ln, bias=cba("tiny")))
                c, inv = 1.0 / (2 * LN2), 2 * LN2
            else:
                ons(lambda e: e.activation(l, z, AF.Ln, bias=cba("tiny")))
                c, inv = 1.0 / LN2, LN2
            ons((lambda c: lambda e: e.activation(gE, l, AF.Copy, scale=c))(c))
            onv((lambda inv: lambda e: e.scalar_tensor_tensor(
                bEi, gE, inv, l, ALU.mult, ALU.is_gt))(inv))
            onv(lambda e: e.scalar_tensor_tensor(
                f, bEi, -1.0, gE, ALU.mult, ALU.add))
            ons((lambda cl: lambda e: e.activation(
                w, f, AF.Relu, bias=cba("c%d" % cl)))(clamp))
            if ties:
                ons((lambda MF: lambda e: e.activation(
                    gE, w, AF.Copy, scale=-M23, bias=MF * M23))(
                        float(127 + M + clamp)))
                onv(lambda e: e.tensor_tensor(tt, z, gE.bitcast(F32),
                                              ALU.mult))
            else:
                ons(lambda e: e.activation(isx, w, AF.Exp, scale=-LN2))
                onv((lambda s_: lambda e: e.scalar_tensor_tensor(
                    tt, z, s_, isx, ALU.mult, ALU.mult))(
                        float(2.0 ** (M + clamp))))
            if signed:
                ons(lambda e: e.activation(g, tt, AF.Copy))
                onv(lambda e: e.scalar_tensor_tensor(
                    r, g, 0.5, tt, ALU.add, ALU.is_le))
                onv(lambda e: e.scalar_tensor_tensor(
                    bs, g, -0.5, tt, ALU.add, ALU.is_gt))
                onv(lambda e: e.scalar_tensor_tensor(
                    r, bs, -1.0, r, ALU.mult, ALU.add))
                onv(lambda e: e.tensor_tensor(r, r, g, ALU.add))
            else:
                ons(lambda e: e.activation(g, tt, AF.Copy, bias=-0.125))
                onv(lambda e: e.scalar_tensor_tensor(
                    r, g, 0.5, tt, ALU.add, ALU.is_le))
                onv(lambda e: e.tensor_tensor(r, r, g, ALU.add))
            if ties:
                onv(lambda e: e.scalar_tensor_tensor(
                    tie, r, -0.5, tt, ALU.add, ALU.is_equal))
                ons(lambda e: e.activation(odd, r, AF.Copy, scale=0.5))
                ons(lambda e: e.activation(p, odd, AF.Copy))
                onv(lambda e: e.tensor_tensor(odd, odd, p, ALU.not_equal))
                onv(lambda e: e.tensor_tensor(tie, tie, odd, ALU.mult))
                onv(lambda e: e.tensor_tensor(r, r, tie, ALU.subtract))
                ons((lambda MF: lambda e: e.activation(
                    gE, w, AF.Copy, scale=M23, bias=MF * M23))(
                        float(127 - M - clamp)))
                onv(lambda e: e.tensor_tensor(
                    out_q, r, gE.bitcast(F32), ALU.mult))
            else:
                ons(lambda e: e.activation(sx, w, AF.Exp, scale=LN2))
                onv((lambda s_: lambda e: e.scalar_tensor_tensor(
                    out_q, r, s_, sx, ALU.mult, ALU.mult))(
                        float(2.0 ** (-(M + clamp)))))
            return ops

        def q_h(z, out_q, width, S=None):
            S = S if S is not None else QS0
            ops = []
            onv = lambda f: ops.append(("vector", f))
            ons = lambda f: ops.append(("scalar", f))
            l, f = S["A"][:, :width], S["B"][:, :width]
            w14, w6 = S["C"][:, :width], S["D"][:, :width]
            tt, r510 = S["F"][:, :width], S["E"][:, :width]
            r43 = S["G"][:, :width]
            isx, sx = S["H"][:, :width], S["H"][:, :width]
            gE, g = S["IA"][:, :width], S["IB"][:, :width]
            bEi, p = S["IC"][:, :width], S["IA"][:, :width]
            tie, odd = S["A"][:, :width], S["B"][:, :width]
            ons(lambda e: e.activation(l, z, AF.Ln, bias=cba("tiny")))
            ons(lambda e: e.activation(gE, l, AF.Copy, scale=1.0 / LN2))
            onv(lambda e: e.scalar_tensor_tensor(
                bEi, gE, LN2, l, ALU.mult, ALU.is_gt))
            onv(lambda e: e.scalar_tensor_tensor(
                f, bEi, -1.0, gE, ALU.mult, ALU.add))
            ons(lambda e: e.activation(w14, f, AF.Relu, bias=cba("c14")))
            ons(lambda e: e.activation(w6, f, AF.Relu, bias=cba("c6")))
            ons(lambda e: e.activation(isx, w14, AF.Exp, scale=-LN2))
            onv(lambda e: e.scalar_tensor_tensor(
                tt, z, float(2.0 ** 24), isx, ALU.mult, ALU.mult))
            ons(lambda e: e.activation(g, tt, AF.Copy, bias=-0.125))
            onv(lambda e: e.scalar_tensor_tensor(
                r510, g, 0.5, tt, ALU.add, ALU.is_le))
            onv(lambda e: e.tensor_tensor(r510, r510, g, ALU.add))
            onv(lambda e: e.tensor_tensor(w14, w14, w6, ALU.subtract))
            ons(lambda e: e.activation(gE, w14, AF.Copy, scale=M23,
                                       bias=112.0 * M23))
            onv(lambda e: e.tensor_tensor(tt, r510, gE.bitcast(F32),
                                          ALU.mult))
            ons(lambda e: e.activation(g, tt, AF.Copy, bias=-0.125))
            onv(lambda e: e.scalar_tensor_tensor(
                r43, g, 0.5, tt, ALU.add, ALU.is_le))
            onv(lambda e: e.tensor_tensor(r43, r43, g, ALU.add))
            onv(lambda e: e.scalar_tensor_tensor(
                tie, r43, -0.5, tt, ALU.add, ALU.is_equal))
            ons(lambda e: e.activation(odd, r43, AF.Copy, scale=0.5))
            ons(lambda e: e.activation(p, odd, AF.Copy))
            onv(lambda e: e.tensor_tensor(odd, odd, p, ALU.not_equal))
            onv(lambda e: e.tensor_tensor(tie, tie, odd, ALU.mult))
            onv(lambda e: e.tensor_tensor(r43, r43, tie, ALU.subtract))
            ons(lambda e: e.activation(sx, w6, AF.Exp, scale=LN2))
            onv(lambda e: e.scalar_tensor_tensor(
                out_q, r43, float(2.0 ** -9), sx, ALU.mult, ALU.mult))
            return ops

        def run_serial(prev, ops):
            for eng, f in ops:
                prev = stage(prev, [(eng, f)])
            return prev

        def run_pipe(prev, chains):
            queue = list(chains)
            cur = queue.pop(0) if queue else None
            i = 0
            nxt, j = None, 0
            while cur is not None:
                prev = stage(prev, [cur[i]])
                i += 1
                if nxt is None and queue and 2 * i >= len(cur):
                    nxt, j = queue.pop(0), 0
                if nxt is not None:
                    prev = stage(prev, [nxt[j]])
                    j += 1
                if i >= len(cur):
                    cur, i, nxt, j = nxt, j, None, 0
                    if cur is None and queue:
                        cur, i = queue.pop(0), 0
            return prev

        def emit_ln(prev, src_ap, t0, width, x_tile, sq_tile, out_tile,
                    do_load=True, dbg_ffn=False):
            x3 = x_tile.rearrange("p (c t) -> p c t", c=8)
            sq3 = sq_tile.rearrange("p (c t) -> p c t", c=8)
            o3 = out_tile.rearrange("p (c t) -> p c t", c=8)
            if do_load:
                prev = stage(prev, [("sync", (lambda: lambda e: e.dma_start(
                    out=x3, in_=src_ap.rearrange("(c p) t -> p c t", p=128)
                    [:, :, t0:t0 + width]))())])
            prev = stage(prev, [("scalar", lambda e: e.activation(
                sq_tile, x_tile, AF.Square))])

            def mk_stats(e):
                for c in range(8):
                    e.matmul(ps_s[0:1, 0:width], ones, x3[:, c, :],
                             start=(c == 0), stop=(c == 7))
                for c in range(8):
                    ins = e.matmul(ps_d[0:1, 0:width], ones, sq3[:, c, :],
                                   start=(c == 0), stop=(c == 7))
                return ins
            prev = stage(prev, [("tensor", mk_stats)])

            def mk_sv1(e):
                e.tensor_scalar(stat[0:1, 0:width], ps_s[0:1, 0:width],
                                1.0 / D, None, ALU.mult)
                return e.tensor_scalar(stat2[0:1, 0:width],
                                       ps_d[0:1, 0:width],
                                       1.0 / D, None, ALU.mult)
            prev = stage(prev, [("vector", mk_sv1)])

            def mk_sv2(e):
                return e.tensor_tensor(stat[0:1, width:2 * width],
                                       stat[0:1, 0:width],
                                       stat[0:1, 0:width], ALU.mult)
            prev = stage(prev, [("vector", mk_sv2)])

            def mk_sv3(e):
                return e.tensor_tensor(stat2[0:1, 0:width],
                                       stat2[0:1, 0:width],
                                       stat[0:1, width:2 * width],
                                       ALU.subtract)
            prev = stage(prev, [("vector", mk_sv3)])

            def mk_ss(e):
                e.activation(stat2[0:1, width:2 * width],
                             stat2[0:1, 0:width], AF.Ln,
                             bias=cba("eps", False))
                return e.activation(stat2[0:1, 0:width],
                                    stat2[0:1, width:2 * width],
                                    AF.Exp, scale=-0.5)
            prev = stage(prev, [("scalar", mk_ss)])

            def mk_bc(e):
                e.matmul(ps_v[:, 0:width], ones_r, stat[0:1, 0:width],
                         start=True, stop=True)
                return e.matmul(ps_v[:, 512:512 + width], ones_r,
                                stat2[0:1, 0:width], start=True, stop=True)
            prev = stage(prev, [("tensor", mk_bc)])
            m_b = ps_v[:, 0:width].unsqueeze(1).broadcast_to([128, 8, width])
            r_b = ps_v[:, 512:512 + width].unsqueeze(1)\
                .broadcast_to([128, 8, width])

            if debug and dbg_ffn:
                def mkd1(e):
                    e.dma_start(out=d["dbg_ffx"][:, 0:8 * width],
                                in_=x_tile[:, 0:8 * width])
                    e.dma_start(out=d["dbg_fstat"][0:1, 0:width],
                                in_=stat[0:1, 0:width])
                    return e.dma_start(out=d["dbg_fstat2"][0:1, 0:width],
                                       in_=stat2[0:1, 0:width])
                prev = stage(prev, [("gpsimd", mkd1)])

            def mk_ap(e):
                e.tensor_tensor(sq3, x3, m_b, ALU.subtract)
                return e.tensor_tensor(o3, sq3, r_b, ALU.mult)
            prev = stage(prev, [("vector", mk_ap)])
            return prev

        def emit_wload(prev, wsrc, wbuf):
            def mk(e):
                return e.dma_start(
                    out=wbuf.rearrange("p (c n) -> p c n", c=8),
                    in_=wsrc.rearrange("(c p) n -> p c n", p=128))
            return stage(prev, [("sync", mk)])

        def emit_proj_t(prev, wbuf, out_sb):
            for dd in range(8):
                ps = ps_p if dd % 2 == 0 else ps_q

                def mk_mm(e, dd=dd, ps=ps):
                    for c in range(8):
                        ins = e.matmul(ps[:, 0:TILE], wslice(wbuf, c, dd),
                                       xh[:, c * TILE:(c + 1) * TILE],
                                       start=(c == 0), stop=(c == 7))
                    return ins
                prev = stage(prev, [("tensor", mk_mm)])

                def mk_ev(e, dd=dd, ps=ps):
                    return e.activation(out_sb[:, dd * TILE:(dd + 1) * TILE],
                                        ps[:, 0:TILE], AF.Copy)
                prev = stage(prev, [("scalar", mk_ev)])
            return prev

        qp4 = qp.rearrange("p (h t) -> p h t", h=H)
        kp4 = kp.rearrange("p (h t) -> p h t", h=H)
        qfc_a4 = qfc_a.rearrange("p (h t) -> p h t", h=H)
        qfc_b4 = qfc_b.rearrange("p (h t) -> p h t", h=H)
        kfc_a4 = kfc_a.rearrange("p (h t) -> p h t", h=H)
        kfc_b4 = kfc_b.rearrange("p (h t) -> p h t", h=H)
        scw4 = sc_w.rearrange("p (h t) -> p h t", h=H)
        on4 = o_n.rearrange("p (h t) -> p h t", h=H)
        kb_b = fmkb_bc.unsqueeze(1).broadcast_to([128, 4, DK])

        def emit_qk_chunk(prev, cix, with_scores):
            c0 = cix * CH
            if with_scores:
                def mk_qf(e, c0=c0):
                    e.activation(qfc_a4, qp4[:, :, c0:c0 + CH], AF.Exp,
                                 bias=fmqb[:, 0:1])
                    e.activation(qfc_b4, qp4[:, :, c0:c0 + CH], AF.Exp,
                                 scale=-1.0, bias=fmqb[:, 1:2])
                    e.activation(kfc_a4, kp4[:, :, c0:c0 + CH], AF.Exp,
                                 bias=fmkb[:, 0:1])
                    return e.activation(kfc_b4, kp4[:, :, c0:c0 + CH],
                                        AF.Exp, scale=-1.0,
                                        bias=fmkb[:, 1:2])
                prev = stage(prev, [("scalar", mk_qf)])

            def mk_tr(e, c0=c0):
                for h in range(H):
                    ps = ps_s if h < 4 else ps_d
                    off = (h % 4) * DK
                    ins = e.transpose(ps[:, off:off + DK],
                                      kp[:, h * TILE + c0:h * TILE + c0 + CH],
                                      idn)
                return ins
            prev = stage(prev, [("tensor", mk_tr)])

            def mk_ktb(e):
                e.tensor_tensor(scw4[:, 0:4, :],
                                ps_s.rearrange("p (h t) -> p h t", h=4),
                                kb_b, ALU.add)
                return e.tensor_tensor(scw4[:, 4:8, :],
                                       ps_d.rearrange("p (h t) -> p h t", h=4),
                                       kb_b, ALU.add)
            prev = stage(prev, [("vector", mk_ktb)])

            def mk_kte(e):
                e.activation(kt_a, sc_w, AF.Exp)
                return e.activation(kt_b, sc_w, AF.Exp, scale=-1.0)
            prev = stage(prev, [("scalar", mk_kte)])
            return prev

        def emit_v_chunk(prev, wvbuf, cix):
            c0 = cix * CH

            def mk_vm(e, c0=c0):
                wv3 = wvbuf.rearrange("p (c n) -> p c n", c=8)
                for half in range(2):
                    for c in range(8):
                        ins = e.matmul(
                            ps_v[:, half * 512:(half + 1) * 512],
                            xh[:, c * TILE + c0:c * TILE + c0 + CH],
                            wv3[:, c, half * 512:(half + 1) * 512],
                            start=(c == 0), stop=(c == 7))
                return ins
            prev = stage(prev, [("tensor", mk_vm)])

            def mk_vev(e):
                e.activation(vt[:, 0:512], ps_v[:, 0:512], AF.Copy)
                return e.activation(vt[:, 512:1024], ps_v[:, 512:1024],
                                    AF.Copy)
            prev = stage(prev, [("scalar", mk_vev)])
            return prev

        tri_b = triu.unsqueeze(1).broadcast_to([128, 4, CH])

        def emit_scan_chunk(prev, cix, with_output):
            c0 = cix * CH
            if with_output:
                def mk_sc(e):
                    for h in range(H):
                        ps = ps_s if h < 4 else ps_p
                        off = (h % 4) * CH
                        e.matmul(ps[:, off:off + CH],
                                 kfc_a[:, h * CH:(h + 1) * CH],
                                 qfc_a[:, h * CH:(h + 1) * CH],
                                 start=True, stop=False)
                        ins = e.matmul(ps[:, off:off + CH],
                                       kfc_b[:, h * CH:(h + 1) * CH],
                                       qfc_b[:, h * CH:(h + 1) * CH],
                                       start=False, stop=True)
                    return ins
                prev = stage(prev, [("tensor", mk_sc)])

                def mk_mask(e):
                    e.tensor_tensor(
                        scw4[:, 0:4, :],
                        ps_s.rearrange("p (h t) -> p h t", h=4), tri_b,
                        ALU.mult)
                    return e.tensor_tensor(
                        scw4[:, 4:8, :],
                        ps_p.rearrange("p (h t) -> p h t", h=4), tri_b,
                        ALU.mult)
                prev = stage(prev, [("vector", mk_mask)])

                def mk_o(e):
                    for h in range(H):
                        ps = ps_o if h < 4 else ps_q
                        off = (h % 4) * CH
                        e.matmul(ps[:, off:off + CH],
                                 vt[:, h * DK:(h + 1) * DK],
                                 sc_w[:, h * CH:(h + 1) * CH],
                                 start=True, stop=False)
                        e.matmul(ps[:, off:off + CH],
                                 S_a[:, h * DK:(h + 1) * DK],
                                 qfc_a[:, h * CH:(h + 1) * CH],
                                 start=False, stop=False)
                        ins = e.matmul(ps[:, off:off + CH],
                                       S_b[:, h * DK:(h + 1) * DK],
                                       qfc_b[:, h * CH:(h + 1) * CH],
                                       start=False, stop=True)
                    return ins
                prev = stage(prev, [("tensor", mk_o)])

                def mk_oev2(e, c0=c0):
                    e.activation(on4[:, 0:4, c0:c0 + CH], ps_o.rearrange(
                        "p (h t) -> p h t", h=4), AF.Copy)
                    e.activation(on4[:, 4:8, c0:c0 + CH], ps_q.rearrange(
                        "p (h t) -> p h t", h=4), AF.Copy)
                    e.activation(yt_w[:, 0:512], ps_o[:, 0:512], AF.Square)
                    return e.activation(yt_w[:, 512:1024], ps_q[:, 0:512],
                                        AF.Square)
                prev = stage(prev, [("scalar", mk_oev2)])

                def mk_osum(e):
                    e.matmul(ps_d[0:1, 0:512], ones, yt_w[:, 0:512],
                             start=True, stop=True)
                    return e.matmul(ps_v[0:1, 0:512], ones, yt_w[:, 512:1024],
                                    start=True, stop=True)
                prev = stage(prev, [("tensor", mk_osum)])

                def mk_r1(e):
                    e.tensor_scalar(stat[0:1, 0:512], ps_d[0:1, 0:512],
                                    1.0 / DK, EPS, ALU.mult, ALU.add)
                    return e.tensor_scalar(stat[0:1, 512:1024],
                                           ps_v[0:1, 0:512],
                                           1.0 / DK, EPS, ALU.mult, ALU.add)
                prev = stage(prev, [("vector", mk_r1)])

                def mk_r2(e):
                    return e.tensor_scalar(
                        stat_i2[0:1, 0:1024], stat[0:1, 0:1024].bitcast(I32),
                        float(2.0 ** -23), -127.0, ALU.mult, ALU.add)
                prev = stage(prev, [("vector", mk_r2)])

                def mk_r3(e):
                    return e.tensor_scalar(stat_i3[0:1, 0:1024],
                                           stat_i2[0:1, 0:1024], -M23,
                                           127.0 * M23, ALU.mult, ALU.add)
                prev = stage(prev, [("vector", mk_r3)])

                def mk_r4(e):
                    return e.tensor_tensor(
                        stat2[0:1, 0:1024], stat[0:1, 0:1024],
                        stat_i3[0:1, 0:1024].bitcast(F32), ALU.mult)
                prev = stage(prev, [("vector", mk_r4)])

                def mk_r5(e):
                    return e.activation(stat2[0:1, 0:1024],
                                        stat2[0:1, 0:1024], AF.Ln)
                prev = stage(prev, [("scalar", mk_r5)])

                def mk_r6(e):
                    return e.scalar_tensor_tensor(
                        stat[0:1, 0:1024], stat_i2[0:1, 0:1024], LN2,
                        stat2[0:1, 0:1024], ALU.mult, ALU.add)
                prev = stage(prev, [("vector", mk_r6)])

                def mk_r7(e):
                    return e.activation(stat[0:1, 0:1024], stat[0:1, 0:1024],
                                        AF.Exp, scale=-0.5)
                prev = stage(prev, [("scalar", mk_r7)])

                def mk_rb(e):
                    e.matmul(ps_o[:, 0:512], ones_r, stat[0:1, 0:512],
                             start=True, stop=True)
                    return e.matmul(ps_q[:, 0:512], ones_r,
                                    stat[0:1, 512:1024], start=True,
                                    stop=True)
                prev = stage(prev, [("tensor", mk_rb)])

                def mk_oap(e, c0=c0):
                    e.scalar_tensor_tensor(
                        on4[:, 0:4, c0:c0 + CH], on4[:, 0:4, c0:c0 + CH],
                        rmsw[:, 0:1],
                        ps_o.rearrange("p (h t) -> p h t", h=4),
                        ALU.mult, ALU.mult)
                    return e.scalar_tensor_tensor(
                        on4[:, 4:8, c0:c0 + CH], on4[:, 4:8, c0:c0 + CH],
                        rmsw[:, 0:1],
                        ps_q.rearrange("p (h t) -> p h t", h=4),
                        ALU.mult, ALU.mult)
                prev = stage(prev, [("vector", mk_oap)])

            def mk_ds(e):
                for h in range(H):
                    ps = ps_s if h < 4 else ps_p
                    off = (h % 4) * CH
                    ins = e.matmul(ps[:, off:off + CH],
                                   kt_a[:, h * DK:(h + 1) * DK],
                                   vt[:, h * DK:(h + 1) * DK],
                                   start=True, stop=True)
                for h in range(H):
                    ps = ps_o if h < 4 else ps_q
                    off = (h % 4) * CH
                    ins = e.matmul(ps[:, off:off + CH],
                                   kt_b[:, h * DK:(h + 1) * DK],
                                   vt[:, h * DK:(h + 1) * DK],
                                   start=True, stop=True)
                return ins
            prev = stage(prev, [("tensor", mk_ds)])

            def mk_sadd(e):
                e.tensor_tensor(S_a[:, 0:512], S_a[:, 0:512], ps_s[:, 0:512],
                                ALU.add)
                e.tensor_tensor(S_a[:, 512:1024], S_a[:, 512:1024],
                                ps_p[:, 0:512], ALU.add)
                e.tensor_tensor(S_b[:, 0:512], S_b[:, 0:512], ps_o[:, 0:512],
                                ALU.add)
                return e.tensor_tensor(S_b[:, 512:1024], S_b[:, 512:1024],
                                       ps_q[:, 0:512], ALU.add)
            prev = stage(prev, [("vector", mk_sadd)])
            return prev

        NCHT = TILE // CH
        for ti in range(NTL):
            t0 = ti * TILE

            def mk_ldx(e, t0=t0):
                return e.dma_start(
                    out=xc.rearrange("p (c t) -> p c t", c=8),
                    in_=d["xpre_t"].rearrange("(c p) t -> p c t", p=128)
                    [:, :, t0:t0 + TILE])
            prev = stage(prev, [("sync", mk_ldx)])
            prev = emit_ln(prev, None, t0, TILE, xc, o_n, xh, do_load=False)
            prev = emit_proj_t(prev, wbA, kp)
            for cix in range(NCHT):
                prev = emit_qk_chunk(prev, cix, with_scores=False)
                prev = emit_v_chunk(prev, wbB, cix)
                prev = emit_scan_chunk(prev, cix, with_output=False)
        prev = dump(prev, "dbg_Sa", S_a, 1024)
        prev = dump(prev, "dbg_Sb", S_b, 1024)

        for ti in range(NTL):
            t0 = ti * TILE

            def mk_ldx(e, t0=t0):
                return e.dma_start(
                    out=xc.rearrange("p (c t) -> p c t", c=8),
                    in_=d["x_t"].rearrange("(c p) t -> p c t", p=128)
                    [:, :, t0:t0 + TILE])
            prev = stage(prev, [("sync", mk_ldx)])
            prev = emit_ln(prev, None, t0, TILE, xc, o_n, xh, do_load=False)
            prev = emit_wload(prev, d["wqf"], wbA)
            prev = emit_proj_t(prev, wbA, qp)
            prev = emit_wload(prev, d["wkf"], wbB)
            prev = emit_proj_t(prev, wbB, kp)
            prev = emit_wload(prev, d["wvT"], wbA)
            for cix in range(NCHT):
                prev = emit_qk_chunk(prev, cix, with_scores=True)
                prev = emit_v_chunk(prev, wbA, cix)
                prev = emit_scan_chunk(prev, cix, with_output=True)
            prev = emit_wload(prev, d["woT"], wbB)
            for dd in range(8):
                ps = ps_o if dd % 2 == 0 else ps_q

                def mk_wo(e, dd=dd, ps=ps):
                    for h in range(H):
                        ins = e.matmul(
                            ps[:, 0:TILE], wslice(wbB, h, dd),
                            o_n[:, h * TILE:(h + 1) * TILE],
                            start=(h == 0), stop=(h == 7))
                    return ins
                prev = stage(prev, [("tensor", mk_wo)])

                def mk_res(e, dd=dd, ps=ps):
                    return e.tensor_tensor(
                        xc[:, dd * TILE:(dd + 1) * TILE],
                        xc[:, dd * TILE:(dd + 1) * TILE],
                        ps[:, 0:TILE], ALU.add)
                prev = stage(prev, [("vector", mk_res)])

            def mk_st(e, t0=t0):
                return e.dma_start(
                    out=d["xb"].rearrange("(c p) t -> p c t", p=128)
                    [:, :, t0:t0 + TILE],
                    in_=xc.rearrange("p (c t) -> p c t", c=8))
            prev = stage(prev, [("gpsimd", mk_st)])

        sch.barrier()
        attn_stack.close()
        ffn_stack = contextlib.ExitStack()
        fsb = mksbuf(ffn_stack)
        QW = 512

        def qset(tag):
            s = {}
            for nm in "ABCDEFGH":
                s[nm] = fsb("QS" + nm + tag, [128, QW])
            for nm in ("IA", "IB", "IC"):
                s[nm] = fsb("Q" + nm + tag, [128, QW], I32)
            return s
        QS0, QS1 = qset("0"), qset("1")
        QSA, QSB, QSC, QSD = QS0["A"], QS0["B"], QS0["C"], QS0["D"]
        QSE, QSF, QSG, QSH = QS0["E"], QS0["F"], QS0["G"], QS0["H"]
        QIA, QIB, QIC = QS0["IA"], QS0["IB"], QS0["IC"]
        fx = fsb("fx", [128, 8 * FT])
        fxh = fsb("fxh", [128, 8 * FT])
        fzA = fsb("fzA", [128, FT])
        fzB = fsb("fzB", [128, FT])
        fyA = fsb("fyA", [128, FT])
        fyB = fsb("fyB", [128, FT])
        x2q = fsb("x2q", [128, 8 * FT], F8)
        hq = fsb("hq", [128, 32 * FT], F8)
        w1q = fsb("w1q", [128, 8 * DH], F8)
        w2q = fsb("w2q", [128, 32 * D], F8)
        msk8 = fsb("msk8", [128, 8 * FT], F8)
        wloadA = fsb("wloadA", [128, 512])
        wloadB = fsb("wloadB", [128, 512])

        def q_weight_chains(srcT, nchunk, cols, dst_fp8, k0):
            src3 = srcT.rearrange("(c p) n -> p c n", p=128)
            step = 512
            chains = []
            k = k0
            for c in range(nchunk):
                for s0 in range(0, cols, step):
                    S = QS0 if k % 2 == 0 else QS1
                    wld = wloadA if k % 2 == 0 else wloadB
                    ops = [("sync", (lambda c, s0, wld, src3:
                                     lambda e: e.dma_start(
                                         out=wld[:, 0:step],
                                         in_=src3[:, c, s0:s0 + step]))
                            (c, s0, wld, src3))]
                    ops += q_ops(wld[:, 0:step],
                                 dst_fp8[:, c * cols + s0:c * cols + s0
                                         + step],
                                 M=3, clamp=6, signed=True, ties=True,
                                 width=step, S=S)
                    chains.append(ops)
                    k += 1
            return chains

        prev = run_pipe(prev, q_weight_chains(d["w1T"], 8, DH, w1q, 0)
                        + q_weight_chains(d["w2T"], 32, D, w2q, 0))
        if debug:
            prev = stage(prev, [("gpsimd", lambda e: e.dma_start(
                out=d["dbg_w1qf"], in_=w1q))])
            prev = stage(prev, [("gpsimd", lambda e: e.dma_start(
                out=d["dbg_w2qf"], in_=w2q))])
        prev = dump(prev, "dbg_b1t", b1t, 32)
        if debug:
            W_ = 32
            z_, out_ = b1t, b1q
            l_, f_, w_ = QSA[:, :W_], QSB[:, :W_], QSC[:, :W_]
            tt_, r_ = QSE[:, :W_], QSF[:, :W_]
            bs_ = QSG[:, :W_]
            gE_, g_ = QIA[:, :W_], QIB[:, :W_]
            bEi_, p_ = QIC[:, :W_], QIA[:, :W_]
            tie_, odd_ = QSG[:, :W_], QSD[:, :W_]
            prev = stage(prev, [("scalar", lambda e: e.activation(tt_, z_, AF.Square))])
            prev = stage(prev, [("scalar", lambda e: e.activation(l_, tt_, AF.Ln, bias=cba("tiny")))])
            prev = stage(prev, [("scalar", lambda e: e.activation(gE_, l_, AF.Copy, scale=1.0 / (2 * LN2)))])
            prev = stage(prev, [("vector", lambda e: e.scalar_tensor_tensor(bEi_, gE_, 2 * LN2, l_, ALU.mult, ALU.is_gt))])
            prev = stage(prev, [("vector", lambda e: e.scalar_tensor_tensor(f_, bEi_, -1.0, gE_, ALU.mult, ALU.add))])
            prev = stage(prev, [("scalar", lambda e: e.activation(w_, f_, AF.Relu, bias=cba("c6")))])
            prev = stage(prev, [("vector", lambda e: e.tensor_scalar(gE_, w_, -M23, 136.0 * M23, ALU.mult, ALU.add))])
            def mkbis(e):
                return e.tensor_copy(QSD[:, 0:W_], gE_)
            prev = stage(prev, [("vector", mkbis)])
            prev = dump(prev, "dbg_bis", QSD, 32)
            prev = stage(prev, [("vector", lambda e: e.tensor_tensor(tt_, z_, gE_.bitcast(F32), ALU.mult))])
            prev = stage(prev, [("vector", lambda e: e.tensor_copy(g_, tt_))])
            prev = stage(prev, [("vector", lambda e: e.scalar_tensor_tensor(r_, g_, 0.5, tt_, ALU.add, ALU.is_le))])
            prev = dump(prev, "dbg_bb1", QSF, 32)
            prev = stage(prev, [("vector", lambda e: e.scalar_tensor_tensor(bs_, g_, -0.5, tt_, ALU.add, ALU.is_gt))])
            prev = dump(prev, "dbg_bb2", QSG, 32)
            prev = stage(prev, [("vector", lambda e: e.scalar_tensor_tensor(r_, bs_, -1.0, r_, ALU.mult, ALU.add))])
            prev = dump(prev, "dbg_bcomb", QSF, 32)
            prev = stage(prev, [("vector", lambda e: e.tensor_tensor(r_, r_, g_, ALU.add))])
            prev = stage(prev, [("vector", lambda e: e.scalar_tensor_tensor(tie_, r_, -0.5, tt_, ALU.add, ALU.is_equal))])
            prev = stage(prev, [("vector", lambda e: e.tensor_scalar(odd_, r_, 0.5, None, ALU.mult))])
            prev = stage(prev, [("vector", lambda e: e.tensor_copy(p_, odd_))])
            prev = stage(prev, [("vector", lambda e: e.tensor_tensor(odd_, odd_, p_, ALU.not_equal))])
            prev = stage(prev, [("vector", lambda e: e.tensor_tensor(tie_, tie_, odd_, ALU.mult))])
            prev = stage(prev, [("vector", lambda e: e.tensor_tensor(r_, r_, tie_, ALU.subtract))])
            prev = stage(prev, [("vector", lambda e: e.tensor_scalar(gE_, w_, M23, 118.0 * M23, ALU.mult, ALU.add))])
            prev = stage(prev, [("vector", lambda e: e.tensor_tensor(out_, r_, gE_.bitcast(F32), ALU.mult))])
        else:
            prev = run_serial(prev, q_ops(b1t, b1q, M=3, clamp=6, signed=True,
                                          ties=True, width=32))
        if debug:
            prev = dump(prev, "dbg_bf", QSB, 32)
            prev = dump(prev, "dbg_bw", QSC, 32)
            prev = dump(prev, "dbg_br", QSF, 32)
            prev = dump(prev, "dbg_bt", QSE, 32)
            def mkg(e):
                return e.tensor_copy(QSD[:, 0:32], QIB[:, 0:32])
            prev = stage(prev, [("vector", mkg)])
            prev = dump(prev, "dbg_bg", QSD, 32)
        prev = run_serial(prev, q_ops(b2t, b2q, M=3, clamp=6, signed=True,
                                      ties=True, width=8))

        if debug:
            def mkdw(e):
                return e.tensor_copy(QSA[:, 0:512], w1q[:, 0:512])
            prev = stage(prev, [("vector", mkdw)])
            prev = dump(prev, "dbg_w1q", QSA, 512)
            def mkdb(e):
                return e.tensor_copy(QSB[:, 0:32], b1q)
            prev = stage(prev, [("vector", mkdb)])
            prev = dump(prev, "dbg_b1q", QSB, 32)

        x2q3 = x2q.rearrange("p (c t) -> p c t", c=8)
        hq3 = hq.rearrange("p (c t) -> p c t", c=32)
        fx3 = fx.rearrange("p (c t) -> p c t", c=8)
        w1q3 = w1q.rearrange("p (c n) -> p c n", c=8)
        w2q3 = w2q.rearrange("p (c n) -> p c n", c=32)
        for ti in range(NFT):
            t0 = ti * FT
            prev = emit_ln(prev, d["xb"], t0, FT, fx, fxh, fxh,
                           dbg_ffn=(ti == 0))

            def mk_mld(e, t0=t0):
                e.dma_start(out=msk8.rearrange("p (c t) -> p c t", c=8),
                            in_=d["maskT"].rearrange("(c p) t -> p c t", p=128)
                            [:, :, t0:t0 + FT])
            prev = stage(prev, [("sync", mk_mld)])
            xchains = []
            for sl in range(8 * FT // 512):
                xchains.append(q_ops(
                    fxh[:, sl * 512:(sl + 1) * 512],
                    x2q[:, sl * 512:(sl + 1) * 512],
                    M=3, clamp=6, signed=True, ties=False, width=512,
                    S=(QS0 if sl % 2 == 0 else QS1)))
            prev = run_pipe(prev, xchains)
            if debug and ti == 0:
                prev = dump(prev, "dbg_fxh", fxh, 8 * FT)
                def mkd(e):
                    return e.tensor_copy(QSA[:, 0:512], x2q[:, 0:512])
                prev = stage(prev, [("vector", mkd)])
                prev = dump(prev, "dbg_x2q", QSA, 512)
            for grp in range(16):
                for sub in range(2):
                    dh = grp * 2 + sub
                    psx = ps_p if sub == 0 else ps_d
                    fzx = fzA if sub == 0 else fzB

                    def mk_m(e, dh=dh, psx=psx):
                        for c in range(8):
                            ins = e.matmul(psx[:, 0:FT],
                                           w1q3[:, c, dh * 128:(dh + 1) * 128],
                                           x2q3[:, c, :],
                                           start=(c == 0), stop=(c == 7))
                        return ins
                    prev = stage(prev, [("tensor", mk_m)])

                    def mk_e(e, dh=dh, psx=psx, fzx=fzx):
                        e.activation(fzx, psx[:, 0:FT], AF.Relu,
                                     bias=b1q[:, dh:dh + 1])
                    prev = stage(prev, [("scalar", mk_e)])
                if debug and ti == 0 and grp == 0:
                    prev = dump(prev, "dbg_fz", fzA, FT)
                prev = run_pipe(prev, [
                    q_h(fzA, hq[:, (2 * grp) * FT:(2 * grp + 1) * FT],
                        width=FT, S=QS0),
                    q_h(fzB, hq[:, (2 * grp + 1) * FT:(2 * grp + 2) * FT],
                        width=FT, S=QS1)])
                if debug and ti == 0 and grp == 0:
                    def mkd2(e):
                        return e.tensor_copy(QSA[:, 0:512], hq[:, 0:512])
                    prev = stage(prev, [("vector", mkd2)])
                    prev = dump(prev, "dbg_hq", QSA, 512)
            for dd in range(8):
                psy = ps_q if dd % 2 == 0 else ps_o
                fyx = fyA if dd % 2 == 0 else fyB
                yqx = fzA if dd % 2 == 0 else fzB

                def mk_m2(e, dd=dd, psy=psy):
                    for c in range(32):
                        ins = e.matmul(psy[:, 0:FT],
                                       w2q3[:, c, dd * 128:(dd + 1) * 128],
                                       hq3[:, c, :],
                                       start=(c == 0), stop=(c == 31))
                    return ins
                prev = stage(prev, [("tensor", mk_m2)])

                def mk_e2(e, dd=dd, psy=psy, fyx=fyx):
                    return e.tensor_scalar(fyx, psy[:, 0:FT],
                                           b2q[:, dd:dd + 1], None, ALU.add)
                prev = stage(prev, [("vector", mk_e2)])
                prev = run_pipe(prev, [
                    q_ops(fyx[:, 0:256], yqx[:, 0:256], M=10, clamp=14,
                          signed=True, ties=False, width=256, S=QS0),
                    q_ops(fyx[:, 256:512], yqx[:, 256:512], M=10, clamp=14,
                          signed=True, ties=False, width=256, S=QS1)])

                def mk_dr1(e, dd=dd, fyx=fyx, yqx=yqx):
                    return e.tensor_tensor(
                        fyx, yqx,
                        msk8.rearrange("p (c t) -> p c t", c=8)[:, dd, :],
                        ALU.mult)
                prev = stage(prev, [("vector", mk_dr1)])

                def mk_dr2(e, dd=dd, fyx=fyx):
                    return e.tensor_tensor(fyx, fyx, fx3[:, dd, :], ALU.add)
                prev = stage(prev, [("vector", mk_dr2)])
                if debug and ti == 0 and dd == 0:
                    prev = dump(prev, "dbg_fy", fyA, FT)

                def mk_out(e, dd=dd, t0=t0, fyx=fyx):
                    e.dma_start(
                        out=d["out_t"].rearrange("(c p) t -> p c t", p=128)
                        [:, dd, t0:t0 + FT], in_=fyx)
                prev = stage(prev, [("gpsimd", mk_out)])

        ffn_stack.close()

        sch.schedule()
        if _SCHED_HOOK is not None:
            _SCHED_HOOK(sch)
        final = sch.inc_total

        class _Replay:
            def __init__(self, eng, oplist, sems):
                self._e = eng
                self._ops = oplist
                self._i = 0
                self._sems = sems

            def __getattr__(self, nm):
                f = getattr(self._e, nm)
                if not callable(f):
                    return f

                def g(*a, **k):
                    op = self._ops[self._i]
                    self._i += 1
                    for pe, v in op.waits:
                        self._e.wait_ge(self._sems[pe], v)
                    r = f(*a, **k)
                    if op.inc_id is not None and hasattr(r, "then_inc"):
                        r.then_inc(self._sems[op.engine],
                                   16 if op.is_dma else 1)
                    return r
                return g

        def engine_runner(ename):
            def run(eng):
                for fni, (e2, fn, bar) in enumerate(sch.fns):
                    if e2 != ename:
                        continue
                    fn(_Replay(eng, sch.op_by_fn[fni], sems))
                if ename == "gpsimd":
                    for e2 in ENGINES:
                        if final[e2] > 0:
                            eng.wait_ge(sems[e2], final[e2])
            return run

        with nc.Block() as block:
            block.sync(engine_runner("sync"))
            block.scalar(engine_runner("scalar"))
            block.vector(engine_runner("vector"))
            block.tensor(engine_runner("tensor"))
            block.gpsimd(engine_runner("gpsimd"))

    return nc


def _dropout_mask_np():
    import jax
    cpu = jax.devices("cpu")[0]
    with jax.default_device(cpu):
        key = jax.random.key(42, impl="threefry2x32")
        keep = jax.random.bernoulli(key, 0.8, (4, 8192, 1024))
        keep = np.asarray(keep)
    return keep


def kernel(L0_attn_x, L0_attn_Wq, L0_attn_Wk, L0_attn_Wv, L0_attn_Wo,
           L0_attn_fmq_w, L0_attn_fmq_b, L0_attn_fmk_w, L0_attn_fmk_b,
           L0_attn_rms_w, L0_ffn_w1, L0_ffn_b1, L0_ffn_w2, L0_ffn_b2):
    x = np.ascontiguousarray(np.asarray(L0_attn_x, dtype=np.float32))
    B, N, _ = x.shape
    keep = _dropout_mask_np()
    maskmul = np.where(keep, np.float32(1.25), np.float32(0.0))
    maskmul8 = maskmul.astype(ml_dtypes.float8_e4m3)

    wqT = np.ascontiguousarray(np.asarray(L0_attn_Wq).T)
    wkT = np.ascontiguousarray(np.asarray(L0_attn_Wk).T)
    wvT = np.ascontiguousarray(np.asarray(L0_attn_Wv).T)
    woT = np.ascontiguousarray(np.asarray(L0_attn_Wo).T)
    fmqT = np.ascontiguousarray(np.asarray(L0_attn_fmq_w).T)
    fmkT = np.ascontiguousarray(np.asarray(L0_attn_fmk_w).T)
    w1T = np.ascontiguousarray(np.asarray(L0_ffn_w1).T)
    w2T = np.ascontiguousarray(np.asarray(L0_ffn_w2).T)

    triu = np.triu(np.ones((CH, CH), dtype=np.float32))
    ident = np.eye(128, dtype=np.float32)
    wq_raw = np.ascontiguousarray(np.asarray(L0_attn_Wq, dtype=np.float32))
    wk_raw = np.ascontiguousarray(np.asarray(L0_attn_Wk, dtype=np.float32))
    fmqb_c = np.ascontiguousarray(
        np.asarray(L0_attn_fmq_b, dtype=np.float32).reshape(DK, 1))
    fmkb_c = np.ascontiguousarray(
        np.asarray(L0_attn_fmk_b, dtype=np.float32).reshape(DK, 1))
    fmkbr_c = np.ascontiguousarray(
        np.asarray(L0_attn_fmk_b, dtype=np.float32).reshape(1, DK))
    rmsw_c = np.ascontiguousarray(
        np.asarray(L0_attn_rms_w, dtype=np.float32).reshape(DK, 1))
    b1_c = np.ascontiguousarray(
        np.asarray(L0_ffn_b1, dtype=np.float32).reshape(32, 128).T)
    b2_c = np.ascontiguousarray(
        np.asarray(L0_ffn_b2, dtype=np.float32).reshape(8, 128).T)
    onesc = np.ones((128, 1), dtype=np.float32)
    zeros_pre = np.zeros((D, T), dtype=np.float32)

    nc = build_program()
    in_maps = []
    for c in range(8):
        b, half = c // 2, c % 2
        tok = slice(half * T, (half + 1) * T)
        in_maps.append({
            "x_t": np.ascontiguousarray(x[b, tok, :].T),
            "xpre_t": (zeros_pre if half == 0 else
                       np.ascontiguousarray(x[b, 0:T, :].T)),
            "wqT": wqT, "wkT": wkT, "wvT": wvT, "woT": woT,
            "fmqT": fmqT, "fmkT": fmkT,
            "fmq_b": fmqb_c, "fmk_b": fmkb_c, "fmk_br": fmkbr_c,
            "rms_w": rmsw_c,
            "w1T": w1T, "b1": b1_c,
            "w2T": w2T, "b2": b2_c,
            "maskT": np.ascontiguousarray(maskmul8[b, tok, :].T),
            "triu": triu, "ones": onesc, "ident": ident,
            "wq_raw": wq_raw, "wk_raw": wk_raw,
        })
    res = run_bass_kernel_spmd(nc, in_maps, core_ids=list(range(8)))
    out = np.empty((B, N, D), dtype=np.float32)
    for c in range(8):
        b, half = c // 2, c % 2
        out[b, half * T:(half + 1) * T, :] = \
            np.asarray(res.results[c]["out_t"]).T
    return out


# version 09
# speedup vs baseline: 2.1212x; 1.2002x over previous
import contextlib
import numpy as np
import ml_dtypes

import concourse.bass as bass
import concourse.mybir as mybir
from concourse.bass_utils import run_bass_kernel_spmd

F32 = mybir.dt.float32
F8 = mybir.dt.float8e4
I32 = mybir.dt.int32
AF = mybir.ActivationFunctionType
ALU = mybir.AluOpType

LN2 = float(np.log(2.0))
M23 = 8388608.0

D = 1024
H = 8
DK = 128
T = 4096
CH = 128
NCH = T // CH
TILE = 512
NTL = T // TILE
FT = 512
NFT = T // FT
DH = 4096
EPS = 1e-5

ENGINES = ("sync", "scalar", "vector", "tensor", "gpsimd")
DMA_ENG = ("sync", "gpsimd")


class OpRec:
    __slots__ = ("engine", "fnidx", "callidx", "reads", "writes", "inc_id",
                 "waits", "is_dma", "_deps")

    def __init__(self, engine, fnidx, callidx, reads, writes, is_dma):
        self.engine = engine
        self.fnidx = fnidx
        self.callidx = callidx
        self.reads = reads
        self.writes = writes
        self.is_dma = is_dma
        self.inc_id = None
        self.waits = []


_ROLES = {
    "matmul": ((0,), (1, 2)),
    "transpose": ((0,), (1, 2)),
    "activation": ((0,), (1, 3, 4)),
    "tensor_copy": ((0,), (1,)),
    "tensor_tensor": ((0,), (1, 2)),
    "tensor_scalar": ((0,), (1, 2, 3)),
    "tensor_single_scalar": ((0,), (1, 2)),
    "scalar_tensor_tensor": ((0,), (1, 2, 3)),
    "memset": ((0,), ()),
    "dma_start": ((), ()),
    "tensor_add": ((0,), (1, 2)),
    "tensor_sub": ((0,), (1, 2)),
    "tensor_mul": ((0,), (1, 2)),
}


def _ap_name(a):
    t = getattr(a, "tensor", None)
    if t is None:
        return None
    return getattr(t, "name", None)


class _Recorder:

    class _Ins:
        def then_inc(self, *a, **k):
            return self

    def __init__(self, engine, sink):
        self._engine = engine
        self._sink = sink

    def __getattr__(self, nm):
        def call(*args, **kwargs):
            roles = _ROLES.get(nm)
            reads, writes = set(), set()
            if roles is None:
                for a in list(args) + list(kwargs.values()):
                    n = _ap_name(a)
                    if n:
                        reads.add(n)
                        writes.add(n)
            else:
                for i, a in enumerate(args):
                    n = _ap_name(a)
                    if not n:
                        continue
                    if i in roles[0]:
                        writes.add(n)
                    else:
                        reads.add(n)
                for k, a in kwargs.items():
                    n = _ap_name(a)
                    if not n:
                        continue
                    if k == "out":
                        writes.add(n)
                    else:
                        reads.add(n)
            self._sink.append((nm, reads, writes, nm == "dma_start"))
            return _Recorder._Ins()
        return call


class Sched:

    def __init__(self):
        self.fns = []
        self.ops = []
        self.per_engine_n = {e: 0 for e in ENGINES}
        self.done = {e: 0 for e in ENGINES}
        self._pending_barrier = False

    def stage(self, deps, items):
        for e, fn in items:
            self.fns.append((e, fn, self._pending_barrier))
            self._pending_barrier = False
        return dict(self.done)

    def snap(self):
        return dict(self.done)

    def barrier(self):
        self._pending_barrier = True

    def schedule(self):
        recs = []
        for e, fn, bar in self.fns:
            sink = []
            fn(_Recorder(e, sink))
            recs.append(sink)
        ops = []
        last_write = {}
        readers = {}
        barrier_join = {}
        last_op_eng = {e: None for e in ENGINES}
        for (e, fn, bar), sink in zip(self.fns, recs):
            if bar:
                barrier_join = {k: v for k, v in last_op_eng.items()
                                if v is not None}
            for callidx, (nm, reads, writes, is_dma) in enumerate(sink):
                op = OpRec(e, len(ops), callidx, reads, writes, is_dma)
                deps = list(barrier_join.values())
                for r in reads:
                    o = last_write.get(r)
                    if o is not None:
                        deps.append(o)
                for w in writes:
                    o = last_write.get(w)
                    if o is not None:
                        deps.append(o)
                    deps.extend(readers.get(w, ()))
                op._deps = deps
                for w in writes:
                    last_write[w] = op
                    readers[w] = []
                for r in reads:
                    readers.setdefault(r, []).append(op)
                ops.append(op)
                last_op_eng[e] = op
        needs_inc = [False] * len(ops)
        for op in ops:
            for d in op._deps:
                if d.engine == op.engine == "tensor":
                    continue
                needs_inc[d.fnidx] = True
        inc_total = {e: 0 for e in ENGINES}
        inc_cum = [0] * len(ops)
        for i, op in enumerate(ops):
            if needs_inc[i] or op.is_dma:
                inc_total[op.engine] += 16 if op.is_dma else 1
                op.inc_id = inc_total[op.engine]
            inc_cum[i] = inc_total[op.engine]
        wm = {e: {e2: 0 for e2 in ENGINES} for e in ENGINES}
        dma_ckpt = {e: 0 for e in ENGINES}
        for op in ops:
            tgt = {}
            if op.is_dma and dma_ckpt[op.engine] > 0:
                tgt[op.engine] = dma_ckpt[op.engine]
            for d in op._deps:
                if d.engine == op.engine == "tensor":
                    continue
                v = d.inc_id if d.inc_id is not None else inc_cum[d.fnidx]
                if v and v > tgt.get(d.engine, 0):
                    tgt[d.engine] = v
            for pe, v in tgt.items():
                if v > wm[op.engine][pe]:
                    op.waits.append((pe, v))
                    wm[op.engine][pe] = v
            if op.is_dma and needs_inc[op.fnidx]:
                dma_ckpt[op.engine] = op.inc_id
        self.ops = ops
        self.inc_total = inc_total
        self.op_by_fn = {}
        idx = 0
        for fni, ((e, fn, bar), sink) in enumerate(zip(self.fns, recs)):
            self.op_by_fn[fni] = ops[idx: idx + len(sink)]
            idx += len(sink)


_SCHED_HOOK = None


def build_program(debug=False):
    nc = bass.Bass("TRN2", target_bir_lowering=False)

    d = {}
    for nm, shape in [
        ("x_t", [D, T]), ("xpre_t", [D, T]),
        ("wqT", [D, D]), ("wkT", [D, D]), ("wvT", [D, D]), ("woT", [D, D]),
        ("fmqT", [DK, DK]), ("fmkT", [DK, DK]),
        ("fmq_b", [DK, 1]), ("fmk_b", [DK, 1]), ("fmk_br", [1, DK]),
        ("rms_w", [DK, 1]),
        ("w1T", [D, DH]), ("b1", [128, 32]), ("w2T", [DH, D]),
        ("b2", [128, 8]),
        ("triu", [CH, CH]), ("ones", [128, 1]),
        ("wq_raw", [D, D]), ("wk_raw", [D, D]), ("ident", [128, 128]),
    ]:
        d[nm] = nc.dram_tensor(nm, shape, F32, kind="ExternalInput").ap()
    for nm in ("wqf", "wkf"):
        d[nm] = nc.dram_tensor(nm, [D, D], F32, kind="Internal").ap()
    d["w1q8"] = nc.dram_tensor("w1q8", [D, DH], F8, kind="Internal").ap()
    d["w2q8"] = nc.dram_tensor("w2q8", [DH, D], F8, kind="Internal").ap()
    d["maskT"] = nc.dram_tensor("maskT", [D, T], F8, kind="ExternalInput").ap()
    d["out_t"] = nc.dram_tensor("out_t", [D, T], F32, kind="ExternalOutput").ap()
    d["xb"] = nc.dram_tensor("xb", [D, T], F32,
                             kind="ExternalOutput" if debug else "Internal").ap()
    if debug:
        for nm, shape, dtp in [
            ("dbg_xh", [128, 1024], F32), ("dbg_q", [128, 1024], F32),
            ("dbg_qfa", [128, 1024], F32), ("dbg_kfa", [128, 1024], F32),
            ("dbg_kta", [128, 1024], F32), ("dbg_vt", [128, 1024], F32),
            ("dbg_sc", [128, 128], F32), ("dbg_on", [128, 1024], F32),
            ("dbg_Sa", [128, 1024], F32), ("dbg_Sb", [128, 1024], F32),
            ("dbg_fxh", [128, 4096], F32), ("dbg_x2q", [128, 4096], F32),
            ("dbg_fz", [128, 2048], F32), ("dbg_hq", [128, 4096], F32),
            ("dbg_fy", [128, 512], F32),
            ("dbg_opre", [128, 128], F32), ("dbg_osq", [128, 128], F32),
            ("dbg_stat", [1, 1024], F32),
            ("dbg_qfb", [128, 1024], F32), ("dbg_kfb", [128, 1024], F32),
            ("dbg_fmqb", [128, 2], F32),
            ("dbg_w1q", [128, 1024], F32), ("dbg_b1q", [128, 32], F32),
            ("dbg_w1qf", [128, 8 * DH], F8), ("dbg_w2qf", [128, 32 * D], F8),
            ("dbg_ffx", [128, 4096], F32), ("dbg_fstat", [1, 1024], F32),
            ("dbg_fstat2", [1, 1024], F32), ("dbg_fpsm", [128, 512], F32),
            ("dbg_fpsr", [128, 512], F32),
            ("dbg_b1t", [128, 32], F32), ("dbg_bf", [128, 32], F32),
            ("dbg_bw", [128, 32], F32), ("dbg_br", [128, 32], F32),
            ("dbg_bt", [128, 32], F32), ("dbg_bg", [128, 32], F32),
            ("dbg_bis", [128, 32], F32), ("dbg_bb1", [128, 32], F32),
            ("dbg_bb2", [128, 32], F32), ("dbg_bcomb", [128, 32], F32),
            ("dbg_sch", [128, 8 * 128], F32),
            ("dbg_oph", [128, 8 * 128], F32),
        ]:
            d[nm] = nc.dram_tensor(nm, shape, dtp,
                                   kind="ExternalOutput").ap()

    with contextlib.ExitStack() as st:
        def mksbuf(stack):
            def sbuf(name, shape, dt=F32):
                return stack.enter_context(nc.sbuf_tensor(name, shape, dt)).ap()
            return sbuf

        sbuf = mksbuf(st)

        def psum(name, shape):
            return st.enter_context(nc.psum_tensor(name, shape, F32)).ap()

        cb = sbuf("cb", [128, 16])
        CIDX = {"tiny": 0, "eps": 1, "c6": 2, "c14": 3}
        CVALS = {"tiny": 1e-37, "eps": EPS, "c6": 6.0, "c14": 14.0}

        def cba(nm, row128=True):
            i = CIDX[nm]
            return cb[:, i:i + 1] if row128 else cb[0:1, i:i + 1]

        fmq = sbuf("fmq", [128, DK])
        fmk = sbuf("fmk", [128, DK])
        fmqb = sbuf("fmqb", [128, 2])
        fmkb = sbuf("fmkb", [128, 2])
        fmkb_row = sbuf("fmkb_row", [1, DK])
        rmsw = sbuf("rmsw", [128, 1])
        b1t = sbuf("b1t", [128, 32])
        b2t = sbuf("b2t", [128, 8])
        b1q = sbuf("b1q", [128, 32])
        b2q = sbuf("b2q", [128, 8])
        triu = sbuf("triu_s", [128, CH])
        ones = sbuf("ones_s", [128, 1])
        ones_r = sbuf("ones_r", [1, 128])
        fmkb_bc = sbuf("fmkb_bc", [128, DK])
        S_a = sbuf("S_a", [128, H * DK])
        S_b = sbuf("S_b", [128, H * DK])
        stat = sbuf("stat", [1, 1024])
        stat2 = sbuf("stat2", [1, 1024])
        stat_i2 = sbuf("stat_i2", [1, 1024], I32)
        stat_i3 = sbuf("stat_i3", [1, 1024], I32)

        attn_stack = contextlib.ExitStack()
        asb = mksbuf(attn_stack)
        wbA = asb("wbA", [128, 8 * D])
        wbB = asb("wbB", [128, 8 * D])
        xc = asb("xc", [128, 8 * TILE])
        xh = asb("xh", [128, 8 * TILE])
        qp = asb("qp", [128, 8 * TILE])
        kp = asb("kp", [128, 8 * TILE])
        qfc_a = asb("qfc_a", [128, H * CH])
        qfc_b = asb("qfc_b", [128, H * CH])
        yt_w = qfc_a
        kfc_a = asb("kfc_a", [128, H * CH])
        kfc_b = asb("kfc_b", [128, H * CH])
        kt_a = asb("kt_a", [128, H * DK])
        kt_b = asb("kt_b", [128, H * DK])
        vt = asb("vt", [128, D])
        sc_w = asb("sc_w", [128, H * CH])
        o_n = asb("o_n", [128, H * TILE])
        idn = asb("idn", [128, 128])
        wpre = [asb("wpre0", [128, 128]), asb("wpre1", [128, 128])]
        wout = [asb("wout0", [128, 128]), asb("wout1", [128, 128])]
        ps_p = psum("ps_p", [128, 512])
        ps_q = psum("ps_q", [128, 512])
        ps_v = psum("ps_v", [128, 1024])
        ps_s = psum("ps_s", [128, 512])
        ps_o = psum("ps_o", [128, 512])
        ps_d = psum("ps_d", [128, 512])

        sems = {e: st.enter_context(nc.semaphore("sem_" + e)) for e in ENGINES}

        nc.const_aps.register(F32, -0.125, cb[:, 7:8])
        nc.const_aps.register(F32, -4 * LN2, cb[:, 6:7])
        nc.const_aps.register(F32, 136.0 * M23, cb[:, 8:9])
        nc.const_aps.register(F32, 118.0 * M23, cb[:, 9:10])
        nc.const_aps.register(F32, 112.0 * M23, cb[:, 10:11])

        sch = Sched()

        def stage(deps, items):
            return sch.stage(deps, items)

        items = [("vector", (lambda i, v: lambda e: e.memset(
            cb[:, i:i + 1], v))(CIDX[nm], CVALS[nm])) for nm in CIDX]
        items.append(("vector", lambda e: e.memset(cb[:, 7:8], -0.125)))
        items.append(("vector", lambda e: e.memset(cb[:, 6:7], -4 * LN2)))
        items.append(("vector", lambda e: e.memset(cb[:, 8:9], 136.0 * M23)))
        items.append(("vector", lambda e: e.memset(cb[:, 9:10], 118.0 * M23)))
        items.append(("vector", lambda e: e.memset(cb[:, 10:11],
                                                   112.0 * M23)))
        items.append(("vector", lambda e: e.memset(ones_r, 1.0)))
        items.append(("vector", lambda e: e.memset(S_a, 0.0)))
        items.append(("vector", lambda e: e.memset(S_b, 0.0)))
        loads = [
            (fmq, d["fmqT"]), (fmk, d["fmkT"]),
            (triu, d["triu"]), (ones, d["ones"]),
            (rmsw, d["rms_w"]),
            (fmqb[:, 0:1], d["fmq_b"]),
            (fmkb[:, 0:1], d["fmk_b"]),
            (fmkb_row, d["fmk_br"]),
            (b1t, d["b1"]),
            (b2t, d["b2"]),
            (idn, d["ident"]),
            (wbB.rearrange("p (c n) -> p c n", c=8),
             d["wvT"].rearrange("(c p) n -> p c n", p=128)),
        ]
        for dst, src in loads:
            items.append(("sync", (lambda dst, src: lambda e: e.dma_start(
                out=dst, in_=src))(dst, src)))
        prev = stage(sch.snap(), items)

        prev = stage(prev, [("scalar", lambda e: e.activation(
            fmqb[:, 1:2], fmqb[:, 0:1], AF.Copy, scale=-1.0,
            bias=-4 * LN2))])
        prev = stage(prev, [("scalar", lambda e: e.activation(
            fmqb[:, 0:1], fmqb[:, 0:1], AF.Copy, scale=1.0,
            bias=-4 * LN2))])
        prev = stage(prev, [("scalar", lambda e: e.activation(
            fmkb[:, 1:2], fmkb[:, 0:1], AF.Copy, scale=-1.0))])
        sch.barrier()
        prev = stage(prev, [("tensor", lambda e: e.matmul(
            ps_v[:, 0:DK], ones_r, fmkb_row, start=True, stop=True))])
        prev = stage(prev, [("scalar", lambda e: e.activation(
            fmkb_bc, ps_v[:, 0:DK], AF.Copy))])

        for raw, fmw, dst in ((d["wq_raw"], fmq, d["wqf"]),
                              (d["wk_raw"], fmk, d["wkf"])):
            for h in range(H):
                for jc in range(8):
                    pi = (h * 8 + jc) % 2
                    wp, wo_, psx = wpre[pi], wout[pi], (ps_s if pi == 0
                                                        else ps_d)

                    def mk_ld(e, raw=raw, h=h, jc=jc, wp=wp):
                        return e.dma_start(
                            out=wp, in_=raw[h * DK:(h + 1) * DK,
                                            jc * 128:(jc + 1) * 128])
                    prev = stage(prev, [("sync", mk_ld)])

                    def mk_mm(e, wp=wp, fmw=fmw, psx=psx):
                        return e.matmul(psx[:, 0:128], wp, fmw,
                                        start=True, stop=True)
                    prev = stage(prev, [("tensor", mk_mm)])

                    def mk_ev(e, wo_=wo_, psx=psx):
                        return e.activation(wo_, psx[:, 0:128], AF.Copy)
                    prev = stage(prev, [("scalar", mk_ev)])

                    def mk_st(e, dst=dst, h=h, jc=jc, wo_=wo_):
                        return e.dma_start(
                            out=dst[jc * 128:(jc + 1) * 128,
                                    h * DK:(h + 1) * DK], in_=wo_)
                    prev = stage(prev, [("gpsimd", mk_st)])
        def mk_wkf(e):
            return e.dma_start(
                out=wbA.rearrange("p (c n) -> p c n", c=8),
                in_=d["wkf"].rearrange("(c p) n -> p c n", p=128))
        prev = stage(prev, [("sync", mk_wkf)])

        def wslice(wt, c, dd):
            return wt[:, c * D + dd * 128: c * D + (dd + 1) * 128]

        def dump(prev, name, src_tile, width):
            if not debug:
                return prev
            def mk(e, name=name, src_tile=src_tile, width=width):
                return e.dma_start(out=d[name][:, 0:width],
                                   in_=src_tile[:, 0:width])
            return stage(prev, [("gpsimd", mk)])

        def q_ops(z, out_q, M, clamp, signed, ties, width, S=None):
            S = S if S is not None else QS0
            ops = []
            onv = lambda f: ops.append(("vector", f))
            ons = lambda f: ops.append(("scalar", f))
            l, f, w = S["A"][:, :width], S["B"][:, :width], S["C"][:, :width]
            tt, r = S["E"][:, :width], S["F"][:, :width]
            bs = S["G"][:, :width]
            isx, sx = S["D"][:, :width], S["H"][:, :width]
            gE, g = S["IA"][:, :width], S["IB"][:, :width]
            bEi, p = S["IC"][:, :width], S["IA"][:, :width]
            tie, odd = S["G"][:, :width], S["D"][:, :width]
            if signed:
                ons(lambda e: e.activation(tt, z, AF.Square))
                ons(lambda e: e.activation(l, tt, AF.Ln, bias=cba("tiny")))
                c, inv = 1.0 / (2 * LN2), 2 * LN2
            else:
                ons(lambda e: e.activation(l, z, AF.Ln, bias=cba("tiny")))
                c, inv = 1.0 / LN2, LN2
            ons((lambda c: lambda e: e.activation(gE, l, AF.Copy, scale=c))(c))
            onv((lambda inv: lambda e: e.scalar_tensor_tensor(
                bEi, gE, inv, l, ALU.mult, ALU.is_gt))(inv))
            onv(lambda e: e.scalar_tensor_tensor(
                f, bEi, -1.0, gE, ALU.mult, ALU.add))
            ons((lambda cl: lambda e: e.activation(
                w, f, AF.Relu, bias=cba("c%d" % cl)))(clamp))
            if ties:
                ons((lambda MF: lambda e: e.activation(
                    gE, w, AF.Copy, scale=-M23, bias=MF * M23))(
                        float(127 + M + clamp)))
                onv(lambda e: e.tensor_tensor(tt, z, gE.bitcast(F32),
                                              ALU.mult))
            else:
                ons(lambda e: e.activation(isx, w, AF.Exp, scale=-LN2))
                onv((lambda s_: lambda e: e.scalar_tensor_tensor(
                    tt, z, s_, isx, ALU.mult, ALU.mult))(
                        float(2.0 ** (M + clamp))))
            if signed:
                ons(lambda e: e.activation(g, tt, AF.Copy))
                onv(lambda e: e.scalar_tensor_tensor(
                    r, g, 0.5, tt, ALU.add, ALU.is_le))
                onv(lambda e: e.scalar_tensor_tensor(
                    bs, g, -0.5, tt, ALU.add, ALU.is_gt))
                onv(lambda e: e.scalar_tensor_tensor(
                    r, bs, -1.0, r, ALU.mult, ALU.add))
                onv(lambda e: e.tensor_tensor(r, r, g, ALU.add))
            else:
                ons(lambda e: e.activation(g, tt, AF.Copy, bias=-0.125))
                onv(lambda e: e.scalar_tensor_tensor(
                    r, g, 0.5, tt, ALU.add, ALU.is_le))
                onv(lambda e: e.tensor_tensor(r, r, g, ALU.add))
            if ties:
                onv(lambda e: e.scalar_tensor_tensor(
                    tie, r, -0.5, tt, ALU.add, ALU.is_equal))
                ons(lambda e: e.activation(odd, r, AF.Copy, scale=0.5))
                ons(lambda e: e.activation(p, odd, AF.Copy))
                onv(lambda e: e.tensor_tensor(odd, odd, p, ALU.not_equal))
                onv(lambda e: e.tensor_tensor(tie, tie, odd, ALU.mult))
                onv(lambda e: e.tensor_tensor(r, r, tie, ALU.subtract))
                ons((lambda MF: lambda e: e.activation(
                    gE, w, AF.Copy, scale=M23, bias=MF * M23))(
                        float(127 - M - clamp)))
                onv(lambda e: e.tensor_tensor(
                    out_q, r, gE.bitcast(F32), ALU.mult))
            else:
                ons(lambda e: e.activation(sx, w, AF.Exp, scale=LN2))
                onv((lambda s_: lambda e: e.scalar_tensor_tensor(
                    out_q, r, s_, sx, ALU.mult, ALU.mult))(
                        float(2.0 ** (-(M + clamp)))))
            return ops

        def q_h(z, out_q, width, S=None):
            S = S if S is not None else QS0
            ops = []
            onv = lambda f: ops.append(("vector", f))
            ons = lambda f: ops.append(("scalar", f))
            l, f = S["A"][:, :width], S["B"][:, :width]
            w14, w6 = S["C"][:, :width], S["D"][:, :width]
            tt, r510 = S["F"][:, :width], S["E"][:, :width]
            r43 = S["G"][:, :width]
            isx, sx = S["H"][:, :width], S["H"][:, :width]
            gE, g = S["IA"][:, :width], S["IB"][:, :width]
            bEi, p = S["IC"][:, :width], S["IA"][:, :width]
            tie, odd = S["A"][:, :width], S["B"][:, :width]
            ons(lambda e: e.activation(l, z, AF.Ln, bias=cba("tiny")))
            ons(lambda e: e.activation(gE, l, AF.Copy, scale=1.0 / LN2))
            onv(lambda e: e.scalar_tensor_tensor(
                bEi, gE, LN2, l, ALU.mult, ALU.is_gt))
            onv(lambda e: e.scalar_tensor_tensor(
                f, bEi, -1.0, gE, ALU.mult, ALU.add))
            ons(lambda e: e.activation(w14, f, AF.Relu, bias=cba("c14")))
            ons(lambda e: e.activation(w6, f, AF.Relu, bias=cba("c6")))
            ons(lambda e: e.activation(isx, w14, AF.Exp, scale=-LN2))
            onv(lambda e: e.scalar_tensor_tensor(
                tt, z, float(2.0 ** 24), isx, ALU.mult, ALU.mult))
            ons(lambda e: e.activation(g, tt, AF.Copy, bias=-0.125))
            onv(lambda e: e.scalar_tensor_tensor(
                r510, g, 0.5, tt, ALU.add, ALU.is_le))
            onv(lambda e: e.tensor_tensor(r510, r510, g, ALU.add))
            onv(lambda e: e.tensor_tensor(w14, w14, w6, ALU.subtract))
            ons(lambda e: e.activation(gE, w14, AF.Copy, scale=M23,
                                       bias=112.0 * M23))
            onv(lambda e: e.tensor_tensor(tt, r510, gE.bitcast(F32),
                                          ALU.mult))
            ons(lambda e: e.activation(g, tt, AF.Copy, bias=-0.125))
            onv(lambda e: e.scalar_tensor_tensor(
                r43, g, 0.5, tt, ALU.add, ALU.is_le))
            onv(lambda e: e.tensor_tensor(r43, r43, g, ALU.add))
            onv(lambda e: e.scalar_tensor_tensor(
                tie, r43, -0.5, tt, ALU.add, ALU.is_equal))
            ons(lambda e: e.activation(odd, r43, AF.Copy, scale=0.5))
            ons(lambda e: e.activation(p, odd, AF.Copy))
            onv(lambda e: e.tensor_tensor(odd, odd, p, ALU.not_equal))
            onv(lambda e: e.tensor_tensor(tie, tie, odd, ALU.mult))
            onv(lambda e: e.tensor_tensor(r43, r43, tie, ALU.subtract))
            ons(lambda e: e.activation(sx, w6, AF.Exp, scale=LN2))
            onv(lambda e: e.scalar_tensor_tensor(
                out_q, r43, float(2.0 ** -9), sx, ALU.mult, ALU.mult))
            return ops

        def run_serial(prev, ops):
            for eng, f in ops:
                prev = stage(prev, [(eng, f)])
            return prev

        def run_pipe(prev, chains, depth=4):
            queue = list(chains)
            active = []
            while active or queue:
                if not active:
                    active.append([queue.pop(0), 0])
                for ch in list(active):
                    prev = stage(prev, [ch[0][ch[1]]])
                    ch[1] += 1
                    if ch[1] >= len(ch[0]):
                        active.remove(ch)
                if queue and len(active) < depth:
                    nw = active[-1] if active else None
                    if nw is None or nw[1] >= max(1, len(nw[0]) // depth):
                        active.append([queue.pop(0), 0])
            return prev

        def emit_ln(prev, src_ap, t0, width, x_tile, sq_tile, out_tile,
                    do_load=True, dbg_ffn=False):
            x3 = x_tile.rearrange("p (c t) -> p c t", c=8)
            sq3 = sq_tile.rearrange("p (c t) -> p c t", c=8)
            o3 = out_tile.rearrange("p (c t) -> p c t", c=8)
            if do_load:
                prev = stage(prev, [("sync", (lambda: lambda e: e.dma_start(
                    out=x3, in_=src_ap.rearrange("(c p) t -> p c t", p=128)
                    [:, :, t0:t0 + width]))())])
            prev = stage(prev, [("scalar", lambda e: e.activation(
                sq_tile, x_tile, AF.Square))])

            def mk_stats(e):
                for c in range(8):
                    e.matmul(ps_s[0:1, 0:width], ones, x3[:, c, :],
                             start=(c == 0), stop=(c == 7))
                for c in range(8):
                    ins = e.matmul(ps_d[0:1, 0:width], ones, sq3[:, c, :],
                                   start=(c == 0), stop=(c == 7))
                return ins
            prev = stage(prev, [("tensor", mk_stats)])

            def mk_sv1(e):
                e.tensor_scalar(stat[0:1, 0:width], ps_s[0:1, 0:width],
                                1.0 / D, None, ALU.mult)
                return e.tensor_scalar(stat2[0:1, 0:width],
                                       ps_d[0:1, 0:width],
                                       1.0 / D, None, ALU.mult)
            prev = stage(prev, [("vector", mk_sv1)])

            def mk_sv2(e):
                return e.tensor_tensor(stat[0:1, width:2 * width],
                                       stat[0:1, 0:width],
                                       stat[0:1, 0:width], ALU.mult)
            prev = stage(prev, [("vector", mk_sv2)])

            def mk_sv3(e):
                return e.tensor_tensor(stat2[0:1, 0:width],
                                       stat2[0:1, 0:width],
                                       stat[0:1, width:2 * width],
                                       ALU.subtract)
            prev = stage(prev, [("vector", mk_sv3)])

            def mk_ss(e):
                e.activation(stat2[0:1, width:2 * width],
                             stat2[0:1, 0:width], AF.Ln,
                             bias=cba("eps", False))
                return e.activation(stat2[0:1, 0:width],
                                    stat2[0:1, width:2 * width],
                                    AF.Exp, scale=-0.5)
            prev = stage(prev, [("scalar", mk_ss)])

            def mk_bc(e):
                e.matmul(ps_v[:, 0:width], ones_r, stat[0:1, 0:width],
                         start=True, stop=True)
                return e.matmul(ps_v[:, 512:512 + width], ones_r,
                                stat2[0:1, 0:width], start=True, stop=True)
            prev = stage(prev, [("tensor", mk_bc)])
            m_b = ps_v[:, 0:width].unsqueeze(1).broadcast_to([128, 8, width])
            r_b = ps_v[:, 512:512 + width].unsqueeze(1)\
                .broadcast_to([128, 8, width])

            if debug and dbg_ffn:
                def mkd1(e):
                    e.dma_start(out=d["dbg_ffx"][:, 0:8 * width],
                                in_=x_tile[:, 0:8 * width])
                    e.dma_start(out=d["dbg_fstat"][0:1, 0:width],
                                in_=stat[0:1, 0:width])
                    return e.dma_start(out=d["dbg_fstat2"][0:1, 0:width],
                                       in_=stat2[0:1, 0:width])
                prev = stage(prev, [("gpsimd", mkd1)])

            def mk_ap(e):
                e.tensor_tensor(sq3, x3, m_b, ALU.subtract)
                return e.tensor_tensor(o3, sq3, r_b, ALU.mult)
            prev = stage(prev, [("vector", mk_ap)])
            return prev

        def emit_wload(prev, wsrc, wbuf):
            def mk(e):
                return e.dma_start(
                    out=wbuf.rearrange("p (c n) -> p c n", c=8),
                    in_=wsrc.rearrange("(c p) n -> p c n", p=128))
            return stage(prev, [("sync", mk)])

        def emit_proj_t(prev, wbuf, out_sb):
            for dd in range(8):
                ps = ps_p if dd % 2 == 0 else ps_q

                def mk_mm(e, dd=dd, ps=ps):
                    for c in range(8):
                        ins = e.matmul(ps[:, 0:TILE], wslice(wbuf, c, dd),
                                       xh[:, c * TILE:(c + 1) * TILE],
                                       start=(c == 0), stop=(c == 7))
                    return ins
                prev = stage(prev, [("tensor", mk_mm)])

                def mk_ev(e, dd=dd, ps=ps):
                    return e.activation(out_sb[:, dd * TILE:(dd + 1) * TILE],
                                        ps[:, 0:TILE], AF.Copy)
                prev = stage(prev, [("scalar", mk_ev)])
            return prev

        qp4 = qp.rearrange("p (h t) -> p h t", h=H)
        kp4 = kp.rearrange("p (h t) -> p h t", h=H)
        qfc_a4 = qfc_a.rearrange("p (h t) -> p h t", h=H)
        qfc_b4 = qfc_b.rearrange("p (h t) -> p h t", h=H)
        kfc_a4 = kfc_a.rearrange("p (h t) -> p h t", h=H)
        kfc_b4 = kfc_b.rearrange("p (h t) -> p h t", h=H)
        scw4 = sc_w.rearrange("p (h t) -> p h t", h=H)
        on4 = o_n.rearrange("p (h t) -> p h t", h=H)
        kb_b = fmkb_bc.unsqueeze(1).broadcast_to([128, 4, DK])

        def emit_qk_chunk(prev, cix, with_scores):
            c0 = cix * CH
            if with_scores:
                def mk_qf(e, c0=c0):
                    e.activation(qfc_a4, qp4[:, :, c0:c0 + CH], AF.Exp,
                                 bias=fmqb[:, 0:1])
                    e.activation(qfc_b4, qp4[:, :, c0:c0 + CH], AF.Exp,
                                 scale=-1.0, bias=fmqb[:, 1:2])
                    e.activation(kfc_a4, kp4[:, :, c0:c0 + CH], AF.Exp,
                                 bias=fmkb[:, 0:1])
                    return e.activation(kfc_b4, kp4[:, :, c0:c0 + CH],
                                        AF.Exp, scale=-1.0,
                                        bias=fmkb[:, 1:2])
                prev = stage(prev, [("scalar", mk_qf)])

            def mk_tr(e, c0=c0):
                for h in range(H):
                    ps = ps_s if h < 4 else ps_d
                    off = (h % 4) * DK
                    ins = e.transpose(ps[:, off:off + DK],
                                      kp[:, h * TILE + c0:h * TILE + c0 + CH],
                                      idn)
                return ins
            prev = stage(prev, [("tensor", mk_tr)])

            def mk_ktb(e):
                e.tensor_tensor(scw4[:, 0:4, :],
                                ps_s.rearrange("p (h t) -> p h t", h=4),
                                kb_b, ALU.add)
                return e.tensor_tensor(scw4[:, 4:8, :],
                                       ps_d.rearrange("p (h t) -> p h t", h=4),
                                       kb_b, ALU.add)
            prev = stage(prev, [("vector", mk_ktb)])

            def mk_kte(e):
                e.activation(kt_a, sc_w, AF.Exp)
                return e.activation(kt_b, sc_w, AF.Exp, scale=-1.0)
            prev = stage(prev, [("scalar", mk_kte)])
            return prev

        def emit_v_chunk(prev, wvbuf, cix):
            c0 = cix * CH

            def mk_vm(e, c0=c0):
                wv3 = wvbuf.rearrange("p (c n) -> p c n", c=8)
                for half in range(2):
                    for c in range(8):
                        ins = e.matmul(
                            ps_v[:, half * 512:(half + 1) * 512],
                            xh[:, c * TILE + c0:c * TILE + c0 + CH],
                            wv3[:, c, half * 512:(half + 1) * 512],
                            start=(c == 0), stop=(c == 7))
                return ins
            prev = stage(prev, [("tensor", mk_vm)])

            def mk_vev(e):
                e.activation(vt[:, 0:512], ps_v[:, 0:512], AF.Copy)
                return e.activation(vt[:, 512:1024], ps_v[:, 512:1024],
                                    AF.Copy)
            prev = stage(prev, [("scalar", mk_vev)])
            return prev

        tri_b = triu.unsqueeze(1).broadcast_to([128, 4, CH])

        def emit_scan_chunk(prev, cix, with_output):
            c0 = cix * CH
            if with_output:
                def mk_sc(e):
                    for h in range(H):
                        ps = ps_s if h < 4 else ps_p
                        off = (h % 4) * CH
                        e.matmul(ps[:, off:off + CH],
                                 kfc_a[:, h * CH:(h + 1) * CH],
                                 qfc_a[:, h * CH:(h + 1) * CH],
                                 start=True, stop=False)
                        ins = e.matmul(ps[:, off:off + CH],
                                       kfc_b[:, h * CH:(h + 1) * CH],
                                       qfc_b[:, h * CH:(h + 1) * CH],
                                       start=False, stop=True)
                    return ins
                prev = stage(prev, [("tensor", mk_sc)])

                def mk_mask(e):
                    e.tensor_tensor(
                        scw4[:, 0:4, :],
                        ps_s.rearrange("p (h t) -> p h t", h=4), tri_b,
                        ALU.mult)
                    return e.tensor_tensor(
                        scw4[:, 4:8, :],
                        ps_p.rearrange("p (h t) -> p h t", h=4), tri_b,
                        ALU.mult)
                prev = stage(prev, [("vector", mk_mask)])

                def mk_o(e):
                    for h in range(H):
                        ps = ps_o if h < 4 else ps_q
                        off = (h % 4) * CH
                        e.matmul(ps[:, off:off + CH],
                                 vt[:, h * DK:(h + 1) * DK],
                                 sc_w[:, h * CH:(h + 1) * CH],
                                 start=True, stop=False)
                        e.matmul(ps[:, off:off + CH],
                                 S_a[:, h * DK:(h + 1) * DK],
                                 qfc_a[:, h * CH:(h + 1) * CH],
                                 start=False, stop=False)
                        ins = e.matmul(ps[:, off:off + CH],
                                       S_b[:, h * DK:(h + 1) * DK],
                                       qfc_b[:, h * CH:(h + 1) * CH],
                                       start=False, stop=True)
                    return ins
                prev = stage(prev, [("tensor", mk_o)])

                def mk_oev2(e, c0=c0):
                    e.activation(on4[:, 0:4, c0:c0 + CH], ps_o.rearrange(
                        "p (h t) -> p h t", h=4), AF.Copy)
                    e.activation(on4[:, 4:8, c0:c0 + CH], ps_q.rearrange(
                        "p (h t) -> p h t", h=4), AF.Copy)
                    e.activation(yt_w[:, 0:512], ps_o[:, 0:512], AF.Square)
                    return e.activation(yt_w[:, 512:1024], ps_q[:, 0:512],
                                        AF.Square)
                prev = stage(prev, [("scalar", mk_oev2)])

                def mk_osum(e):
                    e.matmul(ps_d[0:1, 0:512], ones, yt_w[:, 0:512],
                             start=True, stop=True)
                    return e.matmul(ps_v[0:1, 0:512], ones, yt_w[:, 512:1024],
                                    start=True, stop=True)
                prev = stage(prev, [("tensor", mk_osum)])

                def mk_r1(e):
                    e.tensor_scalar(stat[0:1, 0:512], ps_d[0:1, 0:512],
                                    1.0 / DK, EPS, ALU.mult, ALU.add)
                    return e.tensor_scalar(stat[0:1, 512:1024],
                                           ps_v[0:1, 0:512],
                                           1.0 / DK, EPS, ALU.mult, ALU.add)
                prev = stage(prev, [("vector", mk_r1)])

                def mk_r2(e):
                    return e.tensor_scalar(
                        stat_i2[0:1, 0:1024], stat[0:1, 0:1024].bitcast(I32),
                        float(2.0 ** -23), -127.0, ALU.mult, ALU.add)
                prev = stage(prev, [("vector", mk_r2)])

                def mk_r3(e):
                    return e.tensor_scalar(stat_i3[0:1, 0:1024],
                                           stat_i2[0:1, 0:1024], -M23,
                                           127.0 * M23, ALU.mult, ALU.add)
                prev = stage(prev, [("vector", mk_r3)])

                def mk_r4(e):
                    return e.tensor_tensor(
                        stat2[0:1, 0:1024], stat[0:1, 0:1024],
                        stat_i3[0:1, 0:1024].bitcast(F32), ALU.mult)
                prev = stage(prev, [("vector", mk_r4)])

                def mk_r5(e):
                    return e.activation(stat2[0:1, 0:1024],
                                        stat2[0:1, 0:1024], AF.Ln)
                prev = stage(prev, [("scalar", mk_r5)])

                def mk_r6(e):
                    return e.scalar_tensor_tensor(
                        stat[0:1, 0:1024], stat_i2[0:1, 0:1024], LN2,
                        stat2[0:1, 0:1024], ALU.mult, ALU.add)
                prev = stage(prev, [("vector", mk_r6)])

                def mk_r7(e):
                    return e.activation(stat[0:1, 0:1024], stat[0:1, 0:1024],
                                        AF.Exp, scale=-0.5)
                prev = stage(prev, [("scalar", mk_r7)])

                def mk_rb(e):
                    e.matmul(ps_o[:, 0:512], ones_r, stat[0:1, 0:512],
                             start=True, stop=True)
                    return e.matmul(ps_q[:, 0:512], ones_r,
                                    stat[0:1, 512:1024], start=True,
                                    stop=True)
                prev = stage(prev, [("tensor", mk_rb)])

                def mk_oap(e, c0=c0):
                    e.scalar_tensor_tensor(
                        on4[:, 0:4, c0:c0 + CH], on4[:, 0:4, c0:c0 + CH],
                        rmsw[:, 0:1],
                        ps_o.rearrange("p (h t) -> p h t", h=4),
                        ALU.mult, ALU.mult)
                    return e.scalar_tensor_tensor(
                        on4[:, 4:8, c0:c0 + CH], on4[:, 4:8, c0:c0 + CH],
                        rmsw[:, 0:1],
                        ps_q.rearrange("p (h t) -> p h t", h=4),
                        ALU.mult, ALU.mult)
                prev = stage(prev, [("vector", mk_oap)])

            def mk_ds(e):
                for h in range(H):
                    ps = ps_s if h < 4 else ps_p
                    off = (h % 4) * CH
                    ins = e.matmul(ps[:, off:off + CH],
                                   kt_a[:, h * DK:(h + 1) * DK],
                                   vt[:, h * DK:(h + 1) * DK],
                                   start=True, stop=True)
                for h in range(H):
                    ps = ps_o if h < 4 else ps_q
                    off = (h % 4) * CH
                    ins = e.matmul(ps[:, off:off + CH],
                                   kt_b[:, h * DK:(h + 1) * DK],
                                   vt[:, h * DK:(h + 1) * DK],
                                   start=True, stop=True)
                return ins
            prev = stage(prev, [("tensor", mk_ds)])

            def mk_sadd(e):
                e.tensor_tensor(S_a[:, 0:512], S_a[:, 0:512], ps_s[:, 0:512],
                                ALU.add)
                e.tensor_tensor(S_a[:, 512:1024], S_a[:, 512:1024],
                                ps_p[:, 0:512], ALU.add)
                e.tensor_tensor(S_b[:, 0:512], S_b[:, 0:512], ps_o[:, 0:512],
                                ALU.add)
                return e.tensor_tensor(S_b[:, 512:1024], S_b[:, 512:1024],
                                       ps_q[:, 0:512], ALU.add)
            prev = stage(prev, [("vector", mk_sadd)])
            return prev

        NCHT = TILE // CH
        for ti in range(NTL):
            t0 = ti * TILE

            def mk_ldx(e, t0=t0):
                return e.dma_start(
                    out=xc.rearrange("p (c t) -> p c t", c=8),
                    in_=d["xpre_t"].rearrange("(c p) t -> p c t", p=128)
                    [:, :, t0:t0 + TILE])
            prev = stage(prev, [("sync", mk_ldx)])
            prev = emit_ln(prev, None, t0, TILE, xc, o_n, xh, do_load=False)
            prev = emit_proj_t(prev, wbA, kp)
            for cix in range(NCHT):
                prev = emit_qk_chunk(prev, cix, with_scores=False)
                prev = emit_v_chunk(prev, wbB, cix)
                prev = emit_scan_chunk(prev, cix, with_output=False)
        prev = dump(prev, "dbg_Sa", S_a, 1024)
        prev = dump(prev, "dbg_Sb", S_b, 1024)

        for ti in range(NTL):
            t0 = ti * TILE

            def mk_ldx(e, t0=t0):
                return e.dma_start(
                    out=xc.rearrange("p (c t) -> p c t", c=8),
                    in_=d["x_t"].rearrange("(c p) t -> p c t", p=128)
                    [:, :, t0:t0 + TILE])
            prev = stage(prev, [("sync", mk_ldx)])
            prev = emit_ln(prev, None, t0, TILE, xc, o_n, xh, do_load=False)
            prev = emit_wload(prev, d["wqf"], wbA)
            prev = emit_proj_t(prev, wbA, qp)
            prev = emit_wload(prev, d["wkf"], wbB)
            prev = emit_proj_t(prev, wbB, kp)
            prev = emit_wload(prev, d["wvT"], wbA)
            for cix in range(NCHT):
                prev = emit_qk_chunk(prev, cix, with_scores=True)
                prev = emit_v_chunk(prev, wbA, cix)
                prev = emit_scan_chunk(prev, cix, with_output=True)
            prev = emit_wload(prev, d["woT"], wbB)
            for dd in range(8):
                ps = ps_o if dd % 2 == 0 else ps_q

                def mk_wo(e, dd=dd, ps=ps):
                    for h in range(H):
                        ins = e.matmul(
                            ps[:, 0:TILE], wslice(wbB, h, dd),
                            o_n[:, h * TILE:(h + 1) * TILE],
                            start=(h == 0), stop=(h == 7))
                    return ins
                prev = stage(prev, [("tensor", mk_wo)])

                def mk_res(e, dd=dd, ps=ps):
                    return e.tensor_tensor(
                        xc[:, dd * TILE:(dd + 1) * TILE],
                        xc[:, dd * TILE:(dd + 1) * TILE],
                        ps[:, 0:TILE], ALU.add)
                prev = stage(prev, [("vector", mk_res)])

            def mk_st(e, t0=t0):
                return e.dma_start(
                    out=d["xb"].rearrange("(c p) t -> p c t", p=128)
                    [:, :, t0:t0 + TILE],
                    in_=xc.rearrange("p (c t) -> p c t", c=8))
            prev = stage(prev, [("gpsimd", mk_st)])

        sch.barrier()
        attn_stack.close()
        ffn_stack = contextlib.ExitStack()
        fsb = mksbuf(ffn_stack)
        QW = 512

        def qset(tag):
            s = {}
            for nm in "ABCDEFGH":
                s[nm] = fsb("QS" + nm + tag, [128, QW])
            for nm in ("IA", "IB", "IC"):
                s[nm] = fsb("Q" + nm + tag, [128, QW], I32)
            return s
        QS0, QS1, QS2, QS3 = qset("0"), qset("1"), qset("2"), qset("3")
        QSET = (QS0, QS1, QS2, QS3)
        QSA, QSB, QSC, QSD = QS0["A"], QS0["B"], QS0["C"], QS0["D"]
        QSE, QSF, QSG, QSH = QS0["E"], QS0["F"], QS0["G"], QS0["H"]
        QIA, QIB, QIC = QS0["IA"], QS0["IB"], QS0["IC"]
        fx = fsb("fx", [128, 8 * FT])
        fxh = fsb("fxh", [128, 8 * FT])
        fzT = [fsb("fz%d" % i, [128, FT]) for i in range(4)]
        fyT = [fsb("fy%d" % i, [128, FT]) for i in range(4)]
        x2q = fsb("x2q", [128, 8 * FT], F8)
        hq = fsb("hq", [128, 32 * FT], F8)
        msk8 = fsb("msk8", [128, 8 * FT], F8)
        wloadT = [fsb("wload%d" % i, [128, 512]) for i in range(4)]
        wqstT = [fsb("wqst%d" % i, [128, 512], F8) for i in range(4)]
        w1bT = [fsb("w1b%d" % i, [128, 8 * 128], F8) for i in range(2)]
        w2bT = [fsb("w2b%d" % i, [128, 32 * 128], F8) for i in range(2)]

        def q_weight_chains(srcT, nchunk, cols, dst_dram, k0):
            src3 = srcT.rearrange("(c p) n -> p c n", p=128)
            step = 512
            chains = []
            k = k0
            for c in range(nchunk):
                for s0 in range(0, cols, step):
                    S = QSET[k % 4]
                    wld = wloadT[k % 4]
                    stg = wqstT[k % 4]
                    ops = [("sync", (lambda c, s0, wld, src3:
                                     lambda e: e.dma_start(
                                         out=wld[:, 0:step],
                                         in_=src3[:, c, s0:s0 + step]))
                            (c, s0, wld, src3))]
                    ops += q_ops(wld[:, 0:step], stg[:, 0:step],
                                 M=3, clamp=6, signed=True, ties=True,
                                 width=step, S=S)
                    ops.append(("gpsimd", (lambda c, s0, stg, dst_dram:
                                           lambda e: e.dma_start(
                        out=dst_dram[c * 128:(c + 1) * 128, s0:s0 + step],
                        in_=stg[:, 0:step]))(c, s0, stg, dst_dram)))
                    chains.append(ops)
                    k += 1
            return chains

        prev = run_pipe(prev, q_weight_chains(d["w1T"], 8, DH, d["w1q8"], 0)
                        + q_weight_chains(d["w2T"], 32, D, d["w2q8"], 0))

        prev = dump(prev, "dbg_b1t", b1t, 32)
        if debug:
            W_ = 32
            z_, out_ = b1t, b1q
            l_, f_, w_ = QSA[:, :W_], QSB[:, :W_], QSC[:, :W_]
            tt_, r_ = QSE[:, :W_], QSF[:, :W_]
            bs_ = QSG[:, :W_]
            gE_, g_ = QIA[:, :W_], QIB[:, :W_]
            bEi_, p_ = QIC[:, :W_], QIA[:, :W_]
            tie_, odd_ = QSG[:, :W_], QSD[:, :W_]
            prev = stage(prev, [("scalar", lambda e: e.activation(tt_, z_, AF.Square))])
            prev = stage(prev, [("scalar", lambda e: e.activation(l_, tt_, AF.Ln, bias=cba("tiny")))])
            prev = stage(prev, [("scalar", lambda e: e.activation(gE_, l_, AF.Copy, scale=1.0 / (2 * LN2)))])
            prev = stage(prev, [("vector", lambda e: e.scalar_tensor_tensor(bEi_, gE_, 2 * LN2, l_, ALU.mult, ALU.is_gt))])
            prev = stage(prev, [("vector", lambda e: e.scalar_tensor_tensor(f_, bEi_, -1.0, gE_, ALU.mult, ALU.add))])
            prev = stage(prev, [("scalar", lambda e: e.activation(w_, f_, AF.Relu, bias=cba("c6")))])
            prev = stage(prev, [("vector", lambda e: e.tensor_scalar(gE_, w_, -M23, 136.0 * M23, ALU.mult, ALU.add))])
            def mkbis(e):
                return e.tensor_copy(QSD[:, 0:W_], gE_)
            prev = stage(prev, [("vector", mkbis)])
            prev = dump(prev, "dbg_bis", QSD, 32)
            prev = stage(prev, [("vector", lambda e: e.tensor_tensor(tt_, z_, gE_.bitcast(F32), ALU.mult))])
            prev = stage(prev, [("vector", lambda e: e.tensor_copy(g_, tt_))])
            prev = stage(prev, [("vector", lambda e: e.scalar_tensor_tensor(r_, g_, 0.5, tt_, ALU.add, ALU.is_le))])
            prev = dump(prev, "dbg_bb1", QSF, 32)
            prev = stage(prev, [("vector", lambda e: e.scalar_tensor_tensor(bs_, g_, -0.5, tt_, ALU.add, ALU.is_gt))])
            prev = dump(prev, "dbg_bb2", QSG, 32)
            prev = stage(prev, [("vector", lambda e: e.scalar_tensor_tensor(r_, bs_, -1.0, r_, ALU.mult, ALU.add))])
            prev = dump(prev, "dbg_bcomb", QSF, 32)
            prev = stage(prev, [("vector", lambda e: e.tensor_tensor(r_, r_, g_, ALU.add))])
            prev = stage(prev, [("vector", lambda e: e.scalar_tensor_tensor(tie_, r_, -0.5, tt_, ALU.add, ALU.is_equal))])
            prev = stage(prev, [("vector", lambda e: e.tensor_scalar(odd_, r_, 0.5, None, ALU.mult))])
            prev = stage(prev, [("vector", lambda e: e.tensor_copy(p_, odd_))])
            prev = stage(prev, [("vector", lambda e: e.tensor_tensor(odd_, odd_, p_, ALU.not_equal))])
            prev = stage(prev, [("vector", lambda e: e.tensor_tensor(tie_, tie_, odd_, ALU.mult))])
            prev = stage(prev, [("vector", lambda e: e.tensor_tensor(r_, r_, tie_, ALU.subtract))])
            prev = stage(prev, [("vector", lambda e: e.tensor_scalar(gE_, w_, M23, 118.0 * M23, ALU.mult, ALU.add))])
            prev = stage(prev, [("vector", lambda e: e.tensor_tensor(out_, r_, gE_.bitcast(F32), ALU.mult))])
        else:
            prev = run_serial(prev, q_ops(b1t, b1q, M=3, clamp=6, signed=True,
                                          ties=True, width=32))
        if debug:
            prev = dump(prev, "dbg_bf", QSB, 32)
            prev = dump(prev, "dbg_bw", QSC, 32)
            prev = dump(prev, "dbg_br", QSF, 32)
            prev = dump(prev, "dbg_bt", QSE, 32)
            def mkg(e):
                return e.tensor_copy(QSD[:, 0:32], QIB[:, 0:32])
            prev = stage(prev, [("vector", mkg)])
            prev = dump(prev, "dbg_bg", QSD, 32)
        prev = run_serial(prev, q_ops(b2t, b2q, M=3, clamp=6, signed=True,
                                      ties=True, width=8))

        if debug:
            def mkdb(e):
                return e.tensor_copy(QSB[:, 0:32], b1q)
            prev = stage(prev, [("vector", mkdb)])
            prev = dump(prev, "dbg_b1q", QSB, 32)

        x2q3 = x2q.rearrange("p (c t) -> p c t", c=8)
        hq3 = hq.rearrange("p (c t) -> p c t", c=32)
        fx3 = fx.rearrange("p (c t) -> p c t", c=8)
        w1q8r = d["w1q8"].rearrange("(c p) n -> p c n", p=128)
        w2q8r = d["w2q8"].rearrange("(c p) n -> p c n", p=128)
        for ti in range(NFT):
            t0 = ti * FT
            prev = emit_ln(prev, d["xb"], t0, FT, fx, fxh, fxh,
                           dbg_ffn=(ti == 0))

            def mk_mld(e, t0=t0):
                e.dma_start(out=msk8.rearrange("p (c t) -> p c t", c=8),
                            in_=d["maskT"].rearrange("(c p) t -> p c t", p=128)
                            [:, :, t0:t0 + FT])
            prev = stage(prev, [("sync", mk_mld)])
            xchains = []
            for sl in range(8 * FT // 512):
                xchains.append(q_ops(
                    fxh[:, sl * 512:(sl + 1) * 512],
                    x2q[:, sl * 512:(sl + 1) * 512],
                    M=3, clamp=6, signed=True, ties=False, width=512,
                    S=QSET[sl % 4]))
            prev = run_pipe(prev, xchains)
            if debug and ti == 0:
                prev = dump(prev, "dbg_fxh", fxh, 8 * FT)
                def mkd(e):
                    return e.tensor_copy(QSA[:, 0:512], x2q[:, 0:512])
                prev = stage(prev, [("vector", mkd)])
                prev = dump(prev, "dbg_x2q", QSA, 512)
            hchains = []
            for dh in range(32):
                psx = ps_p if dh % 2 == 0 else ps_d
                fzx = fzT[dh % 4]
                S = QSET[dh % 4]
                wb = w1bT[dh % 2]

                def mk_wl(e, dh=dh, wb=wb):
                    return e.dma_start(
                        out=wb.rearrange("p (c n) -> p c n", c=8),
                        in_=w1q8r[:, :, dh * 128:(dh + 1) * 128])

                def mk_m(e, dh=dh, psx=psx, wb=wb):
                    wb3 = wb.rearrange("p (c n) -> p c n", c=8)
                    for c in range(8):
                        ins = e.matmul(psx[:, 0:FT], wb3[:, c, :],
                                       x2q3[:, c, :],
                                       start=(c == 0), stop=(c == 7))
                    return ins

                def mk_e(e, dh=dh, psx=psx, fzx=fzx):
                    e.activation(fzx, psx[:, 0:FT], AF.Relu,
                                 bias=b1q[:, dh:dh + 1])
                ops = [("sync", mk_wl), ("tensor", mk_m), ("scalar", mk_e)]
                ops += q_h(fzx, hq[:, dh * FT:(dh + 1) * FT], width=FT, S=S)
                hchains.append(ops)
            prev = run_pipe(prev, hchains)
            if debug and ti == 0:
                def mkd2(e):
                    return e.tensor_copy(QSA[:, 0:512], hq[:, 0:512])
                prev = stage(prev, [("vector", mkd2)])
                prev = dump(prev, "dbg_hq", QSA, 512)
            ychains = []
            for dd in range(8):
                psy = ps_q if dd % 2 == 0 else ps_o
                fyx = fyT[dd % 4]
                yqx = fzT[dd % 4]
                S = QSET[dd % 4]
                w2b = w2bT[dd % 2]

                def mk_wl2(e, dd=dd, w2b=w2b):
                    return e.dma_start(
                        out=w2b.rearrange("p (c n) -> p c n", c=32),
                        in_=w2q8r[:, :, dd * 128:(dd + 1) * 128])

                def mk_m2(e, dd=dd, psy=psy, w2b=w2b):
                    w2b3 = w2b.rearrange("p (c n) -> p c n", c=32)
                    for c in range(32):
                        ins = e.matmul(psy[:, 0:FT], w2b3[:, c, :],
                                       hq3[:, c, :],
                                       start=(c == 0), stop=(c == 31))
                    return ins

                def mk_e2(e, dd=dd, psy=psy, fyx=fyx):
                    return e.tensor_scalar(fyx, psy[:, 0:FT],
                                           b2q[:, dd:dd + 1], None, ALU.add)

                def mk_dr1(e, dd=dd, fyx=fyx, yqx=yqx):
                    return e.tensor_tensor(
                        fyx, yqx,
                        msk8.rearrange("p (c t) -> p c t", c=8)[:, dd, :],
                        ALU.mult)

                def mk_dr2(e, dd=dd, fyx=fyx):
                    return e.tensor_tensor(fyx, fyx, fx3[:, dd, :], ALU.add)

                def mk_out(e, dd=dd, t0=t0, fyx=fyx):
                    e.dma_start(
                        out=d["out_t"].rearrange("(c p) t -> p c t", p=128)
                        [:, dd, t0:t0 + FT], in_=fyx)
                ops = [("sync", mk_wl2), ("tensor", mk_m2),
                       ("vector", mk_e2)]
                ops += q_ops(fyx, yqx, M=10, clamp=14, signed=True,
                             ties=False, width=FT, S=S)
                ops += [("vector", mk_dr1), ("vector", mk_dr2),
                        ("gpsimd", mk_out)]
                ychains.append(ops)
            prev = run_pipe(prev, ychains)
            if debug and ti == 0:
                prev = dump(prev, "dbg_fy", fyT[0], FT)

        ffn_stack.close()

        sch.schedule()
        if _SCHED_HOOK is not None:
            _SCHED_HOOK(sch)
        final = sch.inc_total

        class _Replay:
            def __init__(self, eng, oplist, sems):
                self._e = eng
                self._ops = oplist
                self._i = 0
                self._sems = sems

            def __getattr__(self, nm):
                f = getattr(self._e, nm)
                if not callable(f):
                    return f

                def g(*a, **k):
                    op = self._ops[self._i]
                    self._i += 1
                    for pe, v in op.waits:
                        self._e.wait_ge(self._sems[pe], v)
                    r = f(*a, **k)
                    if op.inc_id is not None and hasattr(r, "then_inc"):
                        r.then_inc(self._sems[op.engine],
                                   16 if op.is_dma else 1)
                    return r
                return g

        def engine_runner(ename):
            def run(eng):
                for fni, (e2, fn, bar) in enumerate(sch.fns):
                    if e2 != ename:
                        continue
                    fn(_Replay(eng, sch.op_by_fn[fni], sems))
                if ename == "gpsimd":
                    for e2 in ENGINES:
                        if final[e2] > 0:
                            eng.wait_ge(sems[e2], final[e2])
            return run

        with nc.Block() as block:
            block.sync(engine_runner("sync"))
            block.scalar(engine_runner("scalar"))
            block.vector(engine_runner("vector"))
            block.tensor(engine_runner("tensor"))
            block.gpsimd(engine_runner("gpsimd"))

    return nc


def _dropout_mask_np():
    import jax
    cpu = jax.devices("cpu")[0]
    with jax.default_device(cpu):
        key = jax.random.key(42, impl="threefry2x32")
        keep = jax.random.bernoulli(key, 0.8, (4, 8192, 1024))
        keep = np.asarray(keep)
    return keep


def kernel(L0_attn_x, L0_attn_Wq, L0_attn_Wk, L0_attn_Wv, L0_attn_Wo,
           L0_attn_fmq_w, L0_attn_fmq_b, L0_attn_fmk_w, L0_attn_fmk_b,
           L0_attn_rms_w, L0_ffn_w1, L0_ffn_b1, L0_ffn_w2, L0_ffn_b2):
    x = np.ascontiguousarray(np.asarray(L0_attn_x, dtype=np.float32))
    B, N, _ = x.shape
    keep = _dropout_mask_np()
    maskmul = np.where(keep, np.float32(1.25), np.float32(0.0))
    maskmul8 = maskmul.astype(ml_dtypes.float8_e4m3)

    wqT = np.ascontiguousarray(np.asarray(L0_attn_Wq).T)
    wkT = np.ascontiguousarray(np.asarray(L0_attn_Wk).T)
    wvT = np.ascontiguousarray(np.asarray(L0_attn_Wv).T)
    woT = np.ascontiguousarray(np.asarray(L0_attn_Wo).T)
    fmqT = np.ascontiguousarray(np.asarray(L0_attn_fmq_w).T)
    fmkT = np.ascontiguousarray(np.asarray(L0_attn_fmk_w).T)
    w1T = np.ascontiguousarray(np.asarray(L0_ffn_w1).T)
    w2T = np.ascontiguousarray(np.asarray(L0_ffn_w2).T)

    triu = np.triu(np.ones((CH, CH), dtype=np.float32))
    ident = np.eye(128, dtype=np.float32)
    wq_raw = np.ascontiguousarray(np.asarray(L0_attn_Wq, dtype=np.float32))
    wk_raw = np.ascontiguousarray(np.asarray(L0_attn_Wk, dtype=np.float32))
    fmqb_c = np.ascontiguousarray(
        np.asarray(L0_attn_fmq_b, dtype=np.float32).reshape(DK, 1))
    fmkb_c = np.ascontiguousarray(
        np.asarray(L0_attn_fmk_b, dtype=np.float32).reshape(DK, 1))
    fmkbr_c = np.ascontiguousarray(
        np.asarray(L0_attn_fmk_b, dtype=np.float32).reshape(1, DK))
    rmsw_c = np.ascontiguousarray(
        np.asarray(L0_attn_rms_w, dtype=np.float32).reshape(DK, 1))
    b1_c = np.ascontiguousarray(
        np.asarray(L0_ffn_b1, dtype=np.float32).reshape(32, 128).T)
    b2_c = np.ascontiguousarray(
        np.asarray(L0_ffn_b2, dtype=np.float32).reshape(8, 128).T)
    onesc = np.ones((128, 1), dtype=np.float32)
    zeros_pre = np.zeros((D, T), dtype=np.float32)

    nc = build_program()
    in_maps = []
    for c in range(8):
        b, half = c // 2, c % 2
        tok = slice(half * T, (half + 1) * T)
        in_maps.append({
            "x_t": np.ascontiguousarray(x[b, tok, :].T),
            "xpre_t": (zeros_pre if half == 0 else
                       np.ascontiguousarray(x[b, 0:T, :].T)),
            "wqT": wqT, "wkT": wkT, "wvT": wvT, "woT": woT,
            "fmqT": fmqT, "fmkT": fmkT,
            "fmq_b": fmqb_c, "fmk_b": fmkb_c, "fmk_br": fmkbr_c,
            "rms_w": rmsw_c,
            "w1T": w1T, "b1": b1_c,
            "w2T": w2T, "b2": b2_c,
            "maskT": np.ascontiguousarray(maskmul8[b, tok, :].T),
            "triu": triu, "ones": onesc, "ident": ident,
            "wq_raw": wq_raw, "wk_raw": wk_raw,
        })
    res = run_bass_kernel_spmd(nc, in_maps, core_ids=list(range(8)))
    out = np.empty((B, N, D), dtype=np.float32)
    for c in range(8):
        b, half = c // 2, c % 2
        out[b, half * T:(half + 1) * T, :] = \
            np.asarray(res.results[c]["out_t"]).T
    return out
